# version 30
# baseline (speedup 1.0000x reference)
import numpy as np
import ml_dtypes
from contextlib import ExitStack

import concourse.bass as bass
import concourse.mybir as mybir
from concourse.bass_utils import run_bass_kernel_spmd

F32 = mybir.dt.float32
BF16 = mybir.dt.bfloat16
I32 = mybir.dt.int32
AX = mybir.AxisListType.X
ALU = mybir.AluOpType
ACT = mybir.ActivationFunctionType

D = 2048
T = 2048
NH = 16
G = 16
NC = 16
TT = 512
NTILE = T // TT
INTER = 5504
NI = INTER // 128
NCORES = 8
QSCALE = 1.0 / float(np.sqrt(128.0))
EPS = 1e-5
NEG = -30000.0

N_G = NI * 16
N_TILES_FFN = 3 * N_G
PER_CORE = N_TILES_FFN // NCORES
GATHER_BS = 6
CAST_BS = 4
N_ATTN_TILES = 4 * 256

STOP = "full"

ENGINES = ("sync", "vector", "scalar", "tensor", "gpsimd")


class Sem:
    __slots__ = ("name", "h", "val")

    def __init__(self, name):
        self.name = name
        self.h = None
        self.val = 0

    def tok(self, k=1):
        self.val += k
        return (self, self.val)

    @property
    def cur(self):
        return (self, self.val)


class Plan:
    def __init__(self):
        self.ops = {e: [] for e in ENGINES}
        self.waited = {e: {} for e in ENGINES}

    def emit(self, eng, fn, deps=(), inc=None, amt=1):
        need = {}
        for d in deps:
            if d is None:
                continue
            sem, val = d
            if val > need.get(sem, 0):
                need[sem] = val
        ws = []
        for sem, val in need.items():
            if val > self.waited[eng].get(sem, 0):
                ws.append((sem, val))
                self.waited[eng][sem] = val
        tok = inc.tok(amt) if inc is not None else None
        self.ops[eng].append((ws, fn, inc, amt))
        return tok

    def run(self, eng_name, e):
        for ws, fn, inc, amt in self.ops[eng_name]:
            for sem, val in ws:
                e.wait_ge(sem.h, val)
            inst = fn(e)
            if inc is not None:
                inst.then_inc(inc.h, amt)


def ffn_jobs():
    jobs = []
    for ic in range(NI):
        for hc in range(16):
            jobs.append(("g", ic, hc))
    for ic in range(NI):
        for hc in range(16):
            jobs.append(("u", ic, hc))
    for oc in range(16):
        for ic in range(NI):
            jobs.append(("d", oc, ic))
    return jobs


def build_program():
    nc = bass.Bass("TRN2", target_bir_lowering=False, debug=False)

    x_t = nc.dram_tensor("x_t", [NTILE, 128, NC * TT], F32, kind="ExternalInput").ap()
    winp = [
        nc.dram_tensor(n, [128, 256 * 128], F32, kind="ExternalInput").ap()
        for n in ("wk_t", "wv_t", "wq_t", "wo_t")
    ]
    idxp = nc.dram_tensor("idxp", [128, PER_CORE * 128], I32, kind="ExternalInput").ap()
    lut_in = nc.dram_tensor("lut_in", [16], F32, kind="ExternalInput").ap()
    g_in = nc.dram_tensor("g_in", [D, G], F32, kind="ExternalInput").ap()
    gam_in = nc.dram_tensor("gam_in", [128, 16], F32, kind="ExternalInput").ap()
    bet_in = nc.dram_tensor("bet_in", [128, 16], F32, kind="ExternalInput").ap()
    eye_in = nc.dram_tensor("eye_in", [128, 128], F32, kind="ExternalInput").ap()
    msk_in = nc.dram_tensor("msk_in", [128, 2 * 384], F32, kind="ExternalInput").ap()

    y_out = nc.dram_tensor("y", [D, T], F32, kind="ExternalOutput").ap()

    wscr = nc.dram_tensor("wscr", [128, N_ATTN_TILES * 128], BF16).ap()
    agin = nc.dram_tensor("agin", [128, PER_CORE * 128], BF16).ap()
    agout = nc.dram_tensor("agout", [NCORES, 128, PER_CORE * 128], BF16).ap()
    hscr = nc.dram_tensor("hscr", [NTILE, 128, NC * TT], F32).ap()

    with ExitStack() as ctx:
        sb = lambda n, s, d=F32: ctx.enter_context(nc.sbuf_tensor(n, s, d))
        psum = lambda n, s, d=F32: ctx.enter_context(nc.psum_tensor(n, s, d))

        x_sb = sb("x_sb", [128, NC * TT])
        xn_sb = sb("xn_sb", [128, NC * TT], BF16)
        q_sb = sb("q_sb", [128, NH * TT], BF16)
        ao_sb = sb("ao_sb", [128, NH * TT], BF16)
        AR = sb("AR", [128, 4 * 8192], BF16)
        ws = [sb(f"ws{i}", [128, 2048], BF16) for i in range(4)]
        vtmp = sb("vtmp", [128, 512], BF16)
        vgt = sb("vgt", [128, 16], BF16)
        at_sb = [sb(f"at_sb{i}", [128, 512], BF16) for i in range(3)]
        aat = [sb(f"aat{i}", [128, 512], BF16) for i in range(3)]
        msk_sb = sb("msk_sb", [128, 2 * 384], BF16)
        eye_b = sb("eye_b", [128, 128], BF16)
        gfm_b = sb("gfm_b", [128, NC * G], BF16)
        kg_sb = sb("kg_sb", [128, NH * G], BF16)
        vg_sb = sb("vg_sb", [16, D], BF16)
        gam_sb = sb("gam_sb", [128, 16])
        bet_sb = sb("bet_sb", [128, 16])
        lut_sb = sb("lut_sb", [128, 16])
        ones_b = sb("ones_b", [128, 1], BF16)
        cpool = sb("cpool", [128, 2])
        stf = [sb(f"stf{i}", [128, 512]) for i in range(2)]
        stb = [sb(f"stb{i}", [128, 512], BF16) for i in range(2)]
        sq = [sb(f"sq{i}", [128, 512], BF16) for i in range(2)]
        sg = [sb(f"sg{i}", [128, 512], BF16) for i in range(2)]
        idxi = [sb("idxi0", [128, 768], I32)]
        idxb = [sb("idxb0", [128, 768], BF16)]
        term = [sb("term0", [128, 768], BF16)]
        gout = [sb(f"gout{i}", [128, 768], BF16) for i in range(2)]
        cin = [sb(f"cin{i}", [128, 512]) for i in range(3)]
        smalls = sb("smalls", [1, 1024])
        bcst_t = sb("bcst", [1, 1152])
        bcst = bcst_t[:, 0:1024]
        ones_f = bcst_t[:, 1024:1152]
        negmax = [sb(f"negmax{i}", [128, 1]) for i in range(3)]
        sume = [sb(f"sume{i}", [128, 1]) for i in range(3)]
        rinv = [sb(f"rinv{i}", [128, 1]) for i in range(3)]

        nc.const_aps.register(F32, -0.5, cpool[:, 0:1])
        nc.const_aps.register(F32, float(QSCALE), cpool[:, 1:2])

        KB = lambda b: AR[:, b * 8192 : (b + 1) * 8192]
        VB = lambda b: AR[:, (2 + b) * 8192 : (3 + b) * 8192]
        H2 = AR

        pA = psum("pA", [128, 512])
        pB = psum("pB", [128, 512])
        pBC1 = psum("pBC1", [128, 512])
        pBC2 = psum("pBC2", [128, 512])
        pSC0 = psum("pSC0", [128, 512])
        pSC1 = psum("pSC1", [128, 512])
        pT = psum("pT", [128, 512])
        pAV = psum("pAV", [128, 512])

        names = (["sK", "sX", "sHS", "sST", "sV", "sS", "sT", "sCC"]
                 + [f"sCI{i}" for i in range(4)] + [f"sGI{i}" for i in range(4)]
                 + [f"sCO{i}" for i in range(4)] + [f"sAG{i}" for i in range(4)]
                 + ["sWD0", "sWD1"]
                 + [f"sWS{i}" for i in range(4)])
        sems = {n: Sem(n) for n in names}
        for sm in sems.values():
            sm.h = ctx.enter_context(nc.semaphore(sm.name))
        sK = sems["sK"]; sX = sems["sX"]; sHS = sems["sHS"]; sST = sems["sST"]
        sV = sems["sV"]; sS = sems["sS"]; sT = sems["sT"]; sCC = sems["sCC"]
        sCI = [sems[f"sCI{i}"] for i in range(4)]
        sGI = [sems[f"sGI{i}"] for i in range(4)]
        sCO = [sems[f"sCO{i}"] for i in range(4)]
        sAG = [sems[f"sAG{i}"] for i in range(4)]
        sWD = [sems["sWD0"], sems["sWD1"]]
        sWS = [sems[f"sWS{i}"] for i in range(4)]

        P = Plan()
        E = P.emit

        E("sync", lambda e: e.dma_start(
            out=sq[0][:].bitcast(F32)[:, 0:NC * G].rearrange("p (hc k) -> p hc k", hc=NC),
            in_=g_in.rearrange("(hc p) k -> p hc k", p=128)), inc=sK, amt=16)
        E("sync", lambda e: e.dma_start(out=gam_sb[:], in_=gam_in), inc=sK, amt=16)
        E("sync", lambda e: e.dma_start(out=bet_sb[:], in_=bet_in), inc=sK, amt=16)
        E("sync", lambda e: e.dma_start(
            out=lut_sb[:, :], in_=lut_in[None, :].partition_broadcast(128)), inc=sK, amt=16)
        t_consts = sK.cur
        t_gam = t_bet = t_lut = t_consts
        t_mskf0 = E("sync", lambda e: e.dma_start(out=stf[0][:, 0:384], in_=msk_in[:, 0:384]), inc=sCI[3], amt=16)
        t_mskf1 = E("sync", lambda e: e.dma_start(out=stf[1][:, 0:384], in_=msk_in[:, 384:768]), inc=sGI[3], amt=16)
        t_mc0 = E("vector", lambda e: e.tensor_copy(msk_sb[:, 0:384], stf[0][:, 0:384]),
                  deps=[t_mskf0], inc=sV)
        t_mc1 = E("vector", lambda e: e.tensor_copy(msk_sb[:, 384:768], stf[1][:, 0:384]),
                  deps=[t_mskf1], inc=sV)
        t_eyef = E("sync", lambda e: e.dma_start(out=stf[1][:, 0:128], in_=eye_in),
                   deps=[t_mc1], inc=sGI[3], amt=16)
        t_eyeb = E("vector", lambda e: e.tensor_copy(eye_b[:], stf[1][:, 0:128]),
                   deps=[t_eyef], inc=sV)
        t_gfm = E("vector", lambda e: e.tensor_copy(
            gfm_b[:], sq[0][:].bitcast(F32)[:, 0:NC * G]), deps=[t_consts], inc=sV)
        t_msk = t_mc1
        t_x0 = E("scalar", lambda e: e.dma_start(
            out=x_sb[:], in_=x_t[0]), inc=sX, amt=16)
        t_ones = E("vector", lambda e: e.memset(ones_b[:], 1.0), inc=sV)
        t_onesf = E("vector", lambda e: e.memset(ones_f[:], 1.0), inc=sV)
        t_cp0 = E("vector", lambda e: e.memset(cpool[:, 0:1], -0.5), inc=sV)
        t_cp1 = E("vector", lambda e: e.memset(cpool[:, 1:2], float(QSCALE)), inc=sV)
        t_z1 = E("vector", lambda e: e.memset(KB(1)[:, :], 0.0), inc=sV)
        t_z2 = E("vector", lambda e: e.memset(VB(1)[:, :], 0.0), inc=sV)

        NCB = 256 // CAST_BS
        scr_ready = {}
        cast_in_free = [None, None, None, None, None]
        cast_store_prev = [None, None]
        ws_cast_deps = [[], [], [], []]
        cst = {"i": 0, "done": False, "limit": 16, "tok": {}, "st": {}}

        def cast_pump(n):
            for _ in range(n):
                if cst["done"] or cst["i"] >= cst["limit"]:
                    return
                i = cst["i"]
                m, b = divmod(i, NCB)
                p = i % 5
                land = cin[p] if p < 3 else stf[p - 3]
                psem = sCI[p] if p < 4 else sGI[2]
                s = i % 2
                wp = (i // 4) % 4
                sub = i % 4
                t0 = b * CAST_BS
                src = winp[m][:, t0 * 128:t0 * 128 + 512]
                first_stf = p >= 3 and cast_in_free[p] is None
                tin = E("sync", lambda e, src=src, land=land: e.dma_start(
                    out=land[:], in_=src),
                    deps=[cast_in_free[p]]
                         + ([st8["stf_free"][p - 3]] if first_stf else []),
                    inc=psem, amt=16)
                tc = E("vector", lambda e, land=land, wp=wp, sub=sub:
                       e.tensor_copy(
                    ws[wp][:, sub * 512:(sub + 1) * 512], land[:]),
                    deps=[tin] + ws_cast_deps[wp], inc=sV)
                cast_in_free[p] = tc
                r0 = (m * 256 + t0) * 128
                tst = E("scalar", lambda e, wp=wp, sub=sub, r0=r0: e.dma_start(
                    out=wscr[:, r0:r0 + 512],
                    in_=ws[wp][:, sub * 512:(sub + 1) * 512]),
                    deps=[tc, cast_store_prev[s]], inc=sCO[s], amt=16)
                cast_store_prev[s] = tst
                scr_ready[(m, b)] = tst
                cst["tok"][i] = tc
                cst["st"][i] = tst
                cst["i"] = i + 1
                if cst["i"] >= 4 * NCB:
                    cst["done"] = True
                    st8["stf_free"][0] = cast_in_free[3]
                    st8["stf_free"][1] = cast_in_free[4]

        def cast_direct(m, oc):
            need = m * 64 + oc * 4
            while cst["i"] < need + 4:
                assert cst["i"] < cst["limit"]
                cast_pump(1)
            return [cst["tok"][need + k] for k in range(4)],                    [cst["st"][need + k] for k in range(4)]

        def scr_deps(m, oc):
            bb = oc * 4
            assert cst["done"]
            return [scr_ready[(m, bb + k)] for k in range(4)]

        if STOP == "p1":
            cst["limit"] = 10 ** 9
            cast_pump(10 ** 9)
            E("sync", lambda e: e.nop(),
              deps=[sm.cur for sm in sems.values()])
            finalize(nc, P)
            return nc

        st8 = {
            "stb_free": [None, None],
            "stf_free": [t_mc0, t_eyeb],
            "sq_free": [t_gfm, None],
            "pbc_free": None,
            "pbc2_free": None,
        }

        def ln_stat_chunk(xin, dc, dep, state):
            sl = slice(dc * TT, (dc + 1) * TT)
            p = dc % 2
            tcast = E("vector", lambda e, p=p, sl=sl, xin=xin: e.tensor_copy(
                stb[p][:], xin[:, sl]),
                deps=[st8["stb_free"][p], dep], inc=sV)
            state["sum_last"] = E("tensor", lambda e, p=p, dc=dc: e.matmul(
                pBC1[0:1, :], ones_b[:], stb[p][:],
                start=(dc == 0), stop=(dc == NC - 1)),
                deps=[tcast, t_ones,
                      st8["pbc_free"] if dc == 0 else None], inc=sT)
            st8["stb_free"][p] = state["sum_last"]
            tsq = E("scalar", lambda e, p=p, sl=sl, xin=xin: e.square(
                sq[p][:], xin[:, sl]),
                deps=[st8["sq_free"][p], dep], inc=sS)
            state["sq_last"] = E("tensor", lambda e, p=p, dc=dc: e.matmul(
                pBC2[0:1, :], ones_b[:], sq[p][:],
                start=(dc == 0), stop=(dc == NC - 1)),
                deps=[tsq, t_ones,
                      st8["pbc_free"] if dc == 0 else None,
                      st8["pbc2_free"] if dc == 0 else None], inc=sT)
            st8["sq_free"][p] = state["sq_last"]

        def ln_finish(xin, gamma_beta, state):
            sum_last = state["sum_last"]
            sq_last = state["sq_last"]
            t_mu = E("vector", lambda e: e.tensor_scalar(
                out=smalls[:, 0:512], in0=pBC1[0:1, :], scalar1=1.0 / D,
                scalar2=None, op0=ALU.mult), deps=[sum_last], inc=sV)
            t_e2 = E("vector", lambda e: e.tensor_scalar(
                out=smalls[:, 512:1024], in0=pBC2[0:1, :], scalar1=1.0 / D,
                scalar2=float(EPS), op0=ALU.mult, op1=ALU.add),
                deps=[sq_last], inc=sV)
            t_msq = E("vector", lambda e: e.tensor_tensor(
                out=bcst[:, 0:512], in0=smalls[:, 0:512],
                in1=smalls[:, 0:512], op=ALU.mult), deps=[t_mu], inc=sV)
            t_var = E("vector", lambda e: e.tensor_tensor(
                out=smalls[:, 512:1024], in0=smalls[:, 512:1024],
                in1=bcst[:, 0:512], op=ALU.subtract),
                deps=[t_e2, t_msq], inc=sV)
            t_ln = E("scalar", lambda e: e.activation(
                out=smalls[:, 512:1024], in_=smalls[:, 512:1024],
                func=ACT.Ln, bias=0.0, scale=1.0), deps=[t_var], inc=sS)
            t_rstd = E("scalar", lambda e: e.activation(
                out=bcst[:, 512:1024], in_=smalls[:, 512:1024],
                func=ACT.Exp, bias=0.0, scale=-0.5),
                deps=[t_ln, t_cp0], inc=sS)
            t_mrs = E("vector", lambda e: e.tensor_tensor(
                out=bcst[:, 0:512], in0=smalls[:, 0:512],
                in1=bcst[:, 512:1024], op=ALU.mult), deps=[t_rstd], inc=sV)
            t_bc1 = E("tensor", lambda e: e.matmul(
                pBC1[:, :], ones_f[:], bcst[:, 0:512], start=True, stop=True),
                deps=[t_mrs, t_onesf, t_mu], inc=sT)
            t_bc2 = E("tensor", lambda e: e.matmul(
                pBC2[:, :], ones_f[:], bcst[:, 512:1024], start=True, stop=True),
                deps=[t_rstd, t_onesf, t_e2, st8["pbc2_free"]], inc=sT)
            chunk_toks = []
            for dc in range(NC):
                sl = slice(dc * TT, (dc + 1) * TT)
                p = dc % 2
                t1 = E("vector", lambda e, p=p, sl=sl, xin=xin: e.tensor_tensor(
                    out=stf[p][:], in0=xin[:, sl], in1=pBC2[:, :], op=ALU.mult),
                    deps=[t_bc2, st8["stf_free"][p]], inc=sV)
                if gamma_beta is None:
                    t2 = E("vector", lambda e, p=p, sl=sl, xin=xin: e.tensor_tensor(
                        out=xin[:, sl], in0=stf[p][:], in1=pBC1[:, :],
                        op=ALU.subtract), deps=[t_bc1, t1], inc=sV)
                    st8["stf_free"][p] = t2
                    chunk_toks.append(t2)
                else:
                    gm, bt, dest = gamma_beta
                    t2 = E("vector", lambda e, p=p: e.tensor_tensor(
                        out=stf[p][:], in0=stf[p][:], in1=pBC1[:, :],
                        op=ALU.subtract), deps=[t_bc1, t1], inc=sV)
                    t3 = E("scalar", lambda e, p=p, dc=dc, sl=sl, gm=gm, bt=bt,
                           dest=dest: e.activation(
                        out=dest[:, sl], in_=stf[p][:], func=ACT.Identity,
                        bias=bt[:, dc:dc + 1], scale=gm[:, dc:dc + 1]),
                        deps=[t2, t_consts], inc=sS)
                    st8["stf_free"][p] = t3
                    chunk_toks.append(t3)
            st8["pbc_free"] = chunk_toks[-1]
            return chunk_toks

        def layer_norm(xin, gamma_beta, xdep):
            if len(xdep) == 1:
                xdep = xdep * NC
            state = {}
            for dc in range(NC):
                ln_stat_chunk(xin, dc, xdep[dc], state)
            return ln_finish(xin, gamma_beta, state)

        ln0_t0_toks = layer_norm(x_sb, (gam_sb, bet_sb, xn_sb), xdep=[t_x0])

        NBG = (PER_CORE + GATHER_BS - 1) // GATHER_BS
        gst = {
            "b": 0, "v": -1,
            "in_free": [None],
            "out_free": [None, None],
            "tin": None, "acc_tok": None, "ts15": None,
            "term_free": [None],
            "idxb_free": [None],
            "done": False,
        }

        def gather_pump(n):
            for _ in range(n):
                if gst["done"]:
                    return
                b = gst["b"]
                p3 = b % 2
                t0 = b * GATHER_BS
                nt = min(GATHER_BS, PER_CORE - t0)
                w = nt * 128
                v = gst["v"]
                if v == -1:
                    gst["tin"] = E("gpsimd", lambda e, t0=t0, nt=nt: e.dma_start(
                        out=idxi[0][:, 0:nt * 128],
                        in_=idxp[:, t0 * 128:(t0 + nt) * 128]),
                        deps=[gst["in_free"][0]], inc=sGI[0], amt=16)
                    gst["v"] = -2
                elif v == -2:
                    gst["tin"] = E("vector", lambda e, w=w: e.tensor_copy(
                        idxb[0][:, 0:w], idxi[0][:, 0:w]),
                        deps=[gst["tin"], gst["idxb_free"][0]], inc=sV)
                    gst["in_free"][0] = gst["tin"]
                    gst["v"] = 0
                elif v < 16:
                    if v == 0:
                        gst["acc_tok"] = E("vector", lambda e, p3=p3, w=w:
                            e.tensor_scalar(
                                out=gout[p3][:, 0:w], in0=idxb[0][:, 0:w],
                                scalar1=0.0, scalar2=lut_sb[:, 0:1],
                                op0=ALU.is_equal, op1=ALU.mult),
                            deps=[gst["tin"], t_consts, gst["out_free"][p3]], inc=sV)
                    else:
                        tt_ = E("vector", lambda e, p3=p3, v=v, w=w:
                            e.tensor_scalar(
                                out=term[0][:, 0:w], in0=idxb[0][:, 0:w],
                                scalar1=float(v), scalar2=lut_sb[:, v:v + 1],
                                op0=ALU.is_equal, op1=ALU.mult),
                            deps=[gst["tin"], gst["term_free"][0]], inc=sV)
                        gst["acc_tok"] = E("vector", lambda e, p3=p3, v=v, w=w:
                            e.tensor_tensor(
                                out=gout[p3][:, 0:w], in0=gout[p3][:, 0:w],
                                in1=term[0][:, 0:w], op=ALU.add),
                            deps=[tt_, gst["acc_tok"]], inc=sV)
                        gst["term_free"][0] = gst["acc_tok"]
                        if v == 15:
                            gst["idxb_free"][0] = gst["acc_tok"]
                    gst["v"] = v + 1
                else:
                    tst = E("scalar", lambda e, p3=p3, t0=t0, nt=nt, w=w: e.dma_start(
                        out=agin[:, t0 * 128:(t0 + nt) * 128],
                        in_=gout[p3][:, 0:w]),
                        deps=[gst["acc_tok"]], inc=sAG[p3], amt=16)
                    gst["out_free"][p3] = tst
                    gst["b"] = b + 1
                    gst["v"] = -1
                    if gst["b"] >= NBG:
                        gst["done"] = True


        if STOP == "p2":
            E("sync", lambda e: e.nop(),
              deps=[sm.cur for sm in sems.values()])
            finalize(nc, P)
            return nc

        if STOP.startswith("agdump") and False:
            base = int(STOP[6:] or 0)
            prev = [t_ag, t_ag]
            for bch in range(64):
                p = bch % 2
                t0 = (base + bch * 4) * 128
                tin = E("gpsimd", lambda e, p=p, t0=t0: e.dma_start(
                    out=gout[p][:, 0:512].rearrange("p (j c) -> p j c", j=4),
                    in_=agout[t0:t0 + 512, :].rearrange("(j p) c -> p j c", p=128)),
                    deps=[t_ag, prev[p]], inc=sGI[p], amt=16)
                tcv = E("vector", lambda e, p=p: e.tensor_copy(
                    stf[p][:], gout[p][:, 0:512]), deps=[tin], inc=sV)
                row = (bch * 512) // 2048 * 128
                col = (bch * 512) % 2048
                tst = E("gpsimd", lambda e, p=p, row=row, col=col: e.dma_start(
                    out=y_out[row:row + 128, col:col + 512], in_=stf[p][:]),
                    deps=[tcv], inc=sST, amt=16)
                prev[p] = tst
            E("sync", lambda e: e.nop(),
              deps=[sm.cur for sm in sems.values()])
            finalize(nc, P)
            return nc

        ws_free = [None, None, None, None]
        x_free = None
        xn_free = None
        q_free = None
        ao_free = None
        kv_free = [t_z1, t_z2]
        k_ready = [[t_z1] * NH, [t_z1] * NH]
        v_ready = [[t_z2] * NH, [t_z2] * NH]
        kg_ready = [None] * NH
        vg_ready = [None] * NH
        pX_free = [None, None]
        pSC_free = [None, None]
        pT_state = {"free": None}
        pAV_free = None
        attention_state = {}
        h_dma = [None] * NC
        x_load_tok = t_x0
        last_av_tok = None
        last_gup_mm = None

        xchain = None
        for tau in range(NTILE):
            buf = tau % 2
            if tau > 0:
                if xchain is None:
                    xchain = list(cast_in_free[:3])
                x_chunk = []
                for dc in range(NC):
                    p = dc % 3
                    tok = E("scalar", lambda e, tau=tau, dc=dc: e.dma_start(
                        out=x_sb[:, dc * TT:(dc + 1) * TT],
                        in_=x_t[tau][:, dc * TT:(dc + 1) * TT]),
                        deps=[prev_ln1_toks[dc], h_dma[dc // 2], xchain[p]],
                        inc=sCI[p], amt=16)
                    xchain[p] = tok
                    x_chunk.append(tok)
                x_load_tok = x_chunk[-1]
            else:
                x_chunk = [t_x0] * NC
            if tau == 0:
                ln0_toks = ln0_t0_toks
            else:
                E("vector", lambda e: e.engine_nop(), deps=[xn_free])
                ln0_toks = layer_norm(x_sb, (gam_sb, bet_sb, xn_sb),
                                      xdep=x_chunk)
            if STOP in ("ln0", "ln0only"):
                for dc in range(NC):
                    p = dc % 2
                    tcv = E("vector", lambda e, p=p, dc=dc: e.tensor_copy(
                        stf[p][:], xn_sb[:, dc * TT:(dc + 1) * TT]),
                        deps=[ln0_toks[dc], st8["stf_free"][p]], inc=sV)
                    td = E("gpsimd", lambda e, p=p, dc=dc, tau=tau: e.dma_start(
                        out=y_out[dc * 128:(dc + 1) * 128, tau * TT:(tau + 1) * TT],
                        in_=stf[p][:]), deps=[tcv], inc=sST, amt=16)
                    st8["stf_free"][p] = td
            if STOP == "ln0only" and tau == 0:
                E("sync", lambda e: e.nop(),
                  deps=[sm.cur for sm in sems.values()])
                finalize(nc, P)
                return nc

            def proj(m, evac, tile0_extra=None):
                toks = []
                for oc in range(16):
                    wp = oc % 4
                    if tau == 0:
                        ctoks, stoks = cast_direct(m, oc)
                        wdep = lambda hc, ctoks=ctoks: ctoks[hc // 4]
                    else:
                        base = m * 256 + oc * 16
                        r0 = base * 128
                        tw = E("sync", lambda e, wp=wp, r0=r0: e.dma_start(
                            out=ws[wp][:], in_=wscr[:, r0:r0 + 2048]),
                            deps=[ws_free[wp]] + ws_cast_deps[wp]
                                 + scr_deps(m, oc),
                            inc=sWS[wp], amt=16)
                        wdep = lambda hc, tw=tw: tw
                    pX = [pA, pB][oc % 2]
                    mm = None
                    for hc in range(16):
                        mm = E("tensor", lambda e, wp=wp, hc=hc, pX=pX: e.matmul(
                            pX[:, :], ws[wp][:, hc * 128:(hc + 1) * 128],
                            xn_sb[:, hc * TT:hc * TT + TT],
                            start=(hc == 0), stop=(hc == 15)),
                            deps=[wdep(hc), ln0_toks[hc],
                                  pX_free[oc % 2] if hc == 0 else None],
                            inc=sT if hc == 15 else None)
                    ws_free[wp] = mm
                    etok = evac(oc, pX, mm)
                    pX_free[oc % 2] = etok
                    toks.append(etok)
                    gather_pump(9)
                    if tile0_extra is not None:
                        mmg = None
                        for hc in range(16):
                            mmg = E("tensor", lambda e, wp=wp, hc=hc: e.matmul(
                                pSC0[:, 0:G], ws[wp][:, hc * 128:(hc + 1) * 128],
                                gfm_b[:, hc * G:(hc + 1) * G],
                                start=(hc == 0), stop=(hc == 15)),
                                deps=[t_gfm, pSC_free[0] if hc == 0 else None],
                                inc=sT if hc == 15 else None)
                        ws_free[wp] = mmg
                        pSC_free[0] = tile0_extra(oc, mmg)
                    if tau == 0:
                        ws_cast_deps[wp] = [ws_free[wp]] + stoks
                        cst["limit"] = max(cst["limit"],
                                           (m * 16 + oc + 5) * 4)
                        cast_pump(2)
                return toks

            def k_evac(oc, pX, mm):
                t = E("scalar", lambda e, oc=oc, pX=pX, buf=buf: e.copy(
                    KB(buf)[:, oc * TT:(oc + 1) * TT], pX[:, :]),
                    deps=[mm, kv_free[buf]], inc=sS)
                k_ready[buf][oc] = t
                return t

            def kg_extra(oc, mmg):
                t = E("vector", lambda e, oc=oc: e.tensor_copy(
                    kg_sb[:, oc * G:(oc + 1) * G], pSC0[:, 0:G]),
                    deps=[mmg], inc=sV)
                kg_ready[oc] = t
                return t

            proj(0, k_evac, kg_extra if tau == 0 else None)
            if STOP == "kproj" and tau == 0:
                E("sync", lambda e: e.nop(),
                  deps=[sm.cur for sm in sems.values()])
                finalize(nc, P)
                return nc

            def v_evac(oc, pX, mm):
                tv = E("vector", lambda e, pX=pX: e.tensor_copy(vtmp[:], pX[:, :]),
                       deps=[mm], inc=sV)
                if STOP == "vprojnotr":
                    last = None
                    for j in range(4):
                        last = E("vector", lambda e, j=j, oc=oc, buf=buf: e.tensor_copy(
                            VB(buf)[:, j * 2048 + oc * 128: j * 2048 + (oc + 1) * 128],
                            vtmp[:, j * 128:(j + 1) * 128]),
                            deps=[tv, kv_free[buf]], inc=sV)
                    pT_state["free"] = last
                    v_ready[buf][oc] = last
                    return last
                trs = []
                for j in range(4):
                    trs.append(E("tensor", lambda e, j=j: e.matmul(
                        pT[:, j * 128:(j + 1) * 128], vtmp[:, j * 128:(j + 1) * 128],
                        eye_b[:], start=True, stop=True),
                        deps=[tv, t_eyeb, pT_state["free"] if j == 0 else None],
                        inc=sT if j == 3 else None))
                last = None
                for j in range(4):
                    last = E("scalar", lambda e, j=j, oc=oc, buf=buf: e.copy(
                        VB(buf)[:, j * 2048 + oc * 128: j * 2048 + (oc + 1) * 128],
                        pT[:, j * 128:(j + 1) * 128]),
                        deps=[trs[3], kv_free[buf]], inc=sS)
                pT_state["free"] = last
                v_ready[buf][oc] = last
                return last

            def vg_extra(oc, mmg):
                tv = E("vector", lambda e: e.tensor_copy(
                    vgt[:, 0:G], pSC0[:, 0:G]), deps=[mmg], inc=sV)
                tr = E("tensor", lambda e: e.matmul(
                    pT[0:G, 0:128], vgt[:, 0:G], eye_b[:], start=True, stop=True),
                    deps=[tv, t_eyeb, pT_state["free"]], inc=sT)
                t = E("vector", lambda e, oc=oc: e.tensor_copy(
                    vg_sb[0:G, oc * 128:(oc + 1) * 128], pT[0:G, 0:128]),
                    deps=[tr], inc=sV)
                pT_state["free"] = t
                vg_ready[oc] = t
                return t

            proj(1, v_evac, vg_extra if (tau == 0 and STOP not in ('vprojnog', 'vprojnotr')) else None)
            if STOP in ("vproj", "vprojnog", "vprojnotr") and tau == 0:
                E("sync", lambda e: e.nop(),
                  deps=[sm.cur for sm in sems.values()])
                finalize(nc, P)
                return nc

            def q_evac(oc, pX, mm):
                return E("scalar", lambda e, oc=oc, pX=pX: e.mul(
                    q_sb[:, oc * TT:(oc + 1) * TT], pX[:, :], float(QSCALE)),
                    deps=[mm, q_free, t_cp1], inc=sS)

            q_toks = proj(2, q_evac, None)
            if STOP == "qproj" and tau == 0:
                E("sync", lambda e: e.nop(),
                  deps=[sm.cur for sm in sems.values()])
                finalize(nc, P)
                return nc

            psc3 = [pSC0, pSC1, pBC1]
            pTb3 = [pT, pA, pB]
            if tau == 0:
                att = {"sc": {}, "exp": {}, "ri": {},
                       "scl": {}, "tr": {}, "ate": {}, "av": {}, "ao": {}}
                attention_state["att"] = att
            att = attention_state["att"]
            git0 = tau * 64
            psc_init = [pSC_free[0], pSC_free[1], ln0_toks[-1]]
            pT_init = [pT_state["free"], pX_free[0], pX_free[1]]
            seen_bank = [False, False, False]
            seen_tbank = [False, False, False]
            seen_avb = [False, False]
            ao_toks = []

            def blocks_of(it):
                qc = it // NH
                a_blk = tau * 4 + qc
                return [a_blk - 2, a_blk - 1, a_blk]

            def sc_e(it):
                git = git0 + it
                b3 = git % 3
                psc = psc3[b3]
                qc, h = divmod(it, NH)
                qsl = q_sb[:, h * TT + qc * 128: h * TT + (qc + 1) * 128]
                extra = att["exp"].get(git - 3)
                if not seen_bank[b3]:
                    seen_bank[b3] = True
                    extra2 = psc_init[b3]
                else:
                    extra2 = None
                msel = (1 if (tau == 0 and qc == 0)
                        else (2 if (tau == 0 and qc == 1) else 0))
                E("tensor", lambda e, psc=psc, qsl=qsl, h=h: e.matmul(
                    psc[:, 0:G], qsl, kg_sb[:, h * G:(h + 1) * G],
                    start=True, stop=True),
                    deps=[q_toks[h], kg_ready[h], extra, extra2], inc=None)
                t_mm = None
                for j, bk in enumerate(blocks_of(it)):
                    if msel == 2:
                        mof = 384 if j == 0 else j * 128
                    else:
                        mof = msel * 384 + j * 128
                    E("tensor", lambda e, psc=psc, j=j, mof=mof: e.matmul(
                        psc[:, G + j * 128:G + (j + 1) * 128], eye_b[:],
                        msk_sb[:, mof:mof + 128],
                        start=True, stop=False),
                        deps=[t_msk, t_eyeb] if j == 0 else [], inc=None)
                    bb = (bk // 4) % 2
                    slot = bk % 4
                    t_mm = E("tensor", lambda e, psc=psc, qsl=qsl, bb=bb,
                             slot=slot, h=h, j=j: e.matmul(
                        psc[:, G + j * 128:G + (j + 1) * 128], qsl,
                        KB(bb)[:, h * TT + slot * 128: h * TT + (slot + 1) * 128],
                        start=False, stop=True),
                        deps=[k_ready[bb][h]], inc=sT if j == 2 else None)
                att["sc"][git] = t_mm

            def exp_e(it):
                git = git0 + it
                psc = psc3[git % 3]
                nb = git % 3
                att["exp"][git] = E("scalar", lambda e, psc=psc, nb=nb: e.activation(
                    out=aat[nb][:, 0:400], in_=psc[:, 0:400], func=ACT.Exp,
                    bias=0.0, scale=1.0, accum_out=sume[nb][:, :]),
                    deps=[att["sc"][git], att["tr"].get(git - 3),
                          att["ri"].get(git - 3)], inc=sS)

            def ri_e(it):
                git = git0 + it
                nb = git % 3
                att["ri"][git] = E("vector", lambda e, nb=nb: e.reciprocal(
                    rinv[nb][:], sume[nb][:]),
                    deps=[att["exp"][git], att["scl"].get(git - 3)], inc=sV)

            def scl_e(it):
                git = git0 + it
                nb = git % 3
                att["scl"][git] = E("vector", lambda e, nb=nb: e.tensor_scalar(
                    out=aat[nb][:, 0:400], in0=aat[nb][:, 0:400],
                    scalar1=rinv[nb][:, 0:1], scalar2=None, op0=ALU.mult),
                    deps=[att["ri"][git]], inc=sV)

            def tr_e(it):
                git = git0 + it
                b3 = git % 3
                pTb = pTb3[b3]
                if not seen_tbank[b3]:
                    seen_tbank[b3] = True
                    extra = pT_init[b3]
                else:
                    extra = None
                E("tensor", lambda e, pTb=pTb, b3=b3: e.matmul(
                    pTb[0:G, 0:128], aat[b3][:, 0:G], eye_b[:],
                    start=True, stop=True),
                    deps=[att["scl"][git], att["ate"].get(git - 3), extra,
                          t_eyeb], inc=None)
                tr_last = None
                for j in range(3):
                    tr_last = E("tensor", lambda e, j=j, pTb=pTb, b3=b3: e.matmul(
                        pTb[:, 128 * (j + 1):128 * (j + 2)],
                        aat[b3][:, G + j * 128:G + (j + 1) * 128], eye_b[:],
                        start=True, stop=True),
                        deps=[], inc=sT if j == 2 else None)
                att["tr"][git] = tr_last

            def ate_e(it):
                git = git0 + it
                b3 = git % 3
                att["ate"][git] = E("vector", lambda e, b3=b3: e.tensor_copy(
                    at_sb[b3][:], pTb3[b3][:, 0:512]),
                    deps=[att["tr"][git], att["av"].get(git - 3)], inc=sV)

            def av_e(it):
                git = git0 + it
                b3 = git % 3
                bk2 = git % 2
                pAVb = [pAV, pBC2][bk2]
                rv = ((git // 2) % 2) * 128
                qc, h = divmod(it, NH)
                if bk2 == 1 and not seen_avb[1]:
                    seen_avb[1] = True
                    extra = ln0_toks[-1]
                elif bk2 == 0 and not seen_avb[0]:
                    seen_avb[0] = True
                    extra = pAV_free
                else:
                    extra = None
                E("tensor", lambda e, h=h, b3=b3, rv=rv, pAVb=pAVb: e.matmul(
                    pAVb[:, rv:rv + 128],
                    vg_sb[0:G, h * 128:(h + 1) * 128],
                    at_sb[b3][0:G, 0:128], start=True, stop=False),
                    deps=[att["ate"][git], vg_ready[h],
                          att["ao"].get(git - 4), extra],
                    inc=None)
                t_av = None
                for j, bkk in enumerate(blocks_of(it)):
                    bb = (bkk // 4) % 2
                    slot = bkk % 4
                    t_av = E("tensor", lambda e, bb=bb, slot=slot, h=h, j=j,
                             b3=b3, rv=rv, pAVb=pAVb: e.matmul(
                        pAVb[:, rv:rv + 128],
                        VB(bb)[:, slot * 2048 + h * 128: slot * 2048 + (h + 1) * 128],
                        at_sb[b3][:, 128 * (j + 1):128 * (j + 2)],
                        start=False, stop=(j == 2)),
                        deps=[v_ready[bb][h]], inc=sT if j == 2 else None)
                att["av"][git] = t_av

            def ao_e(it):
                git = git0 + it
                bk2 = git % 2
                pAVb = [pAV, pBC2][bk2]
                rv = ((git // 2) % 2) * 128
                qc, h = divmod(it, NH)
                t_ao = E("scalar", lambda e, h=h, qc=qc, rv=rv, pAVb=pAVb: e.copy(
                    ao_sb[:, h * TT + qc * 128: h * TT + (qc + 1) * 128],
                    pAVb[:, rv:rv + 128]),
                    deps=[att["av"][git], ao_free], inc=sS)
                att["ao"][git] = t_ao
                ao_toks.append(t_ao)

            NIT = 4 * NH
            for p in range(NIT + 6):
                cast_pump(1)
                if p < NIT:
                    sc_e(p)
                if 0 <= p - 2 < NIT:
                    exp_e(p - 2)
                if 0 <= p - 3 < NIT:
                    ri_e(p - 3)
                    scl_e(p - 3)
                if 0 <= p - 4 < NIT:
                    tr_e(p - 4)
                if 0 <= p - 5 < NIT:
                    ate_e(p - 5)
                    av_e(p - 5)
                if 0 <= p - 6 < NIT:
                    ao_e(p - 6)

            glast = git0 + NIT - 1
            pSC_free[0] = att["exp"][[g for g in range(git0, git0 + NIT) if g % 3 == 0][-1]]
            pSC_free[1] = att["exp"][[g for g in range(git0, git0 + NIT) if g % 3 == 1][-1]]
            st8["pbc_free"] = att["exp"][[g for g in range(git0, git0 + NIT) if g % 3 == 2][-1]]
            pT_state["free"] = att["ate"][[g for g in range(git0, git0 + NIT) if g % 3 == 0][-1]]
            pX_free[0] = att["ate"][[g for g in range(git0, git0 + NIT) if g % 3 == 1][-1]]
            pX_free[1] = att["ate"][[g for g in range(git0, git0 + NIT) if g % 3 == 2][-1]]
            pAV_free = att["ao"][[g for g in range(git0, git0 + NIT) if g % 2 == 0][-1]]
            st8["pbc2_free"] = att["ao"][[g for g in range(git0, git0 + NIT) if g % 2 == 1][-1]]
            score_last_tok = att["sc"][glast]
            q_free = score_last_tok
            kv_free[0] = score_last_tok
            kv_free[1] = score_last_tok
            xn_free = q_toks[-1]
            last_av_tok = ao_toks[-1]
            ao_head = [ao_toks[3 * NH + h] for h in range(NH)]
            if STOP == "attn" and tau == 0:
                E("sync", lambda e: e.nop(),
                  deps=[sm.cur for sm in sems.values()])
                finalize(nc, P)
                return nc

            wo_resid = []
            ln1_state = {}
            wo_mm_last = None
            for oc in range(16):
                wp = oc % 4
                if tau == 0:
                    ctoks, stoks = cast_direct(3, oc)
                    wdep = lambda hc, ctoks=ctoks: ctoks[hc // 4]
                else:
                    base = 3 * 256 + oc * 16
                    r0 = base * 128
                    tw = E("sync", lambda e, wp=wp, r0=r0: e.dma_start(
                        out=ws[wp][:], in_=wscr[:, r0:r0 + 2048]),
                        deps=[ws_free[wp]] + ws_cast_deps[wp] + scr_deps(3, oc),
                        inc=sWS[wp], amt=16)
                    wdep = lambda hc, tw=tw: tw
                pX = [pA, pB][oc % 2]
                mm = None
                for hc in range(16):
                    mm = E("tensor", lambda e, wp=wp, hc=hc, pX=pX: e.matmul(
                        pX[:, :], ws[wp][:, hc * 128:(hc + 1) * 128],
                        ao_sb[:, hc * TT:hc * TT + TT],
                        start=(hc == 0), stop=(hc == 15)),
                        deps=[wdep(hc), ao_head[hc],
                              pX_free[oc % 2] if hc == 0 else None],
                        inc=sT if hc == 15 else None)
                ws_free[wp] = mm
                if tau == 0:
                    ws_cast_deps[wp] = [mm] + stoks
                    cst["limit"] = max(cst["limit"], (48 + oc + 5) * 4)
                wo_mm_last = mm
                tr = E("vector", lambda e, oc=oc, pX=pX: e.scalar_tensor_tensor(
                    out=x_sb[:, oc * TT:(oc + 1) * TT],
                    in0=x_sb[:, oc * TT:(oc + 1) * TT], scalar=2.0,
                    in1=pX[:, :], op0=ALU.mult, op1=ALU.add),
                    deps=[mm, x_chunk[oc]], inc=sV)
                pX_free[oc % 2] = tr
                wo_resid.append(tr)
                gather_pump(9)
            ao_free = wo_mm_last

            if STOP == "y1":
                for dc in range(NC):
                    h_dma[dc] = E("gpsimd", lambda e, dc=dc, tau=tau: e.dma_start(
                        out=y_out[dc * 128:(dc + 1) * 128, tau * TT:(tau + 1) * TT],
                        in_=x_sb[:, dc * TT:(dc + 1) * TT]),
                        deps=[wo_resid[dc]], inc=sST, amt=16)
                x_free = wo_resid[-1]
                continue

            for dc in range(NC):
                ln_stat_chunk(x_sb, dc, wo_resid[dc], ln1_state)
            ln1_toks = ln_finish(x_sb, None, ln1_state)
            prev_h = h_dma[NC // 2 - 1] if tau > 0 else None
            for j in range(NC // 2):
                prev_h = E("scalar", lambda e, j=j, tau=tau: e.dma_start(
                    out=hscr[tau][:, 2 * j * TT:(2 * j + 2) * TT],
                    in_=x_sb[:, 2 * j * TT:(2 * j + 2) * TT]),
                    deps=[ln1_toks[2 * j], ln1_toks[2 * j + 1], prev_h],
                    inc=sHS, amt=16)
                h_dma[j] = prev_h
            prev_ln1_toks = ln1_toks
            if STOP == "h":
                for dc in range(NC):
                    E("gpsimd", lambda e, dc=dc, tau=tau: e.dma_start(
                        out=y_out[dc * 128:(dc + 1) * 128, tau * TT:(tau + 1) * TT],
                        in_=x_sb[:, dc * TT:(dc + 1) * TT]),
                        deps=[ln1_toks[dc]], inc=sST, amt=16)
            x_free = ln1_toks[-1]

        cast_pump(10 ** 9)
        gather_pump(10 ** 9)
        t_ag = E("gpsimd", lambda e: e.collective_compute(
            "AllGather", ALU.bypass, replica_groups=[list(range(NCORES))],
            ins=[agin.opt()], outs=[agout.opt()]),
            deps=[sm.cur for sm in sAG], inc=sCC)

        if STOP in ("ln0", "y1", "h"):
            E("sync", lambda e: e.nop(),
              deps=[sm.cur for sm in sems.values()])
            finalize(nc, P)
            return nc

        def ag_pieces(tile0, ntiles):
            pieces = []
            t = tile0
            off = 0
            while ntiles > 0:
                cb = t // PER_CORE
                rel = t % PER_CORE
                n = min(ntiles, PER_CORE - rel)
                pieces.append((cb, rel, off, n))
                t += n
                off += n
                ntiles -= n
            return pieces

        def ag_load(dest, tile0, ntiles, deps, sem):
            tok = None
            for cb, rel, off, n in ag_pieces(tile0, ntiles):
                tok = E("sync", lambda e, cb=cb, rel=rel, off=off, n=n, dest=dest:
                        e.dma_start(
                    out=dest[:, off * 128:(off + n) * 128],
                    in_=agout[cb][:, rel * 128:(rel + n) * 128]),
                    deps=deps, inc=sem, amt=16)
            return tok

        h2_ready = [None] * NI
        wd_free = [q_free, ao_free]
        pG_free = [wo_resid[14], pSC_free[0]]
        pU_free = [wo_resid[15], pSC_free[1]]
        pD_free = [pT_state["free"], pAV_free]
        sg_free = [None, None]
        y_dma = [None] * NC
        hload_free = list(h_dma)
        xn_cons = xn_free
        h2_cons = last_av_tok
        q_ao_free = ao_free

        XN2 = AR[:, NI * TT:NI * TT + NC * TT]
        pre_hb = {}
        cin_free2 = list(cast_in_free[:3])
        xnb_cons = [None, last_av_tok]

        def prefetch_chunk(t, dc):
            p = dc % 3
            buf = xn_sb if t % 2 == 0 else XN2
            tpin = E("scalar", lambda e, p=p, t=t, dc=dc: e.dma_start(
                out=cin[p][:], in_=hscr[t][:, dc * TT:(dc + 1) * TT]),
                deps=[cin_free2[p]], inc=sCI[p], amt=16)
            tc = E("vector", lambda e, p=p, dc=dc, buf=buf: e.tensor_copy(
                buf[:, dc * TT:(dc + 1) * TT], cin[p][:]),
                deps=[tpin, xnb_cons[t % 2]], inc=sV)
            cin_free2[p] = tc
            pre_hb.setdefault(t, []).append(tc)

        for tau in range(NTILE):
            if tau == 0:
                hb_load = []
                hb_toks = []
                for dc in range(NC):
                    p = dc % 3
                    tok = E("scalar", lambda e, dc=dc, p=p: e.dma_start(
                        out=x_sb[:, dc * TT:(dc + 1) * TT],
                        in_=hscr[0][:, dc * TT:(dc + 1) * TT]),
                        deps=[prev_ln1_toks[dc], h_dma[dc // 2], xchain[p]],
                        inc=sCI[p], amt=16)
                    xchain[p] = tok
                    hb_load.append(tok)
                    hb_toks.append(E("vector", lambda e, dc=dc: e.tensor_copy(
                        xn_sb[:, dc * TT:(dc + 1) * TT],
                        x_sb[:, dc * TT:(dc + 1) * TT]),
                        deps=[tok, xn_cons], inc=sV))
            else:
                t_h = E("scalar", lambda e, tau=tau: e.dma_start(
                    out=x_sb[:], in_=hscr[tau]),
                    deps=[x_free] + list(y_dma), inc=sX, amt=16)
                hb_toks = pre_hb[tau]
            XN = xn_sb if tau % 2 == 0 else XN2

            last_mm_u = None
            for ic in range(NI):
                wpg = (2 * ic) % 4
                wpu = (2 * ic + 1) % 4
                twg = ag_load(ws[wpg], ic * 16, 16,
                              [t_ag, ws_free[wpg]], sWS[wpg])
                twu = ag_load(ws[wpu], N_G + ic * 16, 16,
                              [t_ag, ws_free[wpu]], sWS[wpu])
                pg = [pA, pSC0][ic % 2]
                pu = [pB, pSC1][ic % 2]
                mmg = None
                for hc in range(16):
                    mmg = E("tensor", lambda e, wpg=wpg, hc=hc, pg=pg, XN=XN:
                        e.matmul(
                        pg[:, :], ws[wpg][:, hc * 128:(hc + 1) * 128],
                        XN[:, hc * TT:hc * TT + TT],
                        start=(hc == 0), stop=(hc == 15)),
                        deps=[twg, hb_toks[hc],
                              pG_free[ic % 2] if hc == 0 else None],
                        inc=sT if hc == 15 else None)
                mmu = None
                for hc in range(16):
                    mmu = E("tensor", lambda e, wpu=wpu, hc=hc, pu=pu, XN=XN:
                        e.matmul(
                        pu[:, :], ws[wpu][:, hc * 128:(hc + 1) * 128],
                        XN[:, hc * TT:hc * TT + TT],
                        start=(hc == 0), stop=(hc == 15)),
                        deps=[twu, hb_toks[hc],
                              pU_free[ic % 2] if hc == 0 else None],
                        inc=sT if hc == 15 else None)
                ws_free[wpg] = mmg
                ws_free[wpu] = mmu
                last_mm_u = mmu
                sp = ic % 2
                t_si = E("scalar", lambda e, sp=sp, pg=pg: e.activation(
                    out=sg[sp][:], in_=pg[:, :], func=ACT.Silu),
                    deps=[mmg, sg_free[sp]], inc=sS)
                pG_free[ic % 2] = t_si
                t_h2 = E("vector", lambda e, sp=sp, ic=ic, pu=pu: e.tensor_tensor(
                    out=H2[:, ic * TT:(ic + 1) * TT], in0=sg[sp][:], in1=pu[:, :],
                    op=ALU.mult), deps=[t_si, mmu, h2_cons], inc=sV)
                pU_free[ic % 2] = t_h2
                sg_free[sp] = t_h2
                h2_ready[ic] = t_h2
            xnb_cons[tau % 2] = last_mm_u

            down_resid = []
            ln2_state = {}
            last_mm_d = None
            for oc in range(16):
                wd = [q_sb, ao_sb][oc % 2]
                twd = ag_load(wd, 2 * N_G + oc * NI, NI,
                              [t_ag, wd_free[oc % 2], q_ao_free], sWD[oc % 2])
                pD = [pT, pAV][oc % 2]
                mmd = None
                for ic in range(NI):
                    mmd = E("tensor", lambda e, wd=wd, ic=ic, pD=pD: e.matmul(
                        pD[:, :], wd[:, ic * 128:(ic + 1) * 128],
                        H2[:, ic * TT:(ic + 1) * TT],
                        start=(ic == 0), stop=(ic == NI - 1)),
                        deps=[twd, h2_ready[ic],
                              pD_free[oc % 2] if ic == 0 else None],
                        inc=sT if ic == NI - 1 else None)
                wd_free[oc % 2] = mmd
                last_mm_d = mmd
                tr = E("vector", lambda e, oc=oc, pD=pD: e.tensor_tensor(
                    out=x_sb[:, oc * TT:(oc + 1) * TT],
                    in0=x_sb[:, oc * TT:(oc + 1) * TT], in1=pD[:, :], op=ALU.add),
                    deps=[mmd, hb_load[oc] if tau == 0 else t_h], inc=sV)
                pD_free[oc % 2] = tr
                down_resid.append(tr)
                if tau + 1 < NTILE:
                    prefetch_chunk(tau + 1, oc)
            q_ao_free = None
            h2_cons = last_mm_d
            st8["pbc_free"] = down_resid[-1]

            for dc in range(NC):
                ln_stat_chunk(x_sb, dc, down_resid[dc], ln2_state)
            ln2_toks = ln_finish(x_sb, None, ln2_state)
            for dc in range(NC):
                y_dma[dc] = E("scalar", lambda e, dc=dc, tau=tau: e.dma_start(
                    out=y_out[dc * 128:(dc + 1) * 128, tau * TT:(tau + 1) * TT],
                    in_=x_sb[:, dc * TT:(dc + 1) * TT]),
                    deps=[ln2_toks[dc]], inc=sST, amt=16)
            x_free = ln2_toks[-1]

        E("sync", lambda e: e.nop(),
          deps=[sm.cur for sm in sems.values()])
        finalize(nc, P)
    return nc


def finalize(nc, P):
    with nc.Block() as block:
        block.sync(lambda e: P.run("sync", e))
        block.vector(lambda e: P.run("vector", e))
        block.scalar(lambda e: P.run("scalar", e))
        block.tensor(lambda e: P.run("tensor", e))
        block.gpsimd(lambda e: P.run("gpsimd", e))



def _tileize(w):
    w4 = w.reshape(16, 128, 16, 128)
    t = w4.transpose(0, 2, 3, 1).reshape(256, 128, 128)
    return np.ascontiguousarray(t.transpose(1, 0, 2).reshape(128, 256 * 128))


def _masks():
    q = np.arange(128)[:, None]
    k = np.arange(384)[None, :]
    base = np.where((k >= q + 1) & (k <= q + 256), 0.0, NEG).astype(np.float32)
    m0 = np.where(k >= 256, base, NEG).astype(np.float32)
    return np.ascontiguousarray(np.concatenate([base, m0], axis=1))


def _idx_pack(gate_idx, up_idx, down_idx):
    gT = np.ascontiguousarray(gate_idx.T)
    uT = np.ascontiguousarray(up_idx.T)
    dT = np.ascontiguousarray(down_idx.T)
    jobs = ffn_jobs()
    packs = []
    for c in range(NCORES):
        part = jobs[c * PER_CORE:(c + 1) * PER_CORE]
        tiles = np.empty((PER_CORE, 128, 128), np.int32)
        for j, (m, a, b) in enumerate(part):
            if m == "g":
                tiles[j] = gT[b * 128:(b + 1) * 128, a * 128:(a + 1) * 128]
            elif m == "u":
                tiles[j] = uT[b * 128:(b + 1) * 128, a * 128:(a + 1) * 128]
            else:
                tiles[j] = dT[b * 128:(b + 1) * 128, a * 128:(a + 1) * 128]
        packs.append(np.ascontiguousarray(
            tiles.transpose(1, 0, 2).reshape(128, PER_CORE * 128)))
    return packs


_prog_cache = {}


def _get_prog():
    key = STOP
    if key not in _prog_cache:
        _prog_cache[key] = build_program()
    return _prog_cache[key]


def kernel(L0_attn_x, L0_attn_global_tokens, L0_attn_ln_scale, L0_attn_ln_bias,
           L0_attn_Wq, L0_attn_Wk, L0_attn_Wv, L0_attn_Wo,
           L0_ffn_lut, L0_ffn_gate_idx, L0_ffn_up_idx, L0_ffn_down_idx):
    x = np.asarray(L0_attn_x, dtype=np.float32)
    g = np.asarray(L0_attn_global_tokens, dtype=np.float32)
    gam = np.asarray(L0_attn_ln_scale, dtype=np.float32)
    bet = np.asarray(L0_attn_ln_bias, dtype=np.float32)
    wq = np.asarray(L0_attn_Wq, dtype=np.float32)
    wk = np.asarray(L0_attn_Wk, dtype=np.float32)
    wv = np.asarray(L0_attn_Wv, dtype=np.float32)
    wo = np.asarray(L0_attn_Wo, dtype=np.float32)
    lut = np.asarray(L0_ffn_lut, dtype=np.float32)
    gidx = np.asarray(L0_ffn_gate_idx, dtype=np.int32)
    uidx = np.asarray(L0_ffn_up_idx, dtype=np.int32)
    didx = np.asarray(L0_ffn_down_idx, dtype=np.int32)

    wq_t, wk_t, wv_t, wo_t = map(_tileize, (wq, wk, wv, wo))
    g_t = np.ascontiguousarray(g[0].T)
    gam_t = np.ascontiguousarray(gam.reshape(16, 128).T)
    bet_t = np.ascontiguousarray(bet.reshape(16, 128).T)
    masks = _masks()
    eye = np.eye(128, dtype=np.float32)
    packs = _idx_pack(gidx, uidx, didx)

    nc = _get_prog()
    in_maps = []
    for c in range(NCORES):
        in_maps.append({
            "x_t": np.ascontiguousarray(
                x[c].T.reshape(NC, 128, NTILE, TT).transpose(2, 1, 0, 3)
                .reshape(NTILE, 128, NC * TT)),
            "wk_t": wk_t, "wv_t": wv_t, "wq_t": wq_t, "wo_t": wo_t,
            "idxp": packs[c],
            "lut_in": lut,
            "g_in": g_t,
            "gam_in": gam_t,
            "bet_in": bet_t,
            "eye_in": eye,
            "msk_in": masks,
        })
    res = run_bass_kernel_spmd(nc, in_maps, core_ids=list(range(NCORES)))
    out = np.stack([np.ascontiguousarray(r["y"].T) for r in res.results])
    return out


# version 31
# speedup vs baseline: 1.0069x; 1.0069x over previous
import numpy as np
import ml_dtypes
from contextlib import ExitStack

import concourse.bass as bass
import concourse.mybir as mybir
from concourse.bass_utils import run_bass_kernel_spmd

F32 = mybir.dt.float32
BF16 = mybir.dt.bfloat16
I32 = mybir.dt.int32
AX = mybir.AxisListType.X
ALU = mybir.AluOpType
ACT = mybir.ActivationFunctionType

D = 2048
T = 2048
NH = 16
G = 16
NC = 16
TT = 512
NTILE = T // TT
INTER = 5504
NI = INTER // 128
NCORES = 8
QSCALE = 1.0 / float(np.sqrt(128.0))
EPS = 1e-5
NEG = -30000.0

N_G = NI * 16
N_TILES_FFN = 3 * N_G
PER_CORE = N_TILES_FFN // NCORES
GATHER_BS = 6
CAST_BS = 4
N_ATTN_TILES = 4 * 256

STOP = "full"

ENGINES = ("sync", "vector", "scalar", "tensor", "gpsimd")


class Sem:
    __slots__ = ("name", "h", "val")

    def __init__(self, name):
        self.name = name
        self.h = None
        self.val = 0

    def tok(self, k=1):
        self.val += k
        return (self, self.val)

    @property
    def cur(self):
        return (self, self.val)


class Plan:
    def __init__(self):
        self.ops = {e: [] for e in ENGINES}
        self.waited = {e: {} for e in ENGINES}

    def emit(self, eng, fn, deps=(), inc=None, amt=1):
        need = {}
        for d in deps:
            if d is None:
                continue
            sem, val = d
            if val > need.get(sem, 0):
                need[sem] = val
        ws = []
        for sem, val in need.items():
            if val > self.waited[eng].get(sem, 0):
                ws.append((sem, val))
                self.waited[eng][sem] = val
        tok = inc.tok(amt) if inc is not None else None
        self.ops[eng].append((ws, fn, inc, amt))
        return tok

    def run(self, eng_name, e):
        for ws, fn, inc, amt in self.ops[eng_name]:
            for sem, val in ws:
                e.wait_ge(sem.h, val)
            inst = fn(e)
            if inc is not None:
                inst.then_inc(inc.h, amt)


def ffn_jobs():
    jobs = []
    for ic in range(NI):
        for hc in range(16):
            jobs.append(("g", ic, hc))
    for ic in range(NI):
        for hc in range(16):
            jobs.append(("u", ic, hc))
    for oc in range(16):
        for ic in range(NI):
            jobs.append(("d", oc, ic))
    return jobs


def build_program():
    nc = bass.Bass("TRN2", target_bir_lowering=False, debug=False)

    x_t = nc.dram_tensor("x_t", [NTILE, 128, NC * TT], F32, kind="ExternalInput").ap()
    winp = [
        nc.dram_tensor(n, [128, 256 * 128], F32, kind="ExternalInput").ap()
        for n in ("wk_t", "wv_t", "wq_t", "wo_t")
    ]
    idxp = nc.dram_tensor("idxp", [128, PER_CORE * 128], I32, kind="ExternalInput").ap()
    lut_in = nc.dram_tensor("lut_in", [16], F32, kind="ExternalInput").ap()
    g_in = nc.dram_tensor("g_in", [D, G], F32, kind="ExternalInput").ap()
    gam_in = nc.dram_tensor("gam_in", [128, 16], F32, kind="ExternalInput").ap()
    bet_in = nc.dram_tensor("bet_in", [128, 16], F32, kind="ExternalInput").ap()
    eye_in = nc.dram_tensor("eye_in", [128, 128], F32, kind="ExternalInput").ap()
    msk_in = nc.dram_tensor("msk_in", [128, 2 * 384], F32, kind="ExternalInput").ap()

    y_out = nc.dram_tensor("y", [D, T], F32, kind="ExternalOutput").ap()

    wscr = nc.dram_tensor("wscr", [128, N_ATTN_TILES * 128], BF16).ap()
    agin = nc.dram_tensor("agin", [128, PER_CORE * 128], BF16).ap()
    agout = nc.dram_tensor("agout", [NCORES, 128, PER_CORE * 128], BF16).ap()
    hscr = nc.dram_tensor("hscr", [NTILE, 128, NC * TT], F32).ap()

    with ExitStack() as ctx:
        sb = lambda n, s, d=F32: ctx.enter_context(nc.sbuf_tensor(n, s, d))
        psum = lambda n, s, d=F32: ctx.enter_context(nc.psum_tensor(n, s, d))

        x_sb = sb("x_sb", [128, NC * TT])
        xn_sb = sb("xn_sb", [128, NC * TT], BF16)
        q_sb = sb("q_sb", [128, NH * TT], BF16)
        ao_sb = sb("ao_sb", [128, NH * TT], BF16)
        AR = sb("AR", [128, 4 * 8192], BF16)
        ws = [sb(f"ws{i}", [128, 2048], BF16) for i in range(4)]
        vtmp = sb("vtmp", [128, 512], BF16)
        vgt = sb("vgt", [128, 16], BF16)
        at_sb = [sb(f"at_sb{i}", [128, 512], BF16) for i in range(3)]
        aat = [sb(f"aat{i}", [128, 512], BF16) for i in range(3)]
        msk_sb = sb("msk_sb", [128, 2 * 384], BF16)
        eye_b = sb("eye_b", [128, 128], BF16)
        gfm_b = sb("gfm_b", [128, NC * G], BF16)
        kg_sb = sb("kg_sb", [128, NH * G], BF16)
        vg_sb = sb("vg_sb", [16, D], BF16)
        gam_sb = sb("gam_sb", [128, 16])
        bet_sb = sb("bet_sb", [128, 16])
        lut_sb = sb("lut_sb", [128, 16])
        ones_b = sb("ones_b", [128, 1], BF16)
        cpool = sb("cpool", [128, 2])
        stf = [sb(f"stf{i}", [128, 512]) for i in range(2)]
        stb = [sb(f"stb{i}", [128, 512], BF16) for i in range(2)]
        sq = [sb(f"sq{i}", [128, 512], BF16) for i in range(2)]
        sg = [sb(f"sg{i}", [128, 512], BF16) for i in range(2)]
        idxi = [sb("idxi0", [128, 768], I32)]
        idxb = [sb("idxb0", [128, 768], BF16)]
        term = [sb("term0", [128, 768], BF16)]
        gout = [sb(f"gout{i}", [128, 768], BF16) for i in range(2)]
        cin = [sb(f"cin{i}", [128, 512]) for i in range(3)]
        smalls = sb("smalls", [1, 1024])
        bcst_t = sb("bcst", [1, 1152])
        bcst = bcst_t[:, 0:1024]
        ones_f = bcst_t[:, 1024:1152]
        negmax = [sb(f"negmax{i}", [128, 1]) for i in range(3)]
        sume = [sb(f"sume{i}", [128, 1]) for i in range(3)]
        rinv = [sb(f"rinv{i}", [128, 1]) for i in range(3)]

        nc.const_aps.register(F32, -0.5, cpool[:, 0:1])
        nc.const_aps.register(F32, float(QSCALE), cpool[:, 1:2])

        KB = lambda b: AR[:, b * 8192 : (b + 1) * 8192]
        VB = lambda b: AR[:, (2 + b) * 8192 : (3 + b) * 8192]
        H2 = AR

        pA = psum("pA", [128, 512])
        pB = psum("pB", [128, 512])
        pBC1 = psum("pBC1", [128, 512])
        pBC2 = psum("pBC2", [128, 512])
        pSC0 = psum("pSC0", [128, 512])
        pSC1 = psum("pSC1", [128, 512])
        pT = psum("pT", [128, 512])
        pAV = psum("pAV", [128, 512])

        names = (["sK", "sX", "sHS", "sST", "sV", "sS", "sT", "sCC"]
                 + [f"sCI{i}" for i in range(4)] + [f"sGI{i}" for i in range(4)]
                 + [f"sCO{i}" for i in range(4)] + [f"sAG{i}" for i in range(4)]
                 + ["sWD0", "sWD1"]
                 + [f"sWS{i}" for i in range(4)])
        sems = {n: Sem(n) for n in names}
        for sm in sems.values():
            sm.h = ctx.enter_context(nc.semaphore(sm.name))
        sK = sems["sK"]; sX = sems["sX"]; sHS = sems["sHS"]; sST = sems["sST"]
        sV = sems["sV"]; sS = sems["sS"]; sT = sems["sT"]; sCC = sems["sCC"]
        sCI = [sems[f"sCI{i}"] for i in range(4)]
        sGI = [sems[f"sGI{i}"] for i in range(4)]
        sCO = [sems[f"sCO{i}"] for i in range(4)]
        sAG = [sems[f"sAG{i}"] for i in range(4)]
        sWD = [sems["sWD0"], sems["sWD1"]]
        sWS = [sems[f"sWS{i}"] for i in range(4)]

        P = Plan()
        E = P.emit

        E("sync", lambda e: e.dma_start(
            out=sq[0][:].bitcast(F32)[:, 0:NC * G].rearrange("p (hc k) -> p hc k", hc=NC),
            in_=g_in.rearrange("(hc p) k -> p hc k", p=128)), inc=sK, amt=16)
        E("sync", lambda e: e.dma_start(out=gam_sb[:], in_=gam_in), inc=sK, amt=16)
        E("sync", lambda e: e.dma_start(out=bet_sb[:], in_=bet_in), inc=sK, amt=16)
        E("sync", lambda e: e.dma_start(
            out=lut_sb[:, :], in_=lut_in[None, :].partition_broadcast(128)), inc=sK, amt=16)
        t_consts = sK.cur
        t_gam = t_bet = t_lut = t_consts
        t_mskf0 = E("sync", lambda e: e.dma_start(out=stf[0][:, 0:384], in_=msk_in[:, 0:384]), inc=sCI[3], amt=16)
        t_mskf1 = E("sync", lambda e: e.dma_start(out=stf[1][:, 0:384], in_=msk_in[:, 384:768]), inc=sGI[3], amt=16)
        t_mc0 = E("vector", lambda e: e.tensor_copy(msk_sb[:, 0:384], stf[0][:, 0:384]),
                  deps=[t_mskf0], inc=sV)
        t_mc1 = E("vector", lambda e: e.tensor_copy(msk_sb[:, 384:768], stf[1][:, 0:384]),
                  deps=[t_mskf1], inc=sV)
        t_eyef = E("sync", lambda e: e.dma_start(out=stf[1][:, 0:128], in_=eye_in),
                   deps=[t_mc1], inc=sGI[3], amt=16)
        t_eyeb = E("vector", lambda e: e.tensor_copy(eye_b[:], stf[1][:, 0:128]),
                   deps=[t_eyef], inc=sV)
        t_gfm = E("vector", lambda e: e.tensor_copy(
            gfm_b[:], sq[0][:].bitcast(F32)[:, 0:NC * G]), deps=[t_consts], inc=sV)
        t_msk = t_mc1
        t_x0 = E("scalar", lambda e: e.dma_start(
            out=x_sb[:], in_=x_t[0]), inc=sX, amt=16)
        t_ones = E("vector", lambda e: e.memset(ones_b[:], 1.0), inc=sV)
        t_onesf = E("vector", lambda e: e.memset(ones_f[:], 1.0), inc=sV)
        t_cp0 = E("vector", lambda e: e.memset(cpool[:, 0:1], -0.5), inc=sV)
        t_cp1 = E("vector", lambda e: e.memset(cpool[:, 1:2], float(QSCALE)), inc=sV)
        t_z1 = E("vector", lambda e: e.memset(KB(1)[:, :], 0.0), inc=sV)
        t_z2 = E("vector", lambda e: e.memset(VB(1)[:, :], 0.0), inc=sV)

        NCB = 256 // CAST_BS
        scr_ready = {}
        cast_in_free = [None, None, None, None, None]
        cast_store_prev = [None, None]
        ws_cast_deps = [[], [], [], []]
        cst = {"i": 0, "done": False, "limit": 16, "tok": {}, "st": {}}

        def cast_pump(n):
            for _ in range(n):
                if cst["done"] or cst["i"] >= cst["limit"]:
                    return
                i = cst["i"]
                m, b = divmod(i, NCB)
                p = i % 5
                land = cin[p] if p < 3 else stf[p - 3]
                psem = sCI[p] if p < 4 else sGI[2]
                s = i % 2
                wp = (i // 4) % 4
                sub = i % 4
                t0 = b * CAST_BS
                src = winp[m][:, t0 * 128:t0 * 128 + 512]
                first_stf = p >= 3 and cast_in_free[p] is None
                tin = E("sync", lambda e, src=src, land=land: e.dma_start(
                    out=land[:], in_=src),
                    deps=[cast_in_free[p]]
                         + ([st8["stf_free"][p - 3]] if first_stf else []),
                    inc=psem, amt=16)
                tc = E("vector", lambda e, land=land, wp=wp, sub=sub:
                       e.tensor_copy(
                    ws[wp][:, sub * 512:(sub + 1) * 512], land[:]),
                    deps=[tin] + ws_cast_deps[wp], inc=sV)
                cast_in_free[p] = tc
                r0 = (m * 256 + t0) * 128
                tst = E("scalar", lambda e, wp=wp, sub=sub, r0=r0: e.dma_start(
                    out=wscr[:, r0:r0 + 512],
                    in_=ws[wp][:, sub * 512:(sub + 1) * 512]),
                    deps=[tc, cast_store_prev[s]], inc=sCO[s], amt=16)
                cast_store_prev[s] = tst
                scr_ready[(m, b)] = tst
                cst["tok"][i] = tc
                cst["st"][i] = tst
                cst["i"] = i + 1
                if cst["i"] >= 4 * NCB:
                    cst["done"] = True
                    st8["stf_free"][0] = cast_in_free[3]
                    st8["stf_free"][1] = cast_in_free[4]

        def cast_direct(m, oc):
            need = m * 64 + oc * 4
            while cst["i"] < need + 4:
                assert cst["i"] < cst["limit"]
                cast_pump(1)
            return [cst["tok"][need + k] for k in range(4)],                    [cst["st"][need + k] for k in range(4)]

        def scr_deps(m, oc):
            bb = oc * 4
            assert cst["done"]
            return [scr_ready[(m, bb + k)] for k in range(4)]

        if STOP == "p1":
            cst["limit"] = 10 ** 9
            cast_pump(10 ** 9)
            E("sync", lambda e: e.nop(),
              deps=[sm.cur for sm in sems.values()])
            finalize(nc, P)
            return nc

        st8 = {
            "stb_free": [None, None],
            "stf_free": [t_mc0, t_eyeb],
            "sq_free": [t_gfm, None],
            "pbc_free": None,
            "pbc2_free": None,
        }

        def ln_stat_chunk(xin, dc, dep, state):
            sl = slice(dc * TT, (dc + 1) * TT)
            p = dc % 2
            tcast = E("vector", lambda e, p=p, sl=sl, xin=xin: e.tensor_copy(
                stb[p][:], xin[:, sl]),
                deps=[st8["stb_free"][p], dep], inc=sV)
            state["sum_last"] = E("tensor", lambda e, p=p, dc=dc: e.matmul(
                pBC1[0:1, :], ones_b[:], stb[p][:],
                start=(dc == 0), stop=(dc == NC - 1)),
                deps=[tcast, t_ones,
                      st8["pbc_free"] if dc == 0 else None], inc=sT)
            st8["stb_free"][p] = state["sum_last"]
            tsq = E("scalar", lambda e, p=p, sl=sl, xin=xin: e.square(
                sq[p][:], xin[:, sl]),
                deps=[st8["sq_free"][p], dep], inc=sS)
            state["sq_last"] = E("tensor", lambda e, p=p, dc=dc: e.matmul(
                pBC2[0:1, :], ones_b[:], sq[p][:],
                start=(dc == 0), stop=(dc == NC - 1)),
                deps=[tsq, t_ones,
                      st8["pbc_free"] if dc == 0 else None,
                      st8["pbc2_free"] if dc == 0 else None], inc=sT)
            st8["sq_free"][p] = state["sq_last"]

        def ln_finish(xin, gamma_beta, state):
            sum_last = state["sum_last"]
            sq_last = state["sq_last"]
            t_mu = E("vector", lambda e: e.tensor_scalar(
                out=smalls[:, 0:512], in0=pBC1[0:1, :], scalar1=1.0 / D,
                scalar2=None, op0=ALU.mult), deps=[sum_last], inc=sV)
            t_e2 = E("vector", lambda e: e.tensor_scalar(
                out=smalls[:, 512:1024], in0=pBC2[0:1, :], scalar1=1.0 / D,
                scalar2=float(EPS), op0=ALU.mult, op1=ALU.add),
                deps=[sq_last], inc=sV)
            t_msq = E("vector", lambda e: e.tensor_tensor(
                out=bcst[:, 0:512], in0=smalls[:, 0:512],
                in1=smalls[:, 0:512], op=ALU.mult), deps=[t_mu], inc=sV)
            t_var = E("vector", lambda e: e.tensor_tensor(
                out=smalls[:, 512:1024], in0=smalls[:, 512:1024],
                in1=bcst[:, 0:512], op=ALU.subtract),
                deps=[t_e2, t_msq], inc=sV)
            t_ln = E("scalar", lambda e: e.activation(
                out=smalls[:, 512:1024], in_=smalls[:, 512:1024],
                func=ACT.Ln, bias=0.0, scale=1.0), deps=[t_var], inc=sS)
            t_rstd = E("scalar", lambda e: e.activation(
                out=bcst[:, 512:1024], in_=smalls[:, 512:1024],
                func=ACT.Exp, bias=0.0, scale=-0.5),
                deps=[t_ln, t_cp0], inc=sS)
            t_mrs = E("vector", lambda e: e.tensor_tensor(
                out=bcst[:, 0:512], in0=smalls[:, 0:512],
                in1=bcst[:, 512:1024], op=ALU.mult), deps=[t_rstd], inc=sV)
            t_bc1 = E("tensor", lambda e: e.matmul(
                pBC1[:, :], ones_f[:], bcst[:, 0:512], start=True, stop=True),
                deps=[t_mrs, t_onesf, t_mu], inc=sT)
            t_bc2 = E("tensor", lambda e: e.matmul(
                pBC2[:, :], ones_f[:], bcst[:, 512:1024], start=True, stop=True),
                deps=[t_rstd, t_onesf, t_e2, st8["pbc2_free"]], inc=sT)
            chunk_toks = []
            for dc in range(NC):
                sl = slice(dc * TT, (dc + 1) * TT)
                p = dc % 2
                t1 = E("vector", lambda e, p=p, sl=sl, xin=xin: e.tensor_tensor(
                    out=stf[p][:], in0=xin[:, sl], in1=pBC2[:, :], op=ALU.mult),
                    deps=[t_bc2, st8["stf_free"][p]], inc=sV)
                if gamma_beta is None:
                    t2 = E("vector", lambda e, p=p, sl=sl, xin=xin: e.tensor_tensor(
                        out=xin[:, sl], in0=stf[p][:], in1=pBC1[:, :],
                        op=ALU.subtract), deps=[t_bc1, t1], inc=sV)
                    st8["stf_free"][p] = t2
                    chunk_toks.append(t2)
                else:
                    gm, bt, dest = gamma_beta
                    t2 = E("vector", lambda e, p=p: e.tensor_tensor(
                        out=stf[p][:], in0=stf[p][:], in1=pBC1[:, :],
                        op=ALU.subtract), deps=[t_bc1, t1], inc=sV)
                    t3 = E("scalar", lambda e, p=p, dc=dc, sl=sl, gm=gm, bt=bt,
                           dest=dest: e.activation(
                        out=dest[:, sl], in_=stf[p][:], func=ACT.Identity,
                        bias=bt[:, dc:dc + 1], scale=gm[:, dc:dc + 1]),
                        deps=[t2, t_consts], inc=sS)
                    st8["stf_free"][p] = t3
                    chunk_toks.append(t3)
            st8["pbc_free"] = chunk_toks[-1]
            return chunk_toks

        def layer_norm(xin, gamma_beta, xdep):
            if len(xdep) == 1:
                xdep = xdep * NC
            state = {}
            for dc in range(NC):
                ln_stat_chunk(xin, dc, xdep[dc], state)
            return ln_finish(xin, gamma_beta, state)

        ln0_t0_toks = layer_norm(x_sb, (gam_sb, bet_sb, xn_sb), xdep=[t_x0])

        NBG = (PER_CORE + GATHER_BS - 1) // GATHER_BS
        gst = {
            "b": 0, "v": -1,
            "in_free": [None],
            "out_free": [None, None],
            "tin": None, "acc_tok": None, "ts15": None,
            "term_free": [None],
            "idxb_free": [None],
            "done": False,
        }

        def gather_pump(n):
            for _ in range(n):
                if gst["done"]:
                    return
                b = gst["b"]
                p3 = b % 2
                t0 = b * GATHER_BS
                nt = min(GATHER_BS, PER_CORE - t0)
                w = nt * 128
                v = gst["v"]
                if v == -1:
                    gst["tin"] = E("gpsimd", lambda e, t0=t0, nt=nt: e.dma_start(
                        out=idxi[0][:, 0:nt * 128],
                        in_=idxp[:, t0 * 128:(t0 + nt) * 128]),
                        deps=[gst["in_free"][0]], inc=sGI[0], amt=16)
                    gst["v"] = -2
                elif v == -2:
                    gst["tin"] = E("vector", lambda e, w=w: e.tensor_copy(
                        idxb[0][:, 0:w], idxi[0][:, 0:w]),
                        deps=[gst["tin"], gst["idxb_free"][0]], inc=sV)
                    gst["in_free"][0] = gst["tin"]
                    gst["v"] = 0
                elif v < 16:
                    if v == 0:
                        gst["acc_tok"] = E("vector", lambda e, p3=p3, w=w:
                            e.tensor_scalar(
                                out=gout[p3][:, 0:w], in0=idxb[0][:, 0:w],
                                scalar1=0.0, scalar2=lut_sb[:, 0:1],
                                op0=ALU.is_equal, op1=ALU.mult),
                            deps=[gst["tin"], t_consts, gst["out_free"][p3]], inc=sV)
                    else:
                        tt_ = E("vector", lambda e, p3=p3, v=v, w=w:
                            e.tensor_scalar(
                                out=term[0][:, 0:w], in0=idxb[0][:, 0:w],
                                scalar1=float(v), scalar2=lut_sb[:, v:v + 1],
                                op0=ALU.is_equal, op1=ALU.mult),
                            deps=[gst["tin"], gst["term_free"][0]], inc=sV)
                        gst["acc_tok"] = E("vector", lambda e, p3=p3, v=v, w=w:
                            e.tensor_tensor(
                                out=gout[p3][:, 0:w], in0=gout[p3][:, 0:w],
                                in1=term[0][:, 0:w], op=ALU.add),
                            deps=[tt_, gst["acc_tok"]], inc=sV)
                        gst["term_free"][0] = gst["acc_tok"]
                        if v == 15:
                            gst["idxb_free"][0] = gst["acc_tok"]
                    gst["v"] = v + 1
                else:
                    tst = E("scalar", lambda e, p3=p3, t0=t0, nt=nt, w=w: e.dma_start(
                        out=agin[:, t0 * 128:(t0 + nt) * 128],
                        in_=gout[p3][:, 0:w]),
                        deps=[gst["acc_tok"]], inc=sAG[p3], amt=16)
                    gst["out_free"][p3] = tst
                    gst["b"] = b + 1
                    gst["v"] = -1
                    if gst["b"] >= NBG:
                        gst["done"] = True


        if STOP == "p2":
            E("sync", lambda e: e.nop(),
              deps=[sm.cur for sm in sems.values()])
            finalize(nc, P)
            return nc

        if STOP.startswith("agdump") and False:
            base = int(STOP[6:] or 0)
            prev = [t_ag, t_ag]
            for bch in range(64):
                p = bch % 2
                t0 = (base + bch * 4) * 128
                tin = E("gpsimd", lambda e, p=p, t0=t0: e.dma_start(
                    out=gout[p][:, 0:512].rearrange("p (j c) -> p j c", j=4),
                    in_=agout[t0:t0 + 512, :].rearrange("(j p) c -> p j c", p=128)),
                    deps=[t_ag, prev[p]], inc=sGI[p], amt=16)
                tcv = E("vector", lambda e, p=p: e.tensor_copy(
                    stf[p][:], gout[p][:, 0:512]), deps=[tin], inc=sV)
                row = (bch * 512) // 2048 * 128
                col = (bch * 512) % 2048
                tst = E("gpsimd", lambda e, p=p, row=row, col=col: e.dma_start(
                    out=y_out[row:row + 128, col:col + 512], in_=stf[p][:]),
                    deps=[tcv], inc=sST, amt=16)
                prev[p] = tst
            E("sync", lambda e: e.nop(),
              deps=[sm.cur for sm in sems.values()])
            finalize(nc, P)
            return nc

        ws_free = [None, None, None, None]
        x_free = None
        xn_free = None
        q_free = None
        ao_free = None
        kv_free = [t_z1, t_z2]
        k_ready = [[t_z1] * NH, [t_z1] * NH]
        v_ready = [[t_z2] * NH, [t_z2] * NH]
        kg_ready = [None] * NH
        vg_ready = [None] * NH
        pX_free = [None, None]
        pSC_free = [None, None]
        pT_state = {"free": None}
        pAV_free = None
        attention_state = {}
        h_dma = [None] * NC
        x_load_tok = t_x0
        last_av_tok = None
        last_gup_mm = None

        xchain = None
        for tau in range(NTILE):
            buf = tau % 2
            if tau > 0:
                if xchain is None:
                    xchain = list(cast_in_free[:3])
                x_chunk = []
                for dc in range(NC):
                    p = dc % 3
                    tok = E("scalar", lambda e, tau=tau, dc=dc: e.dma_start(
                        out=x_sb[:, dc * TT:(dc + 1) * TT],
                        in_=x_t[tau][:, dc * TT:(dc + 1) * TT]),
                        deps=[prev_ln1_toks[dc], h_dma[dc // 2], xchain[p]],
                        inc=sCI[p], amt=16)
                    xchain[p] = tok
                    x_chunk.append(tok)
                x_load_tok = x_chunk[-1]
            else:
                x_chunk = [t_x0] * NC
            if tau == 0:
                ln0_toks = ln0_t0_toks
            else:
                E("vector", lambda e: e.engine_nop(), deps=[xn_free])
                ln0_toks = layer_norm(x_sb, (gam_sb, bet_sb, xn_sb),
                                      xdep=x_chunk)
            if STOP in ("ln0", "ln0only"):
                for dc in range(NC):
                    p = dc % 2
                    tcv = E("vector", lambda e, p=p, dc=dc: e.tensor_copy(
                        stf[p][:], xn_sb[:, dc * TT:(dc + 1) * TT]),
                        deps=[ln0_toks[dc], st8["stf_free"][p]], inc=sV)
                    td = E("gpsimd", lambda e, p=p, dc=dc, tau=tau: e.dma_start(
                        out=y_out[dc * 128:(dc + 1) * 128, tau * TT:(tau + 1) * TT],
                        in_=stf[p][:]), deps=[tcv], inc=sST, amt=16)
                    st8["stf_free"][p] = td
            if STOP == "ln0only" and tau == 0:
                E("sync", lambda e: e.nop(),
                  deps=[sm.cur for sm in sems.values()])
                finalize(nc, P)
                return nc

            def proj(m, evac, tile0_extra=None):
                toks = []
                for oc in range(16):
                    wp = oc % 4
                    if tau == 0:
                        ctoks, stoks = cast_direct(m, oc)
                        wdep = lambda hc, ctoks=ctoks: ctoks[hc // 4]
                    else:
                        base = m * 256 + oc * 16
                        r0 = base * 128
                        tw = E("sync", lambda e, wp=wp, r0=r0: e.dma_start(
                            out=ws[wp][:], in_=wscr[:, r0:r0 + 2048]),
                            deps=[ws_free[wp]] + ws_cast_deps[wp]
                                 + scr_deps(m, oc),
                            inc=sWS[wp], amt=16)
                        wdep = lambda hc, tw=tw: tw
                    pX = [pA, pB][oc % 2]
                    mm = None
                    for hc in range(16):
                        mm = E("tensor", lambda e, wp=wp, hc=hc, pX=pX: e.matmul(
                            pX[:, :], ws[wp][:, hc * 128:(hc + 1) * 128],
                            xn_sb[:, hc * TT:hc * TT + TT],
                            start=(hc == 0), stop=(hc == 15)),
                            deps=[wdep(hc), ln0_toks[hc],
                                  pX_free[oc % 2] if hc == 0 else None],
                            inc=sT if hc == 15 else None)
                    ws_free[wp] = mm
                    etok = evac(oc, pX, mm)
                    pX_free[oc % 2] = etok
                    toks.append(etok)
                    gather_pump(9)
                    if tile0_extra is not None:
                        mmg = None
                        for hc in range(16):
                            mmg = E("tensor", lambda e, wp=wp, hc=hc: e.matmul(
                                pSC0[:, 0:G], ws[wp][:, hc * 128:(hc + 1) * 128],
                                gfm_b[:, hc * G:(hc + 1) * G],
                                start=(hc == 0), stop=(hc == 15)),
                                deps=[t_gfm, pSC_free[0] if hc == 0 else None],
                                inc=sT if hc == 15 else None)
                        ws_free[wp] = mmg
                        pSC_free[0] = tile0_extra(oc, mmg)
                    if tau == 0:
                        ws_cast_deps[wp] = [ws_free[wp]] + stoks
                        cst["limit"] = max(cst["limit"],
                                           (m * 16 + oc + 5) * 4)
                        cast_pump(2)
                return toks

            def k_evac(oc, pX, mm):
                t = E("scalar", lambda e, oc=oc, pX=pX, buf=buf: e.copy(
                    KB(buf)[:, oc * TT:(oc + 1) * TT], pX[:, :]),
                    deps=[mm, kv_free[buf]], inc=sS)
                k_ready[buf][oc] = t
                return t

            def kg_extra(oc, mmg):
                t = E("vector", lambda e, oc=oc: e.tensor_copy(
                    kg_sb[:, oc * G:(oc + 1) * G], pSC0[:, 0:G]),
                    deps=[mmg], inc=sV)
                kg_ready[oc] = t
                return t

            proj(0, k_evac, kg_extra if tau == 0 else None)
            if STOP == "kproj" and tau == 0:
                E("sync", lambda e: e.nop(),
                  deps=[sm.cur for sm in sems.values()])
                finalize(nc, P)
                return nc

            def v_evac(oc, pX, mm):
                tv = E("vector", lambda e, pX=pX: e.tensor_copy(vtmp[:], pX[:, :]),
                       deps=[mm], inc=sV)
                if STOP == "vprojnotr":
                    last = None
                    for j in range(4):
                        last = E("vector", lambda e, j=j, oc=oc, buf=buf: e.tensor_copy(
                            VB(buf)[:, j * 2048 + oc * 128: j * 2048 + (oc + 1) * 128],
                            vtmp[:, j * 128:(j + 1) * 128]),
                            deps=[tv, kv_free[buf]], inc=sV)
                    pT_state["free"] = last
                    v_ready[buf][oc] = last
                    return last
                trs = []
                for j in range(4):
                    trs.append(E("tensor", lambda e, j=j: e.matmul(
                        pT[:, j * 128:(j + 1) * 128], vtmp[:, j * 128:(j + 1) * 128],
                        eye_b[:], start=True, stop=True),
                        deps=[tv, t_eyeb, pT_state["free"] if j == 0 else None],
                        inc=sT if j == 3 else None))
                last = None
                for j in range(4):
                    last = E("scalar", lambda e, j=j, oc=oc, buf=buf: e.copy(
                        VB(buf)[:, j * 2048 + oc * 128: j * 2048 + (oc + 1) * 128],
                        pT[:, j * 128:(j + 1) * 128]),
                        deps=[trs[3], kv_free[buf]], inc=sS)
                pT_state["free"] = last
                v_ready[buf][oc] = last
                return last

            def vg_extra(oc, mmg):
                tv = E("vector", lambda e: e.tensor_copy(
                    vgt[:, 0:G], pSC0[:, 0:G]), deps=[mmg], inc=sV)
                tr = E("tensor", lambda e: e.matmul(
                    pT[0:G, 0:128], vgt[:, 0:G], eye_b[:], start=True, stop=True),
                    deps=[tv, t_eyeb, pT_state["free"]], inc=sT)
                t = E("vector", lambda e, oc=oc: e.tensor_copy(
                    vg_sb[0:G, oc * 128:(oc + 1) * 128], pT[0:G, 0:128]),
                    deps=[tr], inc=sV)
                pT_state["free"] = t
                vg_ready[oc] = t
                return t

            proj(1, v_evac, vg_extra if (tau == 0 and STOP not in ('vprojnog', 'vprojnotr')) else None)
            if STOP in ("vproj", "vprojnog", "vprojnotr") and tau == 0:
                E("sync", lambda e: e.nop(),
                  deps=[sm.cur for sm in sems.values()])
                finalize(nc, P)
                return nc

            def q_evac(oc, pX, mm):
                return E("scalar", lambda e, oc=oc, pX=pX: e.mul(
                    q_sb[:, oc * TT:(oc + 1) * TT], pX[:, :], float(QSCALE)),
                    deps=[mm, q_free, t_cp1], inc=sS)

            q_toks = proj(2, q_evac, None)
            if STOP == "qproj" and tau == 0:
                E("sync", lambda e: e.nop(),
                  deps=[sm.cur for sm in sems.values()])
                finalize(nc, P)
                return nc

            psc3 = [pSC0, pSC1, pBC1]
            pTb3 = [pT, pA, pB]
            if tau == 0:
                att = {"sc": {}, "exp": {}, "ri": {},
                       "scl": {}, "tr": {}, "ate": {}, "av": {}, "ao": {}}
                attention_state["att"] = att
            att = attention_state["att"]
            git0 = tau * 64
            psc_init = [pSC_free[0], pSC_free[1], ln0_toks[-1]]
            pT_init = [pT_state["free"], pX_free[0], pX_free[1]]
            seen_bank = [False, False, False]
            seen_tbank = [False, False, False]
            seen_avb = [False, False]
            ao_toks = []

            def blocks_of(it):
                qc = it // NH
                a_blk = tau * 4 + qc
                return [a_blk - 2, a_blk - 1, a_blk]

            def sc_e(it):
                git = git0 + it
                b3 = git % 3
                psc = psc3[b3]
                qc, h = divmod(it, NH)
                qsl = q_sb[:, h * TT + qc * 128: h * TT + (qc + 1) * 128]
                extra = att["exp"].get(git - 3)
                if not seen_bank[b3]:
                    seen_bank[b3] = True
                    extra2 = psc_init[b3]
                else:
                    extra2 = None
                msel = (1 if (tau == 0 and qc == 0)
                        else (2 if (tau == 0 and qc == 1) else 0))
                E("tensor", lambda e, psc=psc, qsl=qsl, h=h: e.matmul(
                    psc[:, 0:G], qsl, kg_sb[:, h * G:(h + 1) * G],
                    start=True, stop=True),
                    deps=[q_toks[h], kg_ready[h], extra, extra2], inc=None)
                segs = []
                for j, bk in enumerate(blocks_of(it)):
                    bb = (bk // 4) % 2
                    if segs and segs[-1][0] == bb:
                        segs[-1][3] += 1
                    else:
                        segs.append([bb, j, bk % 4, 1])
                t_mm = None
                for si, (bb, j0, s0, cnt) in enumerate(segs):
                    if msel == 2:
                        mof = 384 if j0 == 0 else j0 * 128
                    else:
                        mof = msel * 384 + j0 * 128
                    E("tensor", lambda e, psc=psc, j0=j0, cnt=cnt, mof=mof:
                      e.matmul(
                        psc[:, G + j0 * 128:G + (j0 + cnt) * 128], eye_b[:],
                        msk_sb[:, mof:mof + cnt * 128],
                        start=True, stop=False),
                        deps=[t_msk, t_eyeb] if si == 0 else [], inc=None)
                    t_mm = E("tensor", lambda e, psc=psc, qsl=qsl, bb=bb,
                             s0=s0, j0=j0, cnt=cnt, h=h: e.matmul(
                        psc[:, G + j0 * 128:G + (j0 + cnt) * 128], qsl,
                        KB(bb)[:, h * TT + s0 * 128: h * TT + (s0 + cnt) * 128],
                        start=False, stop=True),
                        deps=[k_ready[bb][h]],
                        inc=sT if si == len(segs) - 1 else None)
                att["sc"][git] = t_mm

            def exp_e(it):
                git = git0 + it
                psc = psc3[git % 3]
                nb = git % 3
                att["exp"][git] = E("scalar", lambda e, psc=psc, nb=nb: e.activation(
                    out=aat[nb][:, 0:400], in_=psc[:, 0:400], func=ACT.Exp,
                    bias=0.0, scale=1.0, accum_out=sume[nb][:, :]),
                    deps=[att["sc"][git], att["tr"].get(git - 3),
                          att["ri"].get(git - 3)], inc=sS)

            def ri_e(it):
                git = git0 + it
                nb = git % 3
                att["ri"][git] = E("vector", lambda e, nb=nb: e.reciprocal(
                    rinv[nb][:], sume[nb][:]),
                    deps=[att["exp"][git], att["scl"].get(git - 3)], inc=sV)

            def scl_e(it):
                git = git0 + it
                nb = git % 3
                att["scl"][git] = E("vector", lambda e, nb=nb: e.tensor_scalar(
                    out=aat[nb][:, 0:400], in0=aat[nb][:, 0:400],
                    scalar1=rinv[nb][:, 0:1], scalar2=None, op0=ALU.mult),
                    deps=[att["ri"][git]], inc=sV)

            def tr_e(it):
                git = git0 + it
                b3 = git % 3
                pTb = pTb3[b3]
                if not seen_tbank[b3]:
                    seen_tbank[b3] = True
                    extra = pT_init[b3]
                else:
                    extra = None
                E("tensor", lambda e, pTb=pTb, b3=b3: e.matmul(
                    pTb[0:G, 0:128], aat[b3][:, 0:G], eye_b[:],
                    start=True, stop=True),
                    deps=[att["scl"][git], att["ate"].get(git - 3), extra,
                          t_eyeb], inc=None)
                tr_last = None
                for j in range(3):
                    tr_last = E("tensor", lambda e, j=j, pTb=pTb, b3=b3: e.matmul(
                        pTb[:, 128 * (j + 1):128 * (j + 2)],
                        aat[b3][:, G + j * 128:G + (j + 1) * 128], eye_b[:],
                        start=True, stop=True),
                        deps=[], inc=sT if j == 2 else None)
                att["tr"][git] = tr_last

            def ate_e(it):
                git = git0 + it
                b3 = git % 3
                att["ate"][git] = E("vector", lambda e, b3=b3: e.tensor_copy(
                    at_sb[b3][:], pTb3[b3][:, 0:512]),
                    deps=[att["tr"][git], att["av"].get(git - 3)], inc=sV)

            def av_e(it):
                git = git0 + it
                b3 = git % 3
                bk2 = git % 2
                pAVb = [pAV, pBC2][bk2]
                rv = ((git // 2) % 2) * 128
                qc, h = divmod(it, NH)
                if bk2 == 1 and not seen_avb[1]:
                    seen_avb[1] = True
                    extra = ln0_toks[-1]
                elif bk2 == 0 and not seen_avb[0]:
                    seen_avb[0] = True
                    extra = pAV_free
                else:
                    extra = None
                E("tensor", lambda e, h=h, b3=b3, rv=rv, pAVb=pAVb: e.matmul(
                    pAVb[:, rv:rv + 128],
                    vg_sb[0:G, h * 128:(h + 1) * 128],
                    at_sb[b3][0:G, 0:128], start=True, stop=False),
                    deps=[att["ate"][git], vg_ready[h],
                          att["ao"].get(git - 4), extra],
                    inc=None)
                t_av = None
                for j, bkk in enumerate(blocks_of(it)):
                    bb = (bkk // 4) % 2
                    slot = bkk % 4
                    t_av = E("tensor", lambda e, bb=bb, slot=slot, h=h, j=j,
                             b3=b3, rv=rv, pAVb=pAVb: e.matmul(
                        pAVb[:, rv:rv + 128],
                        VB(bb)[:, slot * 2048 + h * 128: slot * 2048 + (h + 1) * 128],
                        at_sb[b3][:, 128 * (j + 1):128 * (j + 2)],
                        start=False, stop=(j == 2)),
                        deps=[v_ready[bb][h]], inc=sT if j == 2 else None)
                att["av"][git] = t_av

            def ao_e(it):
                git = git0 + it
                bk2 = git % 2
                pAVb = [pAV, pBC2][bk2]
                rv = ((git // 2) % 2) * 128
                qc, h = divmod(it, NH)
                t_ao = E("scalar", lambda e, h=h, qc=qc, rv=rv, pAVb=pAVb: e.copy(
                    ao_sb[:, h * TT + qc * 128: h * TT + (qc + 1) * 128],
                    pAVb[:, rv:rv + 128]),
                    deps=[att["av"][git], ao_free], inc=sS)
                att["ao"][git] = t_ao
                ao_toks.append(t_ao)

            NIT = 4 * NH
            for p in range(NIT + 6):
                cast_pump(1)
                if p < NIT:
                    sc_e(p)
                if 0 <= p - 2 < NIT:
                    exp_e(p - 2)
                if 0 <= p - 3 < NIT:
                    ri_e(p - 3)
                    scl_e(p - 3)
                if 0 <= p - 4 < NIT:
                    tr_e(p - 4)
                if 0 <= p - 5 < NIT:
                    ate_e(p - 5)
                    av_e(p - 5)
                if 0 <= p - 6 < NIT:
                    ao_e(p - 6)

            glast = git0 + NIT - 1
            pSC_free[0] = att["exp"][[g for g in range(git0, git0 + NIT) if g % 3 == 0][-1]]
            pSC_free[1] = att["exp"][[g for g in range(git0, git0 + NIT) if g % 3 == 1][-1]]
            st8["pbc_free"] = att["exp"][[g for g in range(git0, git0 + NIT) if g % 3 == 2][-1]]
            pT_state["free"] = att["ate"][[g for g in range(git0, git0 + NIT) if g % 3 == 0][-1]]
            pX_free[0] = att["ate"][[g for g in range(git0, git0 + NIT) if g % 3 == 1][-1]]
            pX_free[1] = att["ate"][[g for g in range(git0, git0 + NIT) if g % 3 == 2][-1]]
            pAV_free = att["ao"][[g for g in range(git0, git0 + NIT) if g % 2 == 0][-1]]
            st8["pbc2_free"] = att["ao"][[g for g in range(git0, git0 + NIT) if g % 2 == 1][-1]]
            score_last_tok = att["sc"][glast]
            q_free = score_last_tok
            kv_free[0] = score_last_tok
            kv_free[1] = score_last_tok
            xn_free = q_toks[-1]
            last_av_tok = ao_toks[-1]
            ao_head = [ao_toks[3 * NH + h] for h in range(NH)]
            if STOP == "attn" and tau == 0:
                E("sync", lambda e: e.nop(),
                  deps=[sm.cur for sm in sems.values()])
                finalize(nc, P)
                return nc

            wo_resid = []
            ln1_state = {}
            wo_mm_last = None
            for oc in range(16):
                wp = oc % 4
                if tau == 0:
                    ctoks, stoks = cast_direct(3, oc)
                    wdep = lambda hc, ctoks=ctoks: ctoks[hc // 4]
                else:
                    base = 3 * 256 + oc * 16
                    r0 = base * 128
                    tw = E("sync", lambda e, wp=wp, r0=r0: e.dma_start(
                        out=ws[wp][:], in_=wscr[:, r0:r0 + 2048]),
                        deps=[ws_free[wp]] + ws_cast_deps[wp] + scr_deps(3, oc),
                        inc=sWS[wp], amt=16)
                    wdep = lambda hc, tw=tw: tw
                pX = [pA, pB][oc % 2]
                mm = None
                for hc in range(16):
                    mm = E("tensor", lambda e, wp=wp, hc=hc, pX=pX: e.matmul(
                        pX[:, :], ws[wp][:, hc * 128:(hc + 1) * 128],
                        ao_sb[:, hc * TT:hc * TT + TT],
                        start=(hc == 0), stop=(hc == 15)),
                        deps=[wdep(hc), ao_head[hc],
                              pX_free[oc % 2] if hc == 0 else None],
                        inc=sT if hc == 15 else None)
                ws_free[wp] = mm
                if tau == 0:
                    ws_cast_deps[wp] = [mm] + stoks
                    cst["limit"] = max(cst["limit"], (48 + oc + 5) * 4)
                wo_mm_last = mm
                tr = E("vector", lambda e, oc=oc, pX=pX: e.scalar_tensor_tensor(
                    out=x_sb[:, oc * TT:(oc + 1) * TT],
                    in0=x_sb[:, oc * TT:(oc + 1) * TT], scalar=2.0,
                    in1=pX[:, :], op0=ALU.mult, op1=ALU.add),
                    deps=[mm, x_chunk[oc]], inc=sV)
                pX_free[oc % 2] = tr
                wo_resid.append(tr)
                gather_pump(9)
            ao_free = wo_mm_last

            if STOP == "y1":
                for dc in range(NC):
                    h_dma[dc] = E("gpsimd", lambda e, dc=dc, tau=tau: e.dma_start(
                        out=y_out[dc * 128:(dc + 1) * 128, tau * TT:(tau + 1) * TT],
                        in_=x_sb[:, dc * TT:(dc + 1) * TT]),
                        deps=[wo_resid[dc]], inc=sST, amt=16)
                x_free = wo_resid[-1]
                continue

            for dc in range(NC):
                ln_stat_chunk(x_sb, dc, wo_resid[dc], ln1_state)
            ln1_toks = ln_finish(x_sb, None, ln1_state)
            prev_h = h_dma[NC // 2 - 1] if tau > 0 else None
            for j in range(NC // 2):
                prev_h = E("scalar", lambda e, j=j, tau=tau: e.dma_start(
                    out=hscr[tau][:, 2 * j * TT:(2 * j + 2) * TT],
                    in_=x_sb[:, 2 * j * TT:(2 * j + 2) * TT]),
                    deps=[ln1_toks[2 * j], ln1_toks[2 * j + 1], prev_h],
                    inc=sHS, amt=16)
                h_dma[j] = prev_h
            prev_ln1_toks = ln1_toks
            if STOP == "h":
                for dc in range(NC):
                    E("gpsimd", lambda e, dc=dc, tau=tau: e.dma_start(
                        out=y_out[dc * 128:(dc + 1) * 128, tau * TT:(tau + 1) * TT],
                        in_=x_sb[:, dc * TT:(dc + 1) * TT]),
                        deps=[ln1_toks[dc]], inc=sST, amt=16)
            x_free = ln1_toks[-1]

        cast_pump(10 ** 9)
        gather_pump(10 ** 9)
        t_ag = E("gpsimd", lambda e: e.collective_compute(
            "AllGather", ALU.bypass, replica_groups=[list(range(NCORES))],
            ins=[agin.opt()], outs=[agout.opt()]),
            deps=[sm.cur for sm in sAG], inc=sCC)

        if STOP in ("ln0", "y1", "h"):
            E("sync", lambda e: e.nop(),
              deps=[sm.cur for sm in sems.values()])
            finalize(nc, P)
            return nc

        def ag_pieces(tile0, ntiles):
            pieces = []
            t = tile0
            off = 0
            while ntiles > 0:
                cb = t // PER_CORE
                rel = t % PER_CORE
                n = min(ntiles, PER_CORE - rel)
                pieces.append((cb, rel, off, n))
                t += n
                off += n
                ntiles -= n
            return pieces

        def ag_load(dest, tile0, ntiles, deps, sem):
            tok = None
            for cb, rel, off, n in ag_pieces(tile0, ntiles):
                tok = E("sync", lambda e, cb=cb, rel=rel, off=off, n=n, dest=dest:
                        e.dma_start(
                    out=dest[:, off * 128:(off + n) * 128],
                    in_=agout[cb][:, rel * 128:(rel + n) * 128]),
                    deps=deps, inc=sem, amt=16)
            return tok

        h2_ready = [None] * NI
        wd_free = [q_free, ao_free]
        pG_free = [wo_resid[14], pSC_free[0]]
        pU_free = [wo_resid[15], pSC_free[1]]
        pD_free = [pT_state["free"], pAV_free]
        sg_free = [None, None]
        y_dma = [None] * NC
        hload_free = list(h_dma)
        xn_cons = xn_free
        h2_cons = last_av_tok
        q_ao_free = ao_free

        XN2 = AR[:, NI * TT:NI * TT + NC * TT]
        pre_hb = {}
        cin_free2 = list(cast_in_free[:3])
        xnb_cons = [None, last_av_tok]

        def prefetch_chunk(t, dc):
            p = dc % 3
            buf = xn_sb if t % 2 == 0 else XN2
            tpin = E("scalar", lambda e, p=p, t=t, dc=dc: e.dma_start(
                out=cin[p][:], in_=hscr[t][:, dc * TT:(dc + 1) * TT]),
                deps=[cin_free2[p]], inc=sCI[p], amt=16)
            tc = E("vector", lambda e, p=p, dc=dc, buf=buf: e.tensor_copy(
                buf[:, dc * TT:(dc + 1) * TT], cin[p][:]),
                deps=[tpin, xnb_cons[t % 2]], inc=sV)
            cin_free2[p] = tc
            pre_hb.setdefault(t, []).append(tc)

        for tau in range(NTILE):
            if tau == 0:
                hb_load = []
                hb_toks = []
                for dc in range(NC):
                    p = dc % 3
                    tok = E("scalar", lambda e, dc=dc, p=p: e.dma_start(
                        out=x_sb[:, dc * TT:(dc + 1) * TT],
                        in_=hscr[0][:, dc * TT:(dc + 1) * TT]),
                        deps=[prev_ln1_toks[dc], h_dma[dc // 2], xchain[p]],
                        inc=sCI[p], amt=16)
                    xchain[p] = tok
                    hb_load.append(tok)
                    hb_toks.append(E("vector", lambda e, dc=dc: e.tensor_copy(
                        xn_sb[:, dc * TT:(dc + 1) * TT],
                        x_sb[:, dc * TT:(dc + 1) * TT]),
                        deps=[tok, xn_cons], inc=sV))
            else:
                t_h = E("scalar", lambda e, tau=tau: e.dma_start(
                    out=x_sb[:], in_=hscr[tau]),
                    deps=[x_free] + list(y_dma), inc=sX, amt=16)
                hb_toks = pre_hb[tau]
            XN = xn_sb if tau % 2 == 0 else XN2

            last_mm_u = None
            for ic in range(NI):
                wpg = (2 * ic) % 4
                wpu = (2 * ic + 1) % 4
                twg = ag_load(ws[wpg], ic * 16, 16,
                              [t_ag, ws_free[wpg]], sWS[wpg])
                twu = ag_load(ws[wpu], N_G + ic * 16, 16,
                              [t_ag, ws_free[wpu]], sWS[wpu])
                pg = [pA, pSC0][ic % 2]
                pu = [pB, pSC1][ic % 2]
                mmg = None
                for hc in range(16):
                    mmg = E("tensor", lambda e, wpg=wpg, hc=hc, pg=pg, XN=XN:
                        e.matmul(
                        pg[:, :], ws[wpg][:, hc * 128:(hc + 1) * 128],
                        XN[:, hc * TT:hc * TT + TT],
                        start=(hc == 0), stop=(hc == 15)),
                        deps=[twg, hb_toks[hc],
                              pG_free[ic % 2] if hc == 0 else None],
                        inc=sT if hc == 15 else None)
                mmu = None
                for hc in range(16):
                    mmu = E("tensor", lambda e, wpu=wpu, hc=hc, pu=pu, XN=XN:
                        e.matmul(
                        pu[:, :], ws[wpu][:, hc * 128:(hc + 1) * 128],
                        XN[:, hc * TT:hc * TT + TT],
                        start=(hc == 0), stop=(hc == 15)),
                        deps=[twu, hb_toks[hc],
                              pU_free[ic % 2] if hc == 0 else None],
                        inc=sT if hc == 15 else None)
                ws_free[wpg] = mmg
                ws_free[wpu] = mmu
                last_mm_u = mmu
                sp = ic % 2
                t_si = E("scalar", lambda e, sp=sp, pg=pg: e.activation(
                    out=sg[sp][:], in_=pg[:, :], func=ACT.Silu),
                    deps=[mmg, sg_free[sp]], inc=sS)
                pG_free[ic % 2] = t_si
                t_h2 = E("vector", lambda e, sp=sp, ic=ic, pu=pu: e.tensor_tensor(
                    out=H2[:, ic * TT:(ic + 1) * TT], in0=sg[sp][:], in1=pu[:, :],
                    op=ALU.mult), deps=[t_si, mmu, h2_cons], inc=sV)
                pU_free[ic % 2] = t_h2
                sg_free[sp] = t_h2
                h2_ready[ic] = t_h2
            xnb_cons[tau % 2] = last_mm_u

            down_resid = []
            ln2_state = {}
            last_mm_d = None
            for oc in range(16):
                wd = [q_sb, ao_sb][oc % 2]
                twd = ag_load(wd, 2 * N_G + oc * NI, NI,
                              [t_ag, wd_free[oc % 2], q_ao_free], sWD[oc % 2])
                pD = [pT, pAV][oc % 2]
                mmd = None
                for ic in range(NI):
                    mmd = E("tensor", lambda e, wd=wd, ic=ic, pD=pD: e.matmul(
                        pD[:, :], wd[:, ic * 128:(ic + 1) * 128],
                        H2[:, ic * TT:(ic + 1) * TT],
                        start=(ic == 0), stop=(ic == NI - 1)),
                        deps=[twd, h2_ready[ic],
                              pD_free[oc % 2] if ic == 0 else None],
                        inc=sT if ic == NI - 1 else None)
                wd_free[oc % 2] = mmd
                last_mm_d = mmd
                tr = E("vector", lambda e, oc=oc, pD=pD: e.tensor_tensor(
                    out=x_sb[:, oc * TT:(oc + 1) * TT],
                    in0=x_sb[:, oc * TT:(oc + 1) * TT], in1=pD[:, :], op=ALU.add),
                    deps=[mmd, hb_load[oc] if tau == 0 else t_h], inc=sV)
                pD_free[oc % 2] = tr
                down_resid.append(tr)
                if tau + 1 < NTILE:
                    prefetch_chunk(tau + 1, oc)
            q_ao_free = None
            h2_cons = last_mm_d
            st8["pbc_free"] = down_resid[-1]

            for dc in range(NC):
                ln_stat_chunk(x_sb, dc, down_resid[dc], ln2_state)
            ln2_toks = ln_finish(x_sb, None, ln2_state)
            for dc in range(NC):
                y_dma[dc] = E("scalar", lambda e, dc=dc, tau=tau: e.dma_start(
                    out=y_out[dc * 128:(dc + 1) * 128, tau * TT:(tau + 1) * TT],
                    in_=x_sb[:, dc * TT:(dc + 1) * TT]),
                    deps=[ln2_toks[dc]], inc=sST, amt=16)
            x_free = ln2_toks[-1]

        E("sync", lambda e: e.nop(),
          deps=[sm.cur for sm in sems.values()])
        finalize(nc, P)
    return nc


def finalize(nc, P):
    with nc.Block() as block:
        block.sync(lambda e: P.run("sync", e))
        block.vector(lambda e: P.run("vector", e))
        block.scalar(lambda e: P.run("scalar", e))
        block.tensor(lambda e: P.run("tensor", e))
        block.gpsimd(lambda e: P.run("gpsimd", e))



def _tileize(w):
    w4 = w.reshape(16, 128, 16, 128)
    t = w4.transpose(0, 2, 3, 1).reshape(256, 128, 128)
    return np.ascontiguousarray(t.transpose(1, 0, 2).reshape(128, 256 * 128))


def _masks():
    q = np.arange(128)[:, None]
    k = np.arange(384)[None, :]
    base = np.where((k >= q + 1) & (k <= q + 256), 0.0, NEG).astype(np.float32)
    m0 = np.where(k >= 256, base, NEG).astype(np.float32)
    return np.ascontiguousarray(np.concatenate([base, m0], axis=1))


def _idx_pack(gate_idx, up_idx, down_idx):
    gT = np.ascontiguousarray(gate_idx.T)
    uT = np.ascontiguousarray(up_idx.T)
    dT = np.ascontiguousarray(down_idx.T)
    jobs = ffn_jobs()
    packs = []
    for c in range(NCORES):
        part = jobs[c * PER_CORE:(c + 1) * PER_CORE]
        tiles = np.empty((PER_CORE, 128, 128), np.int32)
        for j, (m, a, b) in enumerate(part):
            if m == "g":
                tiles[j] = gT[b * 128:(b + 1) * 128, a * 128:(a + 1) * 128]
            elif m == "u":
                tiles[j] = uT[b * 128:(b + 1) * 128, a * 128:(a + 1) * 128]
            else:
                tiles[j] = dT[b * 128:(b + 1) * 128, a * 128:(a + 1) * 128]
        packs.append(np.ascontiguousarray(
            tiles.transpose(1, 0, 2).reshape(128, PER_CORE * 128)))
    return packs


_prog_cache = {}


def _get_prog():
    key = STOP
    if key not in _prog_cache:
        _prog_cache[key] = build_program()
    return _prog_cache[key]


def kernel(L0_attn_x, L0_attn_global_tokens, L0_attn_ln_scale, L0_attn_ln_bias,
           L0_attn_Wq, L0_attn_Wk, L0_attn_Wv, L0_attn_Wo,
           L0_ffn_lut, L0_ffn_gate_idx, L0_ffn_up_idx, L0_ffn_down_idx):
    x = np.asarray(L0_attn_x, dtype=np.float32)
    g = np.asarray(L0_attn_global_tokens, dtype=np.float32)
    gam = np.asarray(L0_attn_ln_scale, dtype=np.float32)
    bet = np.asarray(L0_attn_ln_bias, dtype=np.float32)
    wq = np.asarray(L0_attn_Wq, dtype=np.float32)
    wk = np.asarray(L0_attn_Wk, dtype=np.float32)
    wv = np.asarray(L0_attn_Wv, dtype=np.float32)
    wo = np.asarray(L0_attn_Wo, dtype=np.float32)
    lut = np.asarray(L0_ffn_lut, dtype=np.float32)
    gidx = np.asarray(L0_ffn_gate_idx, dtype=np.int32)
    uidx = np.asarray(L0_ffn_up_idx, dtype=np.int32)
    didx = np.asarray(L0_ffn_down_idx, dtype=np.int32)

    wq_t, wk_t, wv_t, wo_t = map(_tileize, (wq, wk, wv, wo))
    g_t = np.ascontiguousarray(g[0].T)
    gam_t = np.ascontiguousarray(gam.reshape(16, 128).T)
    bet_t = np.ascontiguousarray(bet.reshape(16, 128).T)
    masks = _masks()
    eye = np.eye(128, dtype=np.float32)
    packs = _idx_pack(gidx, uidx, didx)

    nc = _get_prog()
    in_maps = []
    for c in range(NCORES):
        in_maps.append({
            "x_t": np.ascontiguousarray(
                x[c].T.reshape(NC, 128, NTILE, TT).transpose(2, 1, 0, 3)
                .reshape(NTILE, 128, NC * TT)),
            "wk_t": wk_t, "wv_t": wv_t, "wq_t": wq_t, "wo_t": wo_t,
            "idxp": packs[c],
            "lut_in": lut,
            "g_in": g_t,
            "gam_in": gam_t,
            "bet_in": bet_t,
            "eye_in": eye,
            "msk_in": masks,
        })
    res = run_bass_kernel_spmd(nc, in_maps, core_ids=list(range(NCORES)))
    out = np.stack([np.ascontiguousarray(r["y"].T) for r in res.results])
    return out


# version 32
# speedup vs baseline: 1.0163x; 1.0093x over previous
import numpy as np
import ml_dtypes
from contextlib import ExitStack

import concourse.bass as bass
import concourse.mybir as mybir
from concourse.bass_utils import run_bass_kernel_spmd

F32 = mybir.dt.float32
BF16 = mybir.dt.bfloat16
I32 = mybir.dt.int32
AX = mybir.AxisListType.X
ALU = mybir.AluOpType
ACT = mybir.ActivationFunctionType

D = 2048
T = 2048
NH = 16
G = 16
NC = 16
TT = 512
NTILE = T // TT
INTER = 5504
NI = INTER // 128
NCORES = 8
QSCALE = 1.0 / float(np.sqrt(128.0))
EPS = 1e-5
NEG = -30000.0

N_G = NI * 16
N_TILES_FFN = 3 * N_G
PER_CORE = N_TILES_FFN // NCORES
GATHER_BS = 4
CAST_BS = 4
N_ATTN_TILES = 4 * 256

STOP = "full"

ENGINES = ("sync", "vector", "scalar", "tensor", "gpsimd")


class Sem:
    __slots__ = ("name", "h", "val")

    def __init__(self, name):
        self.name = name
        self.h = None
        self.val = 0

    def tok(self, k=1):
        self.val += k
        return (self, self.val)

    @property
    def cur(self):
        return (self, self.val)


class Plan:
    def __init__(self):
        self.ops = {e: [] for e in ENGINES}
        self.waited = {e: {} for e in ENGINES}

    def emit(self, eng, fn, deps=(), inc=None, amt=1):
        need = {}
        for d in deps:
            if d is None:
                continue
            sem, val = d
            if val > need.get(sem, 0):
                need[sem] = val
        ws = []
        for sem, val in need.items():
            if val > self.waited[eng].get(sem, 0):
                ws.append((sem, val))
                self.waited[eng][sem] = val
        tok = inc.tok(amt) if inc is not None else None
        self.ops[eng].append((ws, fn, inc, amt))
        return tok

    def run(self, eng_name, e):
        for ws, fn, inc, amt in self.ops[eng_name]:
            for sem, val in ws:
                e.wait_ge(sem.h, val)
            inst = fn(e)
            if inc is not None:
                inst.then_inc(inc.h, amt)


def ffn_jobs():
    jobs = []
    for ic in range(NI):
        for hc in range(16):
            jobs.append(("g", ic, hc))
    for ic in range(NI):
        for hc in range(16):
            jobs.append(("u", ic, hc))
    for oc in range(16):
        for ic in range(NI):
            jobs.append(("d", oc, ic))
    return jobs


def build_program():
    nc = bass.Bass("TRN2", target_bir_lowering=False, debug=False)

    x_t = nc.dram_tensor("x_t", [NTILE, 128, NC * TT], F32, kind="ExternalInput").ap()
    winp = [
        nc.dram_tensor(n, [128, 256 * 128], F32, kind="ExternalInput").ap()
        for n in ("wk_t", "wv_t", "wq_t", "wo_t")
    ]
    idxp = nc.dram_tensor("idxp", [128, PER_CORE * 128], I32, kind="ExternalInput").ap()
    lut_in = nc.dram_tensor("lut_in", [16], F32, kind="ExternalInput").ap()
    g_in = nc.dram_tensor("g_in", [D, G], F32, kind="ExternalInput").ap()
    gam_in = nc.dram_tensor("gam_in", [128, 16], F32, kind="ExternalInput").ap()
    bet_in = nc.dram_tensor("bet_in", [128, 16], F32, kind="ExternalInput").ap()
    eye_in = nc.dram_tensor("eye_in", [128, 128], F32, kind="ExternalInput").ap()
    msk_in = nc.dram_tensor("msk_in", [128, 2 * 384], F32, kind="ExternalInput").ap()

    y_out = nc.dram_tensor("y", [D, T], F32, kind="ExternalOutput").ap()

    wscr = nc.dram_tensor("wscr", [128, N_ATTN_TILES * 128], BF16).ap()
    agin = nc.dram_tensor("agin", [128, PER_CORE * 128], BF16).ap()
    agout = nc.dram_tensor("agout", [NCORES, 128, PER_CORE * 128], BF16).ap()
    hscr = nc.dram_tensor("hscr", [NTILE, 128, NC * TT], F32).ap()

    with ExitStack() as ctx:
        sb = lambda n, s, d=F32: ctx.enter_context(nc.sbuf_tensor(n, s, d))
        psum = lambda n, s, d=F32: ctx.enter_context(nc.psum_tensor(n, s, d))

        x_sb = sb("x_sb", [128, NC * TT])
        xn_sb = sb("xn_sb", [128, NC * TT], BF16)
        q_sb = sb("q_sb", [128, NH * TT], BF16)
        ao_sb = sb("ao_sb", [128, NH * TT], BF16)
        AR = sb("AR", [128, 4 * 8192], BF16)
        ws = [sb(f"ws{i}", [128, 2048], BF16) for i in range(4)]
        vtmp = sb("vtmp", [128, 512], BF16)
        vgt = sb("vgt", [128, 16], BF16)
        at_sb = [sb(f"at_sb{i}", [128, 512], BF16) for i in range(3)]
        aat = [sb(f"aat{i}", [128, 512], BF16) for i in range(3)]
        msk_sb = sb("msk_sb", [128, 2 * 384], BF16)
        eye_b = sb("eye_b", [128, 128], BF16)
        gfm_b = sb("gfm_b", [128, NC * G], BF16)
        kg_sb = sb("kg_sb", [128, NH * G], BF16)
        vg_sb = sb("vg_sb", [16, D], BF16)
        gam_sb = sb("gam_sb", [128, 16])
        bet_sb = sb("bet_sb", [128, 16])
        lut_sb = sb("lut_sb", [128, 16])
        ones_b = sb("ones_b", [128, 1], BF16)
        cpool = sb("cpool", [128, 2])
        stf = [sb(f"stf{i}", [128, 512]) for i in range(2)]
        stb = [sb(f"stb{i}", [128, 512], BF16) for i in range(2)]
        sq = [sb(f"sq{i}", [128, 512], BF16) for i in range(2)]
        sg = [sb(f"sg{i}", [128, 512], BF16) for i in range(2)]
        idxi = [sb(f"idxi{i}", [128, 512], I32) for i in range(2)]
        idxb = [sb(f"idxb{i}", [128, 512], BF16) for i in range(2)]
        term = [sb("term0", [128, 512], BF16)]
        gout = [sb(f"gout{i}", [128, 512], BF16) for i in range(2)]
        cin = [sb(f"cin{i}", [128, 512]) for i in range(3)]
        smalls = sb("smalls", [1, 1024])
        bcst_t = sb("bcst", [1, 1152])
        bcst = bcst_t[:, 0:1024]
        ones_f = bcst_t[:, 1024:1152]
        negmax = [sb(f"negmax{i}", [128, 1]) for i in range(3)]
        sume = [sb(f"sume{i}", [128, 1]) for i in range(3)]
        rinv = [sb(f"rinv{i}", [128, 1]) for i in range(3)]

        nc.const_aps.register(F32, -0.5, cpool[:, 0:1])
        nc.const_aps.register(F32, float(QSCALE), cpool[:, 1:2])

        KB = lambda b: AR[:, b * 8192 : (b + 1) * 8192]
        VB = lambda b: AR[:, (2 + b) * 8192 : (3 + b) * 8192]
        H2 = AR

        pA = psum("pA", [128, 512])
        pB = psum("pB", [128, 512])
        pBC1 = psum("pBC1", [128, 512])
        pBC2 = psum("pBC2", [128, 512])
        pSC0 = psum("pSC0", [128, 512])
        pSC1 = psum("pSC1", [128, 512])
        pT = psum("pT", [128, 512])
        pAV = psum("pAV", [128, 512])

        names = (["sK", "sX", "sHS", "sST", "sV", "sS", "sT", "sCC"]
                 + [f"sCI{i}" for i in range(4)] + [f"sGI{i}" for i in range(4)]
                 + [f"sCO{i}" for i in range(4)] + [f"sAG{i}" for i in range(4)]
                 + ["sWD0", "sWD1"]
                 + [f"sWS{i}" for i in range(4)])
        sems = {n: Sem(n) for n in names}
        for sm in sems.values():
            sm.h = ctx.enter_context(nc.semaphore(sm.name))
        sK = sems["sK"]; sX = sems["sX"]; sHS = sems["sHS"]; sST = sems["sST"]
        sV = sems["sV"]; sS = sems["sS"]; sT = sems["sT"]; sCC = sems["sCC"]
        sCI = [sems[f"sCI{i}"] for i in range(4)]
        sGI = [sems[f"sGI{i}"] for i in range(4)]
        sCO = [sems[f"sCO{i}"] for i in range(4)]
        sAG = [sems[f"sAG{i}"] for i in range(4)]
        sWD = [sems["sWD0"], sems["sWD1"]]
        sWS = [sems[f"sWS{i}"] for i in range(4)]

        P = Plan()
        E = P.emit

        E("sync", lambda e: e.dma_start(
            out=sq[0][:].bitcast(F32)[:, 0:NC * G].rearrange("p (hc k) -> p hc k", hc=NC),
            in_=g_in.rearrange("(hc p) k -> p hc k", p=128)), inc=sK, amt=16)
        E("sync", lambda e: e.dma_start(out=gam_sb[:], in_=gam_in), inc=sK, amt=16)
        E("sync", lambda e: e.dma_start(out=bet_sb[:], in_=bet_in), inc=sK, amt=16)
        E("sync", lambda e: e.dma_start(
            out=lut_sb[:, :], in_=lut_in[None, :].partition_broadcast(128)), inc=sK, amt=16)
        t_consts = sK.cur
        t_gam = t_bet = t_lut = t_consts
        t_mskf0 = E("sync", lambda e: e.dma_start(out=stf[0][:, 0:384], in_=msk_in[:, 0:384]), inc=sCI[3], amt=16)
        t_mskf1 = E("sync", lambda e: e.dma_start(out=stf[1][:, 0:384], in_=msk_in[:, 384:768]), inc=sGI[3], amt=16)
        t_mc0 = E("vector", lambda e: e.tensor_copy(msk_sb[:, 0:384], stf[0][:, 0:384]),
                  deps=[t_mskf0], inc=sV)
        t_mc1 = E("vector", lambda e: e.tensor_copy(msk_sb[:, 384:768], stf[1][:, 0:384]),
                  deps=[t_mskf1], inc=sV)
        t_eyef = E("sync", lambda e: e.dma_start(out=stf[1][:, 0:128], in_=eye_in),
                   deps=[t_mc1], inc=sGI[3], amt=16)
        t_eyeb = E("vector", lambda e: e.tensor_copy(eye_b[:], stf[1][:, 0:128]),
                   deps=[t_eyef], inc=sV)
        t_gfm = E("vector", lambda e: e.tensor_copy(
            gfm_b[:], sq[0][:].bitcast(F32)[:, 0:NC * G]), deps=[t_consts], inc=sV)
        t_msk = t_mc1
        t_x0 = E("scalar", lambda e: e.dma_start(
            out=x_sb[:], in_=x_t[0]), inc=sX, amt=16)
        t_ones = E("vector", lambda e: e.memset(ones_b[:], 1.0), inc=sV)
        t_onesf = E("vector", lambda e: e.memset(ones_f[:], 1.0), inc=sV)
        t_cp0 = E("vector", lambda e: e.memset(cpool[:, 0:1], -0.5), inc=sV)
        t_cp1 = E("vector", lambda e: e.memset(cpool[:, 1:2], float(QSCALE)), inc=sV)
        t_z1 = E("vector", lambda e: e.memset(KB(1)[:, :], 0.0), inc=sV)
        t_z2 = E("vector", lambda e: e.memset(VB(1)[:, :], 0.0), inc=sV)

        NCB = 256 // CAST_BS
        scr_ready = {}
        cast_in_free = [None, None, None, None, None]
        cast_store_prev = [None, None]
        ws_cast_deps = [[], [], [], []]
        cst = {"i": 0, "done": False, "limit": 16, "tok": {}, "st": {}}

        def cast_pump(n):
            for _ in range(n):
                if cst["done"] or cst["i"] >= cst["limit"]:
                    return
                i = cst["i"]
                m, b = divmod(i, NCB)
                p = i % 5
                land = cin[p] if p < 3 else stf[p - 3]
                psem = sCI[p] if p < 4 else sGI[2]
                s = i % 2
                wp = (i // 4) % 4
                sub = i % 4
                t0 = b * CAST_BS
                src = winp[m][:, t0 * 128:t0 * 128 + 512]
                first_stf = p >= 3 and cast_in_free[p] is None
                tin = E("sync", lambda e, src=src, land=land: e.dma_start(
                    out=land[:], in_=src),
                    deps=[cast_in_free[p]]
                         + ([st8["stf_free"][p - 3]] if first_stf else []),
                    inc=psem, amt=16)
                tc = E("vector", lambda e, land=land, wp=wp, sub=sub:
                       e.tensor_copy(
                    ws[wp][:, sub * 512:(sub + 1) * 512], land[:]),
                    deps=[tin] + ws_cast_deps[wp], inc=sV)
                cast_in_free[p] = tc
                r0 = (m * 256 + t0) * 128
                tst = E("scalar", lambda e, wp=wp, sub=sub, r0=r0: e.dma_start(
                    out=wscr[:, r0:r0 + 512],
                    in_=ws[wp][:, sub * 512:(sub + 1) * 512]),
                    deps=[tc, cast_store_prev[s]], inc=sCO[s], amt=16)
                cast_store_prev[s] = tst
                scr_ready[(m, b)] = tst
                cst["tok"][i] = tc
                cst["st"][i] = tst
                cst["i"] = i + 1
                if cst["i"] >= 4 * NCB:
                    cst["done"] = True
                    st8["stf_free"][0] = cast_in_free[3]
                    st8["stf_free"][1] = cast_in_free[4]

        def cast_direct(m, oc):
            need = m * 64 + oc * 4
            while cst["i"] < need + 4:
                assert cst["i"] < cst["limit"]
                cast_pump(1)
            return [cst["tok"][need + k] for k in range(4)],                    [cst["st"][need + k] for k in range(4)]

        def scr_deps(m, oc):
            bb = oc * 4
            assert cst["done"]
            return [scr_ready[(m, bb + k)] for k in range(4)]

        if STOP == "p1":
            cst["limit"] = 10 ** 9
            cast_pump(10 ** 9)
            E("sync", lambda e: e.nop(),
              deps=[sm.cur for sm in sems.values()])
            finalize(nc, P)
            return nc

        st8 = {
            "stb_free": [None, None],
            "stf_free": [t_mc0, t_eyeb],
            "sq_free": [t_gfm, None],
            "pbc_free": None,
            "pbc2_free": None,
        }

        def ln_stat_chunk(xin, dc, dep, state):
            sl = slice(dc * TT, (dc + 1) * TT)
            p = dc % 2
            tcast = E("vector", lambda e, p=p, sl=sl, xin=xin: e.tensor_copy(
                stb[p][:], xin[:, sl]),
                deps=[st8["stb_free"][p], dep], inc=sV)
            state["sum_last"] = E("tensor", lambda e, p=p, dc=dc: e.matmul(
                pBC1[0:1, :], ones_b[:], stb[p][:],
                start=(dc == 0), stop=(dc == NC - 1)),
                deps=[tcast, t_ones,
                      st8["pbc_free"] if dc == 0 else None], inc=sT)
            st8["stb_free"][p] = state["sum_last"]
            tsq = E("scalar", lambda e, p=p, sl=sl, xin=xin: e.square(
                sq[p][:], xin[:, sl]),
                deps=[st8["sq_free"][p], dep], inc=sS)
            state["sq_last"] = E("tensor", lambda e, p=p, dc=dc: e.matmul(
                pBC2[0:1, :], ones_b[:], sq[p][:],
                start=(dc == 0), stop=(dc == NC - 1)),
                deps=[tsq, t_ones,
                      st8["pbc_free"] if dc == 0 else None,
                      st8["pbc2_free"] if dc == 0 else None], inc=sT)
            st8["sq_free"][p] = state["sq_last"]

        def ln_finish(xin, gamma_beta, state):
            sum_last = state["sum_last"]
            sq_last = state["sq_last"]
            t_mu = E("vector", lambda e: e.tensor_scalar(
                out=smalls[:, 0:512], in0=pBC1[0:1, :], scalar1=1.0 / D,
                scalar2=None, op0=ALU.mult), deps=[sum_last], inc=sV)
            t_e2 = E("vector", lambda e: e.tensor_scalar(
                out=smalls[:, 512:1024], in0=pBC2[0:1, :], scalar1=1.0 / D,
                scalar2=float(EPS), op0=ALU.mult, op1=ALU.add),
                deps=[sq_last], inc=sV)
            t_msq = E("vector", lambda e: e.tensor_tensor(
                out=bcst[:, 0:512], in0=smalls[:, 0:512],
                in1=smalls[:, 0:512], op=ALU.mult), deps=[t_mu], inc=sV)
            t_var = E("vector", lambda e: e.tensor_tensor(
                out=smalls[:, 512:1024], in0=smalls[:, 512:1024],
                in1=bcst[:, 0:512], op=ALU.subtract),
                deps=[t_e2, t_msq], inc=sV)
            t_ln = E("scalar", lambda e: e.activation(
                out=smalls[:, 512:1024], in_=smalls[:, 512:1024],
                func=ACT.Ln, bias=0.0, scale=1.0), deps=[t_var], inc=sS)
            t_rstd = E("scalar", lambda e: e.activation(
                out=bcst[:, 512:1024], in_=smalls[:, 512:1024],
                func=ACT.Exp, bias=0.0, scale=-0.5),
                deps=[t_ln, t_cp0], inc=sS)
            t_mrs = E("vector", lambda e: e.tensor_tensor(
                out=bcst[:, 0:512], in0=smalls[:, 0:512],
                in1=bcst[:, 512:1024], op=ALU.mult), deps=[t_rstd], inc=sV)
            t_bc1 = E("tensor", lambda e: e.matmul(
                pBC1[:, :], ones_f[:], bcst[:, 0:512], start=True, stop=True),
                deps=[t_mrs, t_onesf, t_mu], inc=sT)
            t_bc2 = E("tensor", lambda e: e.matmul(
                pBC2[:, :], ones_f[:], bcst[:, 512:1024], start=True, stop=True),
                deps=[t_rstd, t_onesf, t_e2, st8["pbc2_free"]], inc=sT)
            chunk_toks = []
            for dc in range(NC):
                sl = slice(dc * TT, (dc + 1) * TT)
                p = dc % 2
                t1 = E("vector", lambda e, p=p, sl=sl, xin=xin: e.tensor_tensor(
                    out=stf[p][:], in0=xin[:, sl], in1=pBC2[:, :], op=ALU.mult),
                    deps=[t_bc2, st8["stf_free"][p]], inc=sV)
                if gamma_beta is None:
                    t2 = E("vector", lambda e, p=p, sl=sl, xin=xin: e.tensor_tensor(
                        out=xin[:, sl], in0=stf[p][:], in1=pBC1[:, :],
                        op=ALU.subtract), deps=[t_bc1, t1], inc=sV)
                    st8["stf_free"][p] = t2
                    chunk_toks.append(t2)
                else:
                    gm, bt, dest = gamma_beta
                    t2 = E("vector", lambda e, p=p: e.tensor_tensor(
                        out=stf[p][:], in0=stf[p][:], in1=pBC1[:, :],
                        op=ALU.subtract), deps=[t_bc1, t1], inc=sV)
                    t3 = E("scalar", lambda e, p=p, dc=dc, sl=sl, gm=gm, bt=bt,
                           dest=dest: e.activation(
                        out=dest[:, sl], in_=stf[p][:], func=ACT.Identity,
                        bias=bt[:, dc:dc + 1], scale=gm[:, dc:dc + 1]),
                        deps=[t2, t_consts], inc=sS)
                    st8["stf_free"][p] = t3
                    chunk_toks.append(t3)
            st8["pbc_free"] = chunk_toks[-1]
            return chunk_toks

        def layer_norm(xin, gamma_beta, xdep):
            if len(xdep) == 1:
                xdep = xdep * NC
            state = {}
            for dc in range(NC):
                ln_stat_chunk(xin, dc, xdep[dc], state)
            return ln_finish(xin, gamma_beta, state)

        ln0_t0_toks = layer_norm(x_sb, (gam_sb, bet_sb, xn_sb), xdep=[t_x0])

        NBG = (PER_CORE + GATHER_BS - 1) // GATHER_BS
        gst = {
            "b": 0, "v": -1,
            "in_free": [None, None],
            "out_free": [None, None],
            "tin": None, "acc_tok": None, "ts15": None,
            "term_free": [None],
            "idxb_free": [None, None],
            "done": False,
        }

        def gather_pump(n):
            for _ in range(n):
                if gst["done"]:
                    return
                b = gst["b"]
                p3 = b % 2
                t0 = b * GATHER_BS
                nt = min(GATHER_BS, PER_CORE - t0)
                w = nt * 128
                v = gst["v"]
                if v == -1:
                    pi = b % 2
                    gst["tin"] = E("gpsimd", lambda e, pi=pi, t0=t0, nt=nt: e.dma_start(
                        out=idxi[pi][:, 0:nt * 128],
                        in_=idxp[:, t0 * 128:(t0 + nt) * 128]),
                        deps=[gst["in_free"][pi]], inc=sGI[pi], amt=16)
                    gst["v"] = -2
                elif v == -2:
                    pi = b % 2
                    p2 = b % 2
                    gst["tin"] = E("vector", lambda e, pi=pi, p2=p2, w=w: e.tensor_copy(
                        idxb[p2][:, 0:w], idxi[pi][:, 0:w]),
                        deps=[gst["tin"], gst["idxb_free"][p2]], inc=sV)
                    gst["in_free"][pi] = gst["tin"]
                    gst["v"] = 0
                elif v < 16:
                    if v == 0:
                        gst["acc_tok"] = E("vector", lambda e, p3=p3, p2=b % 2, w=w:
                            e.tensor_scalar(
                                out=gout[p3][:, 0:w], in0=idxb[p2][:, 0:w],
                                scalar1=0.0, scalar2=lut_sb[:, 0:1],
                                op0=ALU.is_equal, op1=ALU.mult),
                            deps=[gst["tin"], t_consts, gst["out_free"][p3]], inc=sV)
                    else:
                        tt_ = E("vector", lambda e, p3=p3, p2=b % 2, v=v, w=w:
                            e.tensor_scalar(
                                out=term[0][:, 0:w], in0=idxb[p2][:, 0:w],
                                scalar1=float(v), scalar2=lut_sb[:, v:v + 1],
                                op0=ALU.is_equal, op1=ALU.mult),
                            deps=[gst["tin"], gst["term_free"][0]], inc=sV)
                        gst["acc_tok"] = E("vector", lambda e, p3=p3, v=v, w=w:
                            e.tensor_tensor(
                                out=gout[p3][:, 0:w], in0=gout[p3][:, 0:w],
                                in1=term[0][:, 0:w], op=ALU.add),
                            deps=[tt_, gst["acc_tok"]], inc=sV)
                        gst["term_free"][0] = gst["acc_tok"]
                        if v == 15:
                            gst["idxb_free"][b % 2] = gst["acc_tok"]
                    gst["v"] = v + 1
                else:
                    tst = E("scalar", lambda e, p3=p3, t0=t0, nt=nt, w=w: e.dma_start(
                        out=agin[:, t0 * 128:(t0 + nt) * 128],
                        in_=gout[p3][:, 0:w]),
                        deps=[gst["acc_tok"]], inc=sAG[p3], amt=16)
                    gst["out_free"][p3] = tst
                    gst["b"] = b + 1
                    gst["v"] = -1
                    if gst["b"] >= NBG:
                        gst["done"] = True


        if STOP == "p2":
            E("sync", lambda e: e.nop(),
              deps=[sm.cur for sm in sems.values()])
            finalize(nc, P)
            return nc

        if STOP.startswith("agdump") and False:
            base = int(STOP[6:] or 0)
            prev = [t_ag, t_ag]
            for bch in range(64):
                p = bch % 2
                t0 = (base + bch * 4) * 128
                tin = E("gpsimd", lambda e, p=p, t0=t0: e.dma_start(
                    out=gout[p][:, 0:512].rearrange("p (j c) -> p j c", j=4),
                    in_=agout[t0:t0 + 512, :].rearrange("(j p) c -> p j c", p=128)),
                    deps=[t_ag, prev[p]], inc=sGI[p], amt=16)
                tcv = E("vector", lambda e, p=p: e.tensor_copy(
                    stf[p][:], gout[p][:, 0:512]), deps=[tin], inc=sV)
                row = (bch * 512) // 2048 * 128
                col = (bch * 512) % 2048
                tst = E("gpsimd", lambda e, p=p, row=row, col=col: e.dma_start(
                    out=y_out[row:row + 128, col:col + 512], in_=stf[p][:]),
                    deps=[tcv], inc=sST, amt=16)
                prev[p] = tst
            E("sync", lambda e: e.nop(),
              deps=[sm.cur for sm in sems.values()])
            finalize(nc, P)
            return nc

        ws_free = [None, None, None, None]
        x_free = None
        xn_free = None
        q_free = None
        ao_free = None
        kv_free = [t_z1, t_z2]
        k_ready = [[t_z1] * NH, [t_z1] * NH]
        v_ready = [[t_z2] * NH, [t_z2] * NH]
        kg_ready = [None] * NH
        vg_ready = [None] * NH
        pX_free = [None, None]
        pSC_free = [None, None]
        pT_state = {"free": None}
        pAV_free = None
        attention_state = {}
        h_dma = [None] * NC
        x_load_tok = t_x0
        last_av_tok = None
        last_gup_mm = None

        xchain = None
        for tau in range(NTILE):
            buf = tau % 2
            if tau > 0:
                if xchain is None:
                    xchain = list(cast_in_free[:3])
                x_chunk = []
                for dc in range(NC):
                    p = dc % 3
                    tok = E("scalar", lambda e, tau=tau, dc=dc: e.dma_start(
                        out=x_sb[:, dc * TT:(dc + 1) * TT],
                        in_=x_t[tau][:, dc * TT:(dc + 1) * TT]),
                        deps=[prev_ln1_toks[dc], h_dma[dc // 2], xchain[p]],
                        inc=sCI[p], amt=16)
                    xchain[p] = tok
                    x_chunk.append(tok)
                x_load_tok = x_chunk[-1]
            else:
                x_chunk = [t_x0] * NC
            if tau == 0:
                ln0_toks = ln0_t0_toks
            else:
                E("vector", lambda e: e.engine_nop(), deps=[xn_free])
                ln0_toks = layer_norm(x_sb, (gam_sb, bet_sb, xn_sb),
                                      xdep=x_chunk)
            if STOP in ("ln0", "ln0only"):
                for dc in range(NC):
                    p = dc % 2
                    tcv = E("vector", lambda e, p=p, dc=dc: e.tensor_copy(
                        stf[p][:], xn_sb[:, dc * TT:(dc + 1) * TT]),
                        deps=[ln0_toks[dc], st8["stf_free"][p]], inc=sV)
                    td = E("gpsimd", lambda e, p=p, dc=dc, tau=tau: e.dma_start(
                        out=y_out[dc * 128:(dc + 1) * 128, tau * TT:(tau + 1) * TT],
                        in_=stf[p][:]), deps=[tcv], inc=sST, amt=16)
                    st8["stf_free"][p] = td
            if STOP == "ln0only" and tau == 0:
                E("sync", lambda e: e.nop(),
                  deps=[sm.cur for sm in sems.values()])
                finalize(nc, P)
                return nc

            def proj(m, evac, tile0_extra=None):
                toks = []
                for oc in range(16):
                    wp = oc % 4
                    if tau == 0:
                        ctoks, stoks = cast_direct(m, oc)
                        wdep = lambda hc, ctoks=ctoks: ctoks[hc // 4]
                    else:
                        base = m * 256 + oc * 16
                        r0 = base * 128
                        tw = E("sync", lambda e, wp=wp, r0=r0: e.dma_start(
                            out=ws[wp][:], in_=wscr[:, r0:r0 + 2048]),
                            deps=[ws_free[wp]] + ws_cast_deps[wp]
                                 + scr_deps(m, oc),
                            inc=sWS[wp], amt=16)
                        wdep = lambda hc, tw=tw: tw
                    pX = [pA, pB][oc % 2]
                    mm = None
                    for hc in range(16):
                        mm = E("tensor", lambda e, wp=wp, hc=hc, pX=pX: e.matmul(
                            pX[:, :], ws[wp][:, hc * 128:(hc + 1) * 128],
                            xn_sb[:, hc * TT:hc * TT + TT],
                            start=(hc == 0), stop=(hc == 15)),
                            deps=[wdep(hc), ln0_toks[hc],
                                  pX_free[oc % 2] if hc == 0 else None],
                            inc=sT if hc == 15 else None)
                    ws_free[wp] = mm
                    etok = evac(oc, pX, mm)
                    pX_free[oc % 2] = etok
                    toks.append(etok)
                    gather_pump(7)
                    if tile0_extra is not None:
                        mmg = None
                        for hc in range(16):
                            mmg = E("tensor", lambda e, wp=wp, hc=hc: e.matmul(
                                pSC0[:, 0:G], ws[wp][:, hc * 128:(hc + 1) * 128],
                                gfm_b[:, hc * G:(hc + 1) * G],
                                start=(hc == 0), stop=(hc == 15)),
                                deps=[t_gfm, pSC_free[0] if hc == 0 else None],
                                inc=sT if hc == 15 else None)
                        ws_free[wp] = mmg
                        pSC_free[0] = tile0_extra(oc, mmg)
                    if tau == 0:
                        ws_cast_deps[wp] = [ws_free[wp]] + stoks
                        cst["limit"] = max(cst["limit"],
                                           (m * 16 + oc + 5) * 4)
                        cast_pump(2)
                return toks

            def k_evac(oc, pX, mm):
                t = E("scalar", lambda e, oc=oc, pX=pX, buf=buf: e.copy(
                    KB(buf)[:, oc * TT:(oc + 1) * TT], pX[:, :]),
                    deps=[mm, kv_free[buf]], inc=sS)
                k_ready[buf][oc] = t
                return t

            def kg_extra(oc, mmg):
                t = E("vector", lambda e, oc=oc: e.tensor_copy(
                    kg_sb[:, oc * G:(oc + 1) * G], pSC0[:, 0:G]),
                    deps=[mmg], inc=sV)
                kg_ready[oc] = t
                return t

            proj(0, k_evac, kg_extra if tau == 0 else None)
            if STOP == "kproj" and tau == 0:
                E("sync", lambda e: e.nop(),
                  deps=[sm.cur for sm in sems.values()])
                finalize(nc, P)
                return nc

            def v_evac(oc, pX, mm):
                tv = E("vector", lambda e, pX=pX: e.tensor_copy(vtmp[:], pX[:, :]),
                       deps=[mm], inc=sV)
                if STOP == "vprojnotr":
                    last = None
                    for j in range(4):
                        last = E("vector", lambda e, j=j, oc=oc, buf=buf: e.tensor_copy(
                            VB(buf)[:, j * 2048 + oc * 128: j * 2048 + (oc + 1) * 128],
                            vtmp[:, j * 128:(j + 1) * 128]),
                            deps=[tv, kv_free[buf]], inc=sV)
                    pT_state["free"] = last
                    v_ready[buf][oc] = last
                    return last
                trs = []
                for j in range(4):
                    trs.append(E("tensor", lambda e, j=j: e.matmul(
                        pT[:, j * 128:(j + 1) * 128], vtmp[:, j * 128:(j + 1) * 128],
                        eye_b[:], start=True, stop=True),
                        deps=[tv, t_eyeb, pT_state["free"] if j == 0 else None],
                        inc=sT if j == 3 else None))
                last = None
                for j in range(4):
                    last = E("scalar", lambda e, j=j, oc=oc, buf=buf: e.copy(
                        VB(buf)[:, j * 2048 + oc * 128: j * 2048 + (oc + 1) * 128],
                        pT[:, j * 128:(j + 1) * 128]),
                        deps=[trs[3], kv_free[buf]], inc=sS)
                pT_state["free"] = last
                v_ready[buf][oc] = last
                return last

            def vg_extra(oc, mmg):
                tv = E("vector", lambda e: e.tensor_copy(
                    vgt[:, 0:G], pSC0[:, 0:G]), deps=[mmg], inc=sV)
                tr = E("tensor", lambda e: e.matmul(
                    pT[0:G, 0:128], vgt[:, 0:G], eye_b[:], start=True, stop=True),
                    deps=[tv, t_eyeb, pT_state["free"]], inc=sT)
                t = E("vector", lambda e, oc=oc: e.tensor_copy(
                    vg_sb[0:G, oc * 128:(oc + 1) * 128], pT[0:G, 0:128]),
                    deps=[tr], inc=sV)
                pT_state["free"] = t
                vg_ready[oc] = t
                return t

            proj(1, v_evac, vg_extra if (tau == 0 and STOP not in ('vprojnog', 'vprojnotr')) else None)
            if STOP in ("vproj", "vprojnog", "vprojnotr") and tau == 0:
                E("sync", lambda e: e.nop(),
                  deps=[sm.cur for sm in sems.values()])
                finalize(nc, P)
                return nc

            def q_evac(oc, pX, mm):
                return E("scalar", lambda e, oc=oc, pX=pX: e.mul(
                    q_sb[:, oc * TT:(oc + 1) * TT], pX[:, :], float(QSCALE)),
                    deps=[mm, q_free, t_cp1], inc=sS)

            q_toks = proj(2, q_evac, None)
            if STOP == "qproj" and tau == 0:
                E("sync", lambda e: e.nop(),
                  deps=[sm.cur for sm in sems.values()])
                finalize(nc, P)
                return nc

            psc3 = [pSC0, pSC1, pBC1]
            pTb3 = [pT, pA, pB]
            if tau == 0:
                att = {"sc": {}, "exp": {}, "ri": {},
                       "scl": {}, "tr": {}, "ate": {}, "av": {}, "ao": {}}
                attention_state["att"] = att
            att = attention_state["att"]
            git0 = tau * 64
            psc_init = [pSC_free[0], pSC_free[1], ln0_toks[-1]]
            pT_init = [pT_state["free"], pX_free[0], pX_free[1]]
            seen_bank = [False, False, False]
            seen_tbank = [False, False, False]
            seen_avb = [False, False]
            ao_toks = []

            def blocks_of(it):
                qc = it // NH
                a_blk = tau * 4 + qc
                return [a_blk - 2, a_blk - 1, a_blk]

            def sc_e(it):
                git = git0 + it
                b3 = git % 3
                psc = psc3[b3]
                qc, h = divmod(it, NH)
                qsl = q_sb[:, h * TT + qc * 128: h * TT + (qc + 1) * 128]
                extra = att["exp"].get(git - 3)
                if not seen_bank[b3]:
                    seen_bank[b3] = True
                    extra2 = psc_init[b3]
                else:
                    extra2 = None
                msel = (1 if (tau == 0 and qc == 0)
                        else (2 if (tau == 0 and qc == 1) else 0))
                E("tensor", lambda e, psc=psc, qsl=qsl, h=h: e.matmul(
                    psc[:, 0:G], qsl, kg_sb[:, h * G:(h + 1) * G],
                    start=True, stop=True),
                    deps=[q_toks[h], kg_ready[h], extra, extra2], inc=None)
                t_mm = None
                for j, bk in enumerate(blocks_of(it)):
                    if msel == 2:
                        mof = 384 if j == 0 else j * 128
                    else:
                        mof = msel * 384 + j * 128
                    E("tensor", lambda e, psc=psc, j=j, mof=mof: e.matmul(
                        psc[:, G + j * 128:G + (j + 1) * 128], eye_b[:],
                        msk_sb[:, mof:mof + 128],
                        start=True, stop=False),
                        deps=[t_msk, t_eyeb] if j == 0 else [], inc=None)
                    bb = (bk // 4) % 2
                    slot = bk % 4
                    t_mm = E("tensor", lambda e, psc=psc, qsl=qsl, bb=bb,
                             slot=slot, h=h, j=j: e.matmul(
                        psc[:, G + j * 128:G + (j + 1) * 128], qsl,
                        KB(bb)[:, h * TT + slot * 128: h * TT + (slot + 1) * 128],
                        start=False, stop=True),
                        deps=[k_ready[bb][h]], inc=sT if j == 2 else None)
                att["sc"][git] = t_mm

            def exp_e(it):
                git = git0 + it
                psc = psc3[git % 3]
                nb = git % 3
                att["exp"][git] = E("scalar", lambda e, psc=psc, nb=nb: e.activation(
                    out=aat[nb][:, 0:400], in_=psc[:, 0:400], func=ACT.Exp,
                    bias=0.0, scale=1.0, accum_out=sume[nb][:, :]),
                    deps=[att["sc"][git], att["tr"].get(git - 3),
                          att["ri"].get(git - 3)], inc=sS)

            def ri_e(it):
                git = git0 + it
                nb = git % 3
                att["ri"][git] = E("vector", lambda e, nb=nb: e.reciprocal(
                    rinv[nb][:], sume[nb][:]),
                    deps=[att["exp"][git], att["scl"].get(git - 3)], inc=sV)

            def scl_e(it):
                git = git0 + it
                nb = git % 3
                att["scl"][git] = E("vector", lambda e, nb=nb: e.tensor_scalar(
                    out=aat[nb][:, 0:400], in0=aat[nb][:, 0:400],
                    scalar1=rinv[nb][:, 0:1], scalar2=None, op0=ALU.mult),
                    deps=[att["ri"][git]], inc=sV)

            def tr_e(it):
                git = git0 + it
                b3 = git % 3
                pTb = pTb3[b3]
                if not seen_tbank[b3]:
                    seen_tbank[b3] = True
                    extra = pT_init[b3]
                else:
                    extra = None
                E("tensor", lambda e, pTb=pTb, b3=b3: e.matmul(
                    pTb[0:G, 0:128], aat[b3][:, 0:G], eye_b[:],
                    start=True, stop=True),
                    deps=[att["scl"][git], att["ate"].get(git - 3), extra,
                          t_eyeb], inc=None)
                tr_last = None
                for j in range(3):
                    tr_last = E("tensor", lambda e, j=j, pTb=pTb, b3=b3: e.matmul(
                        pTb[:, 128 * (j + 1):128 * (j + 2)],
                        aat[b3][:, G + j * 128:G + (j + 1) * 128], eye_b[:],
                        start=True, stop=True),
                        deps=[], inc=sT if j == 2 else None)
                att["tr"][git] = tr_last

            def ate_e(it):
                git = git0 + it
                b3 = git % 3
                att["ate"][git] = E("vector", lambda e, b3=b3: e.tensor_copy(
                    at_sb[b3][:], pTb3[b3][:, 0:512]),
                    deps=[att["tr"][git], att["av"].get(git - 3)], inc=sV)

            def av_e(it):
                git = git0 + it
                b3 = git % 3
                bk2 = git % 2
                pAVb = [pAV, pBC2][bk2]
                rv = ((git // 2) % 2) * 128
                qc, h = divmod(it, NH)
                if bk2 == 1 and not seen_avb[1]:
                    seen_avb[1] = True
                    extra = ln0_toks[-1]
                elif bk2 == 0 and not seen_avb[0]:
                    seen_avb[0] = True
                    extra = pAV_free
                else:
                    extra = None
                E("tensor", lambda e, h=h, b3=b3, rv=rv, pAVb=pAVb: e.matmul(
                    pAVb[:, rv:rv + 128],
                    vg_sb[0:G, h * 128:(h + 1) * 128],
                    at_sb[b3][0:G, 0:128], start=True, stop=False),
                    deps=[att["ate"][git], vg_ready[h],
                          att["ao"].get(git - 4), extra],
                    inc=None)
                t_av = None
                for j, bkk in enumerate(blocks_of(it)):
                    bb = (bkk // 4) % 2
                    slot = bkk % 4
                    t_av = E("tensor", lambda e, bb=bb, slot=slot, h=h, j=j,
                             b3=b3, rv=rv, pAVb=pAVb: e.matmul(
                        pAVb[:, rv:rv + 128],
                        VB(bb)[:, slot * 2048 + h * 128: slot * 2048 + (h + 1) * 128],
                        at_sb[b3][:, 128 * (j + 1):128 * (j + 2)],
                        start=False, stop=(j == 2)),
                        deps=[v_ready[bb][h]], inc=sT if j == 2 else None)
                att["av"][git] = t_av

            def ao_e(it):
                git = git0 + it
                bk2 = git % 2
                pAVb = [pAV, pBC2][bk2]
                rv = ((git // 2) % 2) * 128
                qc, h = divmod(it, NH)
                t_ao = E("scalar", lambda e, h=h, qc=qc, rv=rv, pAVb=pAVb: e.copy(
                    ao_sb[:, h * TT + qc * 128: h * TT + (qc + 1) * 128],
                    pAVb[:, rv:rv + 128]),
                    deps=[att["av"][git], ao_free], inc=sS)
                att["ao"][git] = t_ao
                ao_toks.append(t_ao)

            NIT = 4 * NH
            for p in range(NIT + 6):
                cast_pump(1)
                if tau > 0:
                    gather_pump(2)
                if p < NIT:
                    sc_e(p)
                if 0 <= p - 2 < NIT:
                    exp_e(p - 2)
                if 0 <= p - 3 < NIT:
                    ri_e(p - 3)
                    scl_e(p - 3)
                if 0 <= p - 4 < NIT:
                    tr_e(p - 4)
                if 0 <= p - 5 < NIT:
                    ate_e(p - 5)
                    av_e(p - 5)
                if 0 <= p - 6 < NIT:
                    ao_e(p - 6)

            glast = git0 + NIT - 1
            pSC_free[0] = att["exp"][[g for g in range(git0, git0 + NIT) if g % 3 == 0][-1]]
            pSC_free[1] = att["exp"][[g for g in range(git0, git0 + NIT) if g % 3 == 1][-1]]
            st8["pbc_free"] = att["exp"][[g for g in range(git0, git0 + NIT) if g % 3 == 2][-1]]
            pT_state["free"] = att["ate"][[g for g in range(git0, git0 + NIT) if g % 3 == 0][-1]]
            pX_free[0] = att["ate"][[g for g in range(git0, git0 + NIT) if g % 3 == 1][-1]]
            pX_free[1] = att["ate"][[g for g in range(git0, git0 + NIT) if g % 3 == 2][-1]]
            pAV_free = att["ao"][[g for g in range(git0, git0 + NIT) if g % 2 == 0][-1]]
            st8["pbc2_free"] = att["ao"][[g for g in range(git0, git0 + NIT) if g % 2 == 1][-1]]
            score_last_tok = att["sc"][glast]
            q_free = score_last_tok
            kv_free[0] = score_last_tok
            kv_free[1] = score_last_tok
            xn_free = q_toks[-1]
            last_av_tok = ao_toks[-1]
            ao_head = [ao_toks[3 * NH + h] for h in range(NH)]
            if STOP == "attn" and tau == 0:
                E("sync", lambda e: e.nop(),
                  deps=[sm.cur for sm in sems.values()])
                finalize(nc, P)
                return nc

            wo_resid = []
            ln1_state = {}
            wo_mm_last = None
            for oc in range(16):
                wp = oc % 4
                if tau == 0:
                    ctoks, stoks = cast_direct(3, oc)
                    wdep = lambda hc, ctoks=ctoks: ctoks[hc // 4]
                else:
                    base = 3 * 256 + oc * 16
                    r0 = base * 128
                    tw = E("sync", lambda e, wp=wp, r0=r0: e.dma_start(
                        out=ws[wp][:], in_=wscr[:, r0:r0 + 2048]),
                        deps=[ws_free[wp]] + ws_cast_deps[wp] + scr_deps(3, oc),
                        inc=sWS[wp], amt=16)
                    wdep = lambda hc, tw=tw: tw
                pX = [pA, pB][oc % 2]
                mm = None
                for hc in range(16):
                    mm = E("tensor", lambda e, wp=wp, hc=hc, pX=pX: e.matmul(
                        pX[:, :], ws[wp][:, hc * 128:(hc + 1) * 128],
                        ao_sb[:, hc * TT:hc * TT + TT],
                        start=(hc == 0), stop=(hc == 15)),
                        deps=[wdep(hc), ao_head[hc],
                              pX_free[oc % 2] if hc == 0 else None],
                        inc=sT if hc == 15 else None)
                ws_free[wp] = mm
                if tau == 0:
                    ws_cast_deps[wp] = [mm] + stoks
                    cst["limit"] = max(cst["limit"], (48 + oc + 5) * 4)
                wo_mm_last = mm
                tr = E("vector", lambda e, oc=oc, pX=pX: e.scalar_tensor_tensor(
                    out=x_sb[:, oc * TT:(oc + 1) * TT],
                    in0=x_sb[:, oc * TT:(oc + 1) * TT], scalar=2.0,
                    in1=pX[:, :], op0=ALU.mult, op1=ALU.add),
                    deps=[mm, x_chunk[oc]], inc=sV)
                pX_free[oc % 2] = tr
                wo_resid.append(tr)
                gather_pump(7)
            ao_free = wo_mm_last

            if STOP == "y1":
                for dc in range(NC):
                    h_dma[dc] = E("gpsimd", lambda e, dc=dc, tau=tau: e.dma_start(
                        out=y_out[dc * 128:(dc + 1) * 128, tau * TT:(tau + 1) * TT],
                        in_=x_sb[:, dc * TT:(dc + 1) * TT]),
                        deps=[wo_resid[dc]], inc=sST, amt=16)
                x_free = wo_resid[-1]
                continue

            for dc in range(NC):
                ln_stat_chunk(x_sb, dc, wo_resid[dc], ln1_state)
            ln1_toks = ln_finish(x_sb, None, ln1_state)
            prev_h = h_dma[NC // 2 - 1] if tau > 0 else None
            for j in range(NC // 2):
                prev_h = E("scalar", lambda e, j=j, tau=tau: e.dma_start(
                    out=hscr[tau][:, 2 * j * TT:(2 * j + 2) * TT],
                    in_=x_sb[:, 2 * j * TT:(2 * j + 2) * TT]),
                    deps=[ln1_toks[2 * j], ln1_toks[2 * j + 1], prev_h],
                    inc=sHS, amt=16)
                h_dma[j] = prev_h
            prev_ln1_toks = ln1_toks
            if STOP == "h":
                for dc in range(NC):
                    E("gpsimd", lambda e, dc=dc, tau=tau: e.dma_start(
                        out=y_out[dc * 128:(dc + 1) * 128, tau * TT:(tau + 1) * TT],
                        in_=x_sb[:, dc * TT:(dc + 1) * TT]),
                        deps=[ln1_toks[dc]], inc=sST, amt=16)
            x_free = ln1_toks[-1]

        cast_pump(10 ** 9)
        gather_pump(10 ** 9)
        t_ag = E("gpsimd", lambda e: e.collective_compute(
            "AllGather", ALU.bypass, replica_groups=[list(range(NCORES))],
            ins=[agin.opt()], outs=[agout.opt()]),
            deps=[sm.cur for sm in sAG], inc=sCC)

        if STOP in ("ln0", "y1", "h"):
            E("sync", lambda e: e.nop(),
              deps=[sm.cur for sm in sems.values()])
            finalize(nc, P)
            return nc

        def ag_pieces(tile0, ntiles):
            pieces = []
            t = tile0
            off = 0
            while ntiles > 0:
                cb = t // PER_CORE
                rel = t % PER_CORE
                n = min(ntiles, PER_CORE - rel)
                pieces.append((cb, rel, off, n))
                t += n
                off += n
                ntiles -= n
            return pieces

        def ag_load(dest, tile0, ntiles, deps, sem):
            tok = None
            for cb, rel, off, n in ag_pieces(tile0, ntiles):
                tok = E("sync", lambda e, cb=cb, rel=rel, off=off, n=n, dest=dest:
                        e.dma_start(
                    out=dest[:, off * 128:(off + n) * 128],
                    in_=agout[cb][:, rel * 128:(rel + n) * 128]),
                    deps=deps, inc=sem, amt=16)
            return tok

        h2_ready = [None] * NI
        wd_free = [q_free, ao_free]
        pG_free = [wo_resid[14], pSC_free[0]]
        pU_free = [wo_resid[15], pSC_free[1]]
        pD_free = [pT_state["free"], pAV_free]
        sg_free = [None, None]
        y_dma = [None] * NC
        hload_free = list(h_dma)
        xn_cons = xn_free
        h2_cons = last_av_tok
        q_ao_free = ao_free

        XN2 = AR[:, NI * TT:NI * TT + NC * TT]
        pre_hb = {}
        cin_free2 = list(cast_in_free[:3])
        xnb_cons = [None, last_av_tok]

        def prefetch_chunk(t, dc):
            p = dc % 3
            buf = xn_sb if t % 2 == 0 else XN2
            tpin = E("scalar", lambda e, p=p, t=t, dc=dc: e.dma_start(
                out=cin[p][:], in_=hscr[t][:, dc * TT:(dc + 1) * TT]),
                deps=[cin_free2[p]], inc=sCI[p], amt=16)
            tc = E("vector", lambda e, p=p, dc=dc, buf=buf: e.tensor_copy(
                buf[:, dc * TT:(dc + 1) * TT], cin[p][:]),
                deps=[tpin, xnb_cons[t % 2]], inc=sV)
            cin_free2[p] = tc
            pre_hb.setdefault(t, []).append(tc)

        for tau in range(NTILE):
            if tau == 0:
                hb_load = []
                hb_toks = []
                for dc in range(NC):
                    p = dc % 3
                    tok = E("scalar", lambda e, dc=dc, p=p: e.dma_start(
                        out=x_sb[:, dc * TT:(dc + 1) * TT],
                        in_=hscr[0][:, dc * TT:(dc + 1) * TT]),
                        deps=[prev_ln1_toks[dc], h_dma[dc // 2], xchain[p]],
                        inc=sCI[p], amt=16)
                    xchain[p] = tok
                    hb_load.append(tok)
                    hb_toks.append(E("vector", lambda e, dc=dc: e.tensor_copy(
                        xn_sb[:, dc * TT:(dc + 1) * TT],
                        x_sb[:, dc * TT:(dc + 1) * TT]),
                        deps=[tok, xn_cons], inc=sV))
            else:
                t_h = E("scalar", lambda e, tau=tau: e.dma_start(
                    out=x_sb[:], in_=hscr[tau]),
                    deps=[x_free] + list(y_dma), inc=sX, amt=16)
                hb_toks = pre_hb[tau]
            XN = xn_sb if tau % 2 == 0 else XN2

            last_mm_u = None
            for ic in range(NI):
                wpg = (2 * ic) % 4
                wpu = (2 * ic + 1) % 4
                twg = ag_load(ws[wpg], ic * 16, 16,
                              [t_ag, ws_free[wpg]], sWS[wpg])
                twu = ag_load(ws[wpu], N_G + ic * 16, 16,
                              [t_ag, ws_free[wpu]], sWS[wpu])
                pg = [pA, pSC0][ic % 2]
                pu = [pB, pSC1][ic % 2]
                mmg = None
                for hc in range(16):
                    mmg = E("tensor", lambda e, wpg=wpg, hc=hc, pg=pg, XN=XN:
                        e.matmul(
                        pg[:, :], ws[wpg][:, hc * 128:(hc + 1) * 128],
                        XN[:, hc * TT:hc * TT + TT],
                        start=(hc == 0), stop=(hc == 15)),
                        deps=[twg, hb_toks[hc],
                              pG_free[ic % 2] if hc == 0 else None],
                        inc=sT if hc == 15 else None)
                mmu = None
                for hc in range(16):
                    mmu = E("tensor", lambda e, wpu=wpu, hc=hc, pu=pu, XN=XN:
                        e.matmul(
                        pu[:, :], ws[wpu][:, hc * 128:(hc + 1) * 128],
                        XN[:, hc * TT:hc * TT + TT],
                        start=(hc == 0), stop=(hc == 15)),
                        deps=[twu, hb_toks[hc],
                              pU_free[ic % 2] if hc == 0 else None],
                        inc=sT if hc == 15 else None)
                ws_free[wpg] = mmg
                ws_free[wpu] = mmu
                last_mm_u = mmu
                sp = ic % 2
                t_si = E("scalar", lambda e, sp=sp, pg=pg: e.activation(
                    out=sg[sp][:], in_=pg[:, :], func=ACT.Silu),
                    deps=[mmg, sg_free[sp]], inc=sS)
                pG_free[ic % 2] = t_si
                t_h2 = E("vector", lambda e, sp=sp, ic=ic, pu=pu: e.tensor_tensor(
                    out=H2[:, ic * TT:(ic + 1) * TT], in0=sg[sp][:], in1=pu[:, :],
                    op=ALU.mult), deps=[t_si, mmu, h2_cons], inc=sV)
                pU_free[ic % 2] = t_h2
                sg_free[sp] = t_h2
                h2_ready[ic] = t_h2
            xnb_cons[tau % 2] = last_mm_u

            down_resid = []
            ln2_state = {}
            last_mm_d = None
            for oc in range(16):
                wd = [q_sb, ao_sb][oc % 2]
                twd = ag_load(wd, 2 * N_G + oc * NI, NI,
                              [t_ag, wd_free[oc % 2], q_ao_free], sWD[oc % 2])
                pD = [pT, pAV][oc % 2]
                mmd = None
                for ic in range(NI):
                    mmd = E("tensor", lambda e, wd=wd, ic=ic, pD=pD: e.matmul(
                        pD[:, :], wd[:, ic * 128:(ic + 1) * 128],
                        H2[:, ic * TT:(ic + 1) * TT],
                        start=(ic == 0), stop=(ic == NI - 1)),
                        deps=[twd, h2_ready[ic],
                              pD_free[oc % 2] if ic == 0 else None],
                        inc=sT if ic == NI - 1 else None)
                wd_free[oc % 2] = mmd
                last_mm_d = mmd
                tr = E("vector", lambda e, oc=oc, pD=pD: e.tensor_tensor(
                    out=x_sb[:, oc * TT:(oc + 1) * TT],
                    in0=x_sb[:, oc * TT:(oc + 1) * TT], in1=pD[:, :], op=ALU.add),
                    deps=[mmd, hb_load[oc] if tau == 0 else t_h], inc=sV)
                pD_free[oc % 2] = tr
                down_resid.append(tr)
                if tau + 1 < NTILE:
                    prefetch_chunk(tau + 1, oc)
            q_ao_free = None
            h2_cons = last_mm_d
            st8["pbc_free"] = down_resid[-1]

            for dc in range(NC):
                ln_stat_chunk(x_sb, dc, down_resid[dc], ln2_state)
            ln2_toks = ln_finish(x_sb, None, ln2_state)
            for dc in range(NC):
                y_dma[dc] = E("scalar", lambda e, dc=dc, tau=tau: e.dma_start(
                    out=y_out[dc * 128:(dc + 1) * 128, tau * TT:(tau + 1) * TT],
                    in_=x_sb[:, dc * TT:(dc + 1) * TT]),
                    deps=[ln2_toks[dc]], inc=sST, amt=16)
            x_free = ln2_toks[-1]

        E("sync", lambda e: e.nop(),
          deps=[sm.cur for sm in sems.values()])
        finalize(nc, P)
    return nc


def finalize(nc, P):
    with nc.Block() as block:
        block.sync(lambda e: P.run("sync", e))
        block.vector(lambda e: P.run("vector", e))
        block.scalar(lambda e: P.run("scalar", e))
        block.tensor(lambda e: P.run("tensor", e))
        block.gpsimd(lambda e: P.run("gpsimd", e))



def _tileize(w):
    w4 = w.reshape(16, 128, 16, 128)
    t = w4.transpose(0, 2, 3, 1).reshape(256, 128, 128)
    return np.ascontiguousarray(t.transpose(1, 0, 2).reshape(128, 256 * 128))


def _masks():
    q = np.arange(128)[:, None]
    k = np.arange(384)[None, :]
    base = np.where((k >= q + 1) & (k <= q + 256), 0.0, NEG).astype(np.float32)
    m0 = np.where(k >= 256, base, NEG).astype(np.float32)
    return np.ascontiguousarray(np.concatenate([base, m0], axis=1))


def _idx_pack(gate_idx, up_idx, down_idx):
    gT = np.ascontiguousarray(gate_idx.T)
    uT = np.ascontiguousarray(up_idx.T)
    dT = np.ascontiguousarray(down_idx.T)
    jobs = ffn_jobs()
    packs = []
    for c in range(NCORES):
        part = jobs[c * PER_CORE:(c + 1) * PER_CORE]
        tiles = np.empty((PER_CORE, 128, 128), np.int32)
        for j, (m, a, b) in enumerate(part):
            if m == "g":
                tiles[j] = gT[b * 128:(b + 1) * 128, a * 128:(a + 1) * 128]
            elif m == "u":
                tiles[j] = uT[b * 128:(b + 1) * 128, a * 128:(a + 1) * 128]
            else:
                tiles[j] = dT[b * 128:(b + 1) * 128, a * 128:(a + 1) * 128]
        packs.append(np.ascontiguousarray(
            tiles.transpose(1, 0, 2).reshape(128, PER_CORE * 128)))
    return packs


_prog_cache = {}


def _get_prog():
    key = STOP
    if key not in _prog_cache:
        _prog_cache[key] = build_program()
    return _prog_cache[key]


def kernel(L0_attn_x, L0_attn_global_tokens, L0_attn_ln_scale, L0_attn_ln_bias,
           L0_attn_Wq, L0_attn_Wk, L0_attn_Wv, L0_attn_Wo,
           L0_ffn_lut, L0_ffn_gate_idx, L0_ffn_up_idx, L0_ffn_down_idx):
    x = np.asarray(L0_attn_x, dtype=np.float32)
    g = np.asarray(L0_attn_global_tokens, dtype=np.float32)
    gam = np.asarray(L0_attn_ln_scale, dtype=np.float32)
    bet = np.asarray(L0_attn_ln_bias, dtype=np.float32)
    wq = np.asarray(L0_attn_Wq, dtype=np.float32)
    wk = np.asarray(L0_attn_Wk, dtype=np.float32)
    wv = np.asarray(L0_attn_Wv, dtype=np.float32)
    wo = np.asarray(L0_attn_Wo, dtype=np.float32)
    lut = np.asarray(L0_ffn_lut, dtype=np.float32)
    gidx = np.asarray(L0_ffn_gate_idx, dtype=np.int32)
    uidx = np.asarray(L0_ffn_up_idx, dtype=np.int32)
    didx = np.asarray(L0_ffn_down_idx, dtype=np.int32)

    wq_t, wk_t, wv_t, wo_t = map(_tileize, (wq, wk, wv, wo))
    g_t = np.ascontiguousarray(g[0].T)
    gam_t = np.ascontiguousarray(gam.reshape(16, 128).T)
    bet_t = np.ascontiguousarray(bet.reshape(16, 128).T)
    masks = _masks()
    eye = np.eye(128, dtype=np.float32)
    packs = _idx_pack(gidx, uidx, didx)

    nc = _get_prog()
    in_maps = []
    for c in range(NCORES):
        in_maps.append({
            "x_t": np.ascontiguousarray(
                x[c].T.reshape(NC, 128, NTILE, TT).transpose(2, 1, 0, 3)
                .reshape(NTILE, 128, NC * TT)),
            "wk_t": wk_t, "wv_t": wv_t, "wq_t": wq_t, "wo_t": wo_t,
            "idxp": packs[c],
            "lut_in": lut,
            "g_in": g_t,
            "gam_in": gam_t,
            "bet_in": bet_t,
            "eye_in": eye,
            "msk_in": masks,
        })
    res = run_bass_kernel_spmd(nc, in_maps, core_ids=list(range(NCORES)))
    out = np.stack([np.ascontiguousarray(r["y"].T) for r in res.results])
    return out


# version 33
# speedup vs baseline: 1.0177x; 1.0014x over previous
import numpy as np
import ml_dtypes
from contextlib import ExitStack

import concourse.bass as bass
import concourse.mybir as mybir
from concourse.bass_utils import run_bass_kernel_spmd

F32 = mybir.dt.float32
BF16 = mybir.dt.bfloat16
I32 = mybir.dt.int32
AX = mybir.AxisListType.X
ALU = mybir.AluOpType
ACT = mybir.ActivationFunctionType

D = 2048
T = 2048
NH = 16
G = 16
NC = 16
TT = 512
NTILE = T // TT
INTER = 5504
NI = INTER // 128
NCORES = 8
QSCALE = 1.0 / float(np.sqrt(128.0))
EPS = 1e-5
NEG = -30000.0

N_G = NI * 16
N_TILES_FFN = 3 * N_G
PER_CORE = N_TILES_FFN // NCORES
GATHER_BS = 4
CAST_BS = 4
N_ATTN_TILES = 4 * 256

STOP = "full"

ENGINES = ("sync", "vector", "scalar", "tensor", "gpsimd")


class Sem:
    __slots__ = ("name", "h", "val")

    def __init__(self, name):
        self.name = name
        self.h = None
        self.val = 0

    def tok(self, k=1):
        self.val += k
        return (self, self.val)

    @property
    def cur(self):
        return (self, self.val)


class Plan:
    def __init__(self):
        self.ops = {e: [] for e in ENGINES}
        self.waited = {e: {} for e in ENGINES}

    def emit(self, eng, fn, deps=(), inc=None, amt=1):
        need = {}
        for d in deps:
            if d is None:
                continue
            sem, val = d
            if val > need.get(sem, 0):
                need[sem] = val
        ws = []
        for sem, val in need.items():
            if val > self.waited[eng].get(sem, 0):
                ws.append((sem, val))
                self.waited[eng][sem] = val
        tok = inc.tok(amt) if inc is not None else None
        self.ops[eng].append((ws, fn, inc, amt))
        return tok

    def run(self, eng_name, e):
        for ws, fn, inc, amt in self.ops[eng_name]:
            for sem, val in ws:
                e.wait_ge(sem.h, val)
            inst = fn(e)
            if inc is not None:
                inst.then_inc(inc.h, amt)


def ffn_jobs():
    jobs = []
    for ic in range(NI):
        for hc in range(16):
            jobs.append(("g", ic, hc))
    for ic in range(NI):
        for hc in range(16):
            jobs.append(("u", ic, hc))
    for oc in range(16):
        for ic in range(NI):
            jobs.append(("d", oc, ic))
    return jobs


def build_program():
    nc = bass.Bass("TRN2", target_bir_lowering=False, debug=False)

    x_t = nc.dram_tensor("x_t", [NTILE, 128, NC * TT], F32, kind="ExternalInput").ap()
    winp = [
        nc.dram_tensor(n, [128, 256 * 128], F32, kind="ExternalInput").ap()
        for n in ("wk_t", "wv_t", "wq_t", "wo_t")
    ]
    idxp = nc.dram_tensor("idxp", [128, PER_CORE * 128], I32, kind="ExternalInput").ap()
    lut_in = nc.dram_tensor("lut_in", [16], F32, kind="ExternalInput").ap()
    g_in = nc.dram_tensor("g_in", [D, G], F32, kind="ExternalInput").ap()
    gam_in = nc.dram_tensor("gam_in", [128, 16], F32, kind="ExternalInput").ap()
    bet_in = nc.dram_tensor("bet_in", [128, 16], F32, kind="ExternalInput").ap()
    eye_in = nc.dram_tensor("eye_in", [128, 128], F32, kind="ExternalInput").ap()
    msk_in = nc.dram_tensor("msk_in", [128, 2 * 384], F32, kind="ExternalInput").ap()

    y_out = nc.dram_tensor("y", [D, T], F32, kind="ExternalOutput").ap()

    wscr = nc.dram_tensor("wscr", [128, N_ATTN_TILES * 128], BF16).ap()
    agin = nc.dram_tensor("agin", [128, PER_CORE * 128], BF16).ap()
    agout = nc.dram_tensor("agout", [NCORES, 128, PER_CORE * 128], BF16).ap()
    hscr = nc.dram_tensor("hscr", [NTILE, 128, NC * TT], F32).ap()

    with ExitStack() as ctx:
        sb = lambda n, s, d=F32: ctx.enter_context(nc.sbuf_tensor(n, s, d))
        psum = lambda n, s, d=F32: ctx.enter_context(nc.psum_tensor(n, s, d))

        x_sb = sb("x_sb", [128, NC * TT])
        xn_sb = sb("xn_sb", [128, NC * TT], BF16)
        q_sb = sb("q_sb", [128, NH * TT], BF16)
        ao_sb = sb("ao_sb", [128, NH * TT], BF16)
        AR = sb("AR", [128, 4 * 8192], BF16)
        ws = [sb(f"ws{i}", [128, 2048], BF16) for i in range(4)]
        vtmp = sb("vtmp", [128, 512], BF16)
        vgt = sb("vgt", [128, 16], BF16)
        at_sb = [sb(f"at_sb{i}", [128, 512], BF16) for i in range(3)]
        aat = [sb(f"aat{i}", [128, 512], BF16) for i in range(3)]
        msk_sb = sb("msk_sb", [128, 2 * 384], BF16)
        eye_b = sb("eye_b", [128, 128], BF16)
        gfm_b = sb("gfm_b", [128, NC * G], BF16)
        kg_sb = sb("kg_sb", [128, NH * G], BF16)
        vg_sb = sb("vg_sb", [16, D], BF16)
        gam_sb = sb("gam_sb", [128, 16])
        bet_sb = sb("bet_sb", [128, 16])
        lut_sb = sb("lut_sb", [128, 16])
        ones_b = sb("ones_b", [128, 1], BF16)
        cpool = sb("cpool", [128, 2])
        stf = [sb(f"stf{i}", [128, 512]) for i in range(2)]
        stb = [sb(f"stb{i}", [128, 512], BF16) for i in range(2)]
        sq = [sb(f"sq{i}", [128, 512], BF16) for i in range(2)]
        sg = [sb(f"sg{i}", [128, 512], BF16) for i in range(2)]
        idxi = [sb(f"idxi{i}", [128, 512], I32) for i in range(2)]
        idxb = [sb(f"idxb{i}", [128, 512], BF16) for i in range(2)]
        term = [sb("term0", [128, 512], BF16)]
        gout = [sb(f"gout{i}", [128, 512], BF16) for i in range(2)]
        cin = [sb(f"cin{i}", [128, 512]) for i in range(3)]
        smalls = sb("smalls", [1, 1024])
        bcst_t = sb("bcst", [1, 1152])
        bcst = bcst_t[:, 0:1024]
        ones_f = bcst_t[:, 1024:1152]
        negmax = [sb(f"negmax{i}", [128, 1]) for i in range(3)]
        sume = [sb(f"sume{i}", [128, 1]) for i in range(3)]
        rinv = [sb(f"rinv{i}", [128, 1]) for i in range(3)]

        nc.const_aps.register(F32, -0.5, cpool[:, 0:1])
        nc.const_aps.register(F32, float(QSCALE), cpool[:, 1:2])

        KB = lambda b: AR[:, b * 8192 : (b + 1) * 8192]
        VB = lambda b: AR[:, (2 + b) * 8192 : (3 + b) * 8192]
        H2 = AR

        pA = psum("pA", [128, 512])
        pB = psum("pB", [128, 512])
        pBC1 = psum("pBC1", [128, 512])
        pBC2 = psum("pBC2", [128, 512])
        pSC0 = psum("pSC0", [128, 512])
        pSC1 = psum("pSC1", [128, 512])
        pT = psum("pT", [128, 512])
        pAV = psum("pAV", [128, 512])

        names = (["sK", "sX", "sHS", "sST", "sV", "sS", "sT", "sCC"]
                 + [f"sCI{i}" for i in range(4)] + [f"sGI{i}" for i in range(4)]
                 + [f"sCO{i}" for i in range(4)] + [f"sAG{i}" for i in range(4)]
                 + ["sWD0", "sWD1"]
                 + [f"sWS{i}" for i in range(4)])
        sems = {n: Sem(n) for n in names}
        for sm in sems.values():
            sm.h = ctx.enter_context(nc.semaphore(sm.name))
        sK = sems["sK"]; sX = sems["sX"]; sHS = sems["sHS"]; sST = sems["sST"]
        sV = sems["sV"]; sS = sems["sS"]; sT = sems["sT"]; sCC = sems["sCC"]
        sCI = [sems[f"sCI{i}"] for i in range(4)]
        sGI = [sems[f"sGI{i}"] for i in range(4)]
        sCO = [sems[f"sCO{i}"] for i in range(4)]
        sAG = [sems[f"sAG{i}"] for i in range(4)]
        sWD = [sems["sWD0"], sems["sWD1"]]
        sWS = [sems[f"sWS{i}"] for i in range(4)]

        P = Plan()
        E = P.emit

        E("sync", lambda e: e.dma_start(
            out=sq[0][:].bitcast(F32)[:, 0:NC * G].rearrange("p (hc k) -> p hc k", hc=NC),
            in_=g_in.rearrange("(hc p) k -> p hc k", p=128)), inc=sK, amt=16)
        E("sync", lambda e: e.dma_start(out=gam_sb[:], in_=gam_in), inc=sK, amt=16)
        E("sync", lambda e: e.dma_start(out=bet_sb[:], in_=bet_in), inc=sK, amt=16)
        E("sync", lambda e: e.dma_start(
            out=lut_sb[:, :], in_=lut_in[None, :].partition_broadcast(128)), inc=sK, amt=16)
        t_consts = sK.cur
        t_gam = t_bet = t_lut = t_consts
        t_mskf0 = E("sync", lambda e: e.dma_start(out=stf[0][:, 0:384], in_=msk_in[:, 0:384]), inc=sCI[3], amt=16)
        t_mskf1 = E("sync", lambda e: e.dma_start(out=stf[1][:, 0:384], in_=msk_in[:, 384:768]), inc=sGI[3], amt=16)
        t_mc0 = E("vector", lambda e: e.tensor_copy(msk_sb[:, 0:384], stf[0][:, 0:384]),
                  deps=[t_mskf0], inc=sV)
        t_mc1 = E("vector", lambda e: e.tensor_copy(msk_sb[:, 384:768], stf[1][:, 0:384]),
                  deps=[t_mskf1], inc=sV)
        t_eyef = E("sync", lambda e: e.dma_start(out=stf[1][:, 0:128], in_=eye_in),
                   deps=[t_mc1], inc=sGI[3], amt=16)
        t_eyeb = E("vector", lambda e: e.tensor_copy(eye_b[:], stf[1][:, 0:128]),
                   deps=[t_eyef], inc=sV)
        t_gfm = E("vector", lambda e: e.tensor_copy(
            gfm_b[:], sq[0][:].bitcast(F32)[:, 0:NC * G]), deps=[t_consts], inc=sV)
        t_msk = t_mc1
        t_x0 = E("scalar", lambda e: e.dma_start(
            out=x_sb[:], in_=x_t[0]), inc=sX, amt=16)
        t_ones = E("vector", lambda e: e.memset(ones_b[:], 1.0), inc=sV)
        t_onesf = E("vector", lambda e: e.memset(ones_f[:], 1.0), inc=sV)
        t_cp0 = E("vector", lambda e: e.memset(cpool[:, 0:1], -0.5), inc=sV)
        t_cp1 = E("vector", lambda e: e.memset(cpool[:, 1:2], float(QSCALE)), inc=sV)
        t_z1 = E("vector", lambda e: e.memset(KB(1)[:, :], 0.0), inc=sV)
        t_z2 = E("vector", lambda e: e.memset(VB(1)[:, :], 0.0), inc=sV)

        NCB = 256 // CAST_BS
        scr_ready = {}
        cast_in_free = [None, None, None, None, None]
        cast_store_prev = [None, None]
        ws_cast_deps = [[], [], [], []]
        cst = {"i": 0, "done": False, "limit": 16, "tok": {}, "st": {}}

        def cast_pump(n):
            for _ in range(n):
                if cst["done"] or cst["i"] >= cst["limit"]:
                    return
                i = cst["i"]
                m, b = divmod(i, NCB)
                p = i % 5
                land = cin[p] if p < 3 else stf[p - 3]
                psem = sCI[p] if p < 4 else sGI[2]
                s = i % 2
                wp = (i // 4) % 4
                sub = i % 4
                t0 = b * CAST_BS
                src = winp[m][:, t0 * 128:t0 * 128 + 512]
                first_stf = p >= 3 and cast_in_free[p] is None
                tin = E("sync", lambda e, src=src, land=land: e.dma_start(
                    out=land[:], in_=src),
                    deps=[cast_in_free[p]]
                         + ([st8["stf_free"][p - 3]] if first_stf else []),
                    inc=psem, amt=16)
                tc = E("vector", lambda e, land=land, wp=wp, sub=sub:
                       e.tensor_copy(
                    ws[wp][:, sub * 512:(sub + 1) * 512], land[:]),
                    deps=[tin] + ws_cast_deps[wp], inc=sV)
                cast_in_free[p] = tc
                r0 = (m * 256 + t0) * 128
                tst = E("scalar", lambda e, wp=wp, sub=sub, r0=r0: e.dma_start(
                    out=wscr[:, r0:r0 + 512],
                    in_=ws[wp][:, sub * 512:(sub + 1) * 512]),
                    deps=[tc, cast_store_prev[s]], inc=sCO[s], amt=16)
                cast_store_prev[s] = tst
                scr_ready[(m, b)] = tst
                cst["tok"][i] = tc
                cst["st"][i] = tst
                cst["i"] = i + 1
                if cst["i"] >= 4 * NCB:
                    cst["done"] = True
                    st8["stf_free"][0] = cast_in_free[3]
                    st8["stf_free"][1] = cast_in_free[4]

        def cast_direct(m, oc):
            need = m * 64 + oc * 4
            while cst["i"] < need + 4:
                assert cst["i"] < cst["limit"]
                cast_pump(1)
            return [cst["tok"][need + k] for k in range(4)],                    [cst["st"][need + k] for k in range(4)]

        def scr_deps(m, oc):
            bb = oc * 4
            assert cst["done"]
            return [scr_ready[(m, bb + k)] for k in range(4)]

        if STOP == "p1":
            cst["limit"] = 10 ** 9
            cast_pump(10 ** 9)
            E("sync", lambda e: e.nop(),
              deps=[sm.cur for sm in sems.values()])
            finalize(nc, P)
            return nc

        st8 = {
            "stb_free": [None, None],
            "stf_free": [t_mc0, t_eyeb],
            "sq_free": [t_gfm, None],
            "pbc_free": None,
            "pbc2_free": None,
        }

        def ln_stat_chunk(xin, dc, dep, state):
            sl = slice(dc * TT, (dc + 1) * TT)
            p = dc % 2
            tcast = E("vector", lambda e, p=p, sl=sl, xin=xin: e.tensor_copy(
                stb[p][:], xin[:, sl]),
                deps=[st8["stb_free"][p], dep], inc=sV)
            state["sum_last"] = E("tensor", lambda e, p=p, dc=dc: e.matmul(
                pBC1[0:1, :], ones_b[:], stb[p][:],
                start=(dc == 0), stop=(dc == NC - 1)),
                deps=[tcast, t_ones,
                      st8["pbc_free"] if dc == 0 else None], inc=sT)
            st8["stb_free"][p] = state["sum_last"]
            tsq = E("scalar", lambda e, p=p, sl=sl, xin=xin: e.square(
                sq[p][:], xin[:, sl]),
                deps=[st8["sq_free"][p], dep], inc=sS)
            state["sq_last"] = E("tensor", lambda e, p=p, dc=dc: e.matmul(
                pBC2[0:1, :], ones_b[:], sq[p][:],
                start=(dc == 0), stop=(dc == NC - 1)),
                deps=[tsq, t_ones,
                      st8["pbc_free"] if dc == 0 else None,
                      st8["pbc2_free"] if dc == 0 else None], inc=sT)
            st8["sq_free"][p] = state["sq_last"]

        def ln_finish(xin, gamma_beta, state):
            sum_last = state["sum_last"]
            sq_last = state["sq_last"]
            t_mu = E("vector", lambda e: e.tensor_scalar(
                out=smalls[:, 0:512], in0=pBC1[0:1, :], scalar1=1.0 / D,
                scalar2=None, op0=ALU.mult), deps=[sum_last], inc=sV)
            t_e2 = E("vector", lambda e: e.tensor_scalar(
                out=smalls[:, 512:1024], in0=pBC2[0:1, :], scalar1=1.0 / D,
                scalar2=float(EPS), op0=ALU.mult, op1=ALU.add),
                deps=[sq_last], inc=sV)
            t_msq = E("vector", lambda e: e.tensor_tensor(
                out=bcst[:, 0:512], in0=smalls[:, 0:512],
                in1=smalls[:, 0:512], op=ALU.mult), deps=[t_mu], inc=sV)
            t_var = E("vector", lambda e: e.tensor_tensor(
                out=smalls[:, 512:1024], in0=smalls[:, 512:1024],
                in1=bcst[:, 0:512], op=ALU.subtract),
                deps=[t_e2, t_msq], inc=sV)
            t_ln = E("scalar", lambda e: e.activation(
                out=smalls[:, 512:1024], in_=smalls[:, 512:1024],
                func=ACT.Ln, bias=0.0, scale=1.0), deps=[t_var], inc=sS)
            t_rstd = E("scalar", lambda e: e.activation(
                out=bcst[:, 512:1024], in_=smalls[:, 512:1024],
                func=ACT.Exp, bias=0.0, scale=-0.5),
                deps=[t_ln, t_cp0], inc=sS)
            t_mrs = E("vector", lambda e: e.tensor_tensor(
                out=bcst[:, 0:512], in0=smalls[:, 0:512],
                in1=bcst[:, 512:1024], op=ALU.mult), deps=[t_rstd], inc=sV)
            t_bc1 = E("tensor", lambda e: e.matmul(
                pBC1[:, :], ones_f[:], bcst[:, 0:512], start=True, stop=True),
                deps=[t_mrs, t_onesf, t_mu], inc=sT)
            t_bc2 = E("tensor", lambda e: e.matmul(
                pBC2[:, :], ones_f[:], bcst[:, 512:1024], start=True, stop=True),
                deps=[t_rstd, t_onesf, t_e2, st8["pbc2_free"]], inc=sT)
            chunk_toks = []
            for dc in range(NC):
                sl = slice(dc * TT, (dc + 1) * TT)
                p = dc % 2
                t1 = E("vector", lambda e, p=p, sl=sl, xin=xin: e.tensor_tensor(
                    out=stf[p][:], in0=xin[:, sl], in1=pBC2[:, :], op=ALU.mult),
                    deps=[t_bc2, st8["stf_free"][p]], inc=sV)
                if gamma_beta is None:
                    t2 = E("vector", lambda e, p=p, sl=sl, xin=xin: e.tensor_tensor(
                        out=xin[:, sl], in0=stf[p][:], in1=pBC1[:, :],
                        op=ALU.subtract), deps=[t_bc1, t1], inc=sV)
                    st8["stf_free"][p] = t2
                    chunk_toks.append(t2)
                else:
                    gm, bt, dest = gamma_beta
                    t2 = E("vector", lambda e, p=p: e.tensor_tensor(
                        out=stf[p][:], in0=stf[p][:], in1=pBC1[:, :],
                        op=ALU.subtract), deps=[t_bc1, t1], inc=sV)
                    t3 = E("scalar", lambda e, p=p, dc=dc, sl=sl, gm=gm, bt=bt,
                           dest=dest: e.activation(
                        out=dest[:, sl], in_=stf[p][:], func=ACT.Identity,
                        bias=bt[:, dc:dc + 1], scale=gm[:, dc:dc + 1]),
                        deps=[t2, t_consts], inc=sS)
                    st8["stf_free"][p] = t3
                    chunk_toks.append(t3)
            st8["pbc_free"] = chunk_toks[-1]
            return chunk_toks

        def layer_norm(xin, gamma_beta, xdep):
            if len(xdep) == 1:
                xdep = xdep * NC
            state = {}
            for dc in range(NC):
                ln_stat_chunk(xin, dc, xdep[dc], state)
            return ln_finish(xin, gamma_beta, state)

        ln0_t0_toks = layer_norm(x_sb, (gam_sb, bet_sb, xn_sb), xdep=[t_x0])

        NBG = (PER_CORE + GATHER_BS - 1) // GATHER_BS
        gst = {
            "b": 0, "v": -1,
            "in_free": [None, None],
            "out_free": [None, None],
            "tin": None, "acc_tok": None, "ts15": None,
            "term_free": [None],
            "idxb_free": [None, None],
            "done": False,
        }

        def gather_pump(n):
            for _ in range(n):
                if gst["done"]:
                    return
                b = gst["b"]
                p3 = b % 2
                t0 = b * GATHER_BS
                nt = min(GATHER_BS, PER_CORE - t0)
                w = nt * 128
                v = gst["v"]
                if v == -1:
                    pi = b % 2
                    gst["tin"] = E("gpsimd", lambda e, pi=pi, t0=t0, nt=nt: e.dma_start(
                        out=idxi[pi][:, 0:nt * 128],
                        in_=idxp[:, t0 * 128:(t0 + nt) * 128]),
                        deps=[gst["in_free"][pi]], inc=sGI[pi], amt=16)
                    gst["v"] = -2
                elif v == -2:
                    pi = b % 2
                    p2 = b % 2
                    gst["tin"] = E("vector", lambda e, pi=pi, p2=p2, w=w: e.tensor_copy(
                        idxb[p2][:, 0:w], idxi[pi][:, 0:w]),
                        deps=[gst["tin"], gst["idxb_free"][p2]], inc=sV)
                    gst["in_free"][pi] = gst["tin"]
                    gst["v"] = 0
                elif v < 16:
                    if v == 0:
                        gst["acc_tok"] = E("vector", lambda e, p3=p3, p2=b % 2, w=w:
                            e.tensor_scalar(
                                out=gout[p3][:, 0:w], in0=idxb[p2][:, 0:w],
                                scalar1=0.0, scalar2=lut_sb[:, 0:1],
                                op0=ALU.is_equal, op1=ALU.mult),
                            deps=[gst["tin"], t_consts, gst["out_free"][p3]], inc=sV)
                    else:
                        tt_ = E("vector", lambda e, p3=p3, p2=b % 2, v=v, w=w:
                            e.tensor_scalar(
                                out=term[0][:, 0:w], in0=idxb[p2][:, 0:w],
                                scalar1=float(v), scalar2=lut_sb[:, v:v + 1],
                                op0=ALU.is_equal, op1=ALU.mult),
                            deps=[gst["tin"], gst["term_free"][0]], inc=sV)
                        gst["acc_tok"] = E("vector", lambda e, p3=p3, v=v, w=w:
                            e.tensor_tensor(
                                out=gout[p3][:, 0:w], in0=gout[p3][:, 0:w],
                                in1=term[0][:, 0:w], op=ALU.add),
                            deps=[tt_, gst["acc_tok"]], inc=sV)
                        gst["term_free"][0] = gst["acc_tok"]
                        if v == 15:
                            gst["idxb_free"][b % 2] = gst["acc_tok"]
                    gst["v"] = v + 1
                else:
                    tst = E("scalar", lambda e, p3=p3, t0=t0, nt=nt, w=w: e.dma_start(
                        out=agin[:, t0 * 128:(t0 + nt) * 128],
                        in_=gout[p3][:, 0:w]),
                        deps=[gst["acc_tok"]], inc=sAG[p3], amt=16)
                    gst["out_free"][p3] = tst
                    gst["b"] = b + 1
                    gst["v"] = -1
                    if gst["b"] >= NBG:
                        gst["done"] = True


        if STOP == "p2":
            E("sync", lambda e: e.nop(),
              deps=[sm.cur for sm in sems.values()])
            finalize(nc, P)
            return nc

        if STOP.startswith("agdump") and False:
            base = int(STOP[6:] or 0)
            prev = [t_ag, t_ag]
            for bch in range(64):
                p = bch % 2
                t0 = (base + bch * 4) * 128
                tin = E("gpsimd", lambda e, p=p, t0=t0: e.dma_start(
                    out=gout[p][:, 0:512].rearrange("p (j c) -> p j c", j=4),
                    in_=agout[t0:t0 + 512, :].rearrange("(j p) c -> p j c", p=128)),
                    deps=[t_ag, prev[p]], inc=sGI[p], amt=16)
                tcv = E("vector", lambda e, p=p: e.tensor_copy(
                    stf[p][:], gout[p][:, 0:512]), deps=[tin], inc=sV)
                row = (bch * 512) // 2048 * 128
                col = (bch * 512) % 2048
                tst = E("gpsimd", lambda e, p=p, row=row, col=col: e.dma_start(
                    out=y_out[row:row + 128, col:col + 512], in_=stf[p][:]),
                    deps=[tcv], inc=sST, amt=16)
                prev[p] = tst
            E("sync", lambda e: e.nop(),
              deps=[sm.cur for sm in sems.values()])
            finalize(nc, P)
            return nc

        ws_free = [None, None, None, None]
        x_free = None
        xn_free = None
        q_free = None
        ao_free = None
        kv_free = [t_z1, t_z2]
        k_ready = [[t_z1] * NH, [t_z1] * NH]
        v_ready = [[t_z2] * NH, [t_z2] * NH]
        kg_ready = [None] * NH
        vg_ready = [None] * NH
        pX_free = [None, None, None, None]
        pSC_free = [None, None]
        pT_state = {"free": None}
        pAV_free = None
        attention_state = {}
        h_dma = [None] * NC
        x_load_tok = t_x0
        last_av_tok = None
        last_gup_mm = None

        xchain = None
        for tau in range(NTILE):
            buf = tau % 2
            if tau > 0:
                if xchain is None:
                    xchain = list(cast_in_free[:3])
                x_chunk = []
                for dc in range(NC):
                    p = dc % 3
                    tok = E("scalar", lambda e, tau=tau, dc=dc: e.dma_start(
                        out=x_sb[:, dc * TT:(dc + 1) * TT],
                        in_=x_t[tau][:, dc * TT:(dc + 1) * TT]),
                        deps=[prev_ln1_toks[dc], h_dma[dc // 2], xchain[p]],
                        inc=sCI[p], amt=16)
                    xchain[p] = tok
                    x_chunk.append(tok)
                x_load_tok = x_chunk[-1]
            else:
                x_chunk = [t_x0] * NC
            if tau == 0:
                ln0_toks = ln0_t0_toks
            else:
                pX_free[2] = pSC_free[0]
                pX_free[3] = pSC_free[1]
                E("vector", lambda e: e.engine_nop(), deps=[xn_free])
                ln0_toks = layer_norm(x_sb, (gam_sb, bet_sb, xn_sb),
                                      xdep=x_chunk)
            if STOP in ("ln0", "ln0only"):
                for dc in range(NC):
                    p = dc % 2
                    tcv = E("vector", lambda e, p=p, dc=dc: e.tensor_copy(
                        stf[p][:], xn_sb[:, dc * TT:(dc + 1) * TT]),
                        deps=[ln0_toks[dc], st8["stf_free"][p]], inc=sV)
                    td = E("gpsimd", lambda e, p=p, dc=dc, tau=tau: e.dma_start(
                        out=y_out[dc * 128:(dc + 1) * 128, tau * TT:(tau + 1) * TT],
                        in_=stf[p][:]), deps=[tcv], inc=sST, amt=16)
                    st8["stf_free"][p] = td
            if STOP == "ln0only" and tau == 0:
                E("sync", lambda e: e.nop(),
                  deps=[sm.cur for sm in sems.values()])
                finalize(nc, P)
                return nc

            def proj(m, evac, tile0_extra=None):
                toks = []
                for oc in range(16):
                    wp = oc % 4
                    if tau == 0:
                        ctoks, stoks = cast_direct(m, oc)
                        wdep = lambda hc, ctoks=ctoks: ctoks[hc // 4]
                    else:
                        base = m * 256 + oc * 16
                        r0 = base * 128
                        tw = E("sync", lambda e, wp=wp, r0=r0: e.dma_start(
                            out=ws[wp][:], in_=wscr[:, r0:r0 + 2048]),
                            deps=[ws_free[wp]] + ws_cast_deps[wp]
                                 + scr_deps(m, oc),
                            inc=sWS[wp], amt=16)
                        wdep = lambda hc, tw=tw: tw
                    nb = 2 if tau == 0 else 4
                    bx = oc % nb
                    pX = [pA, pB, pSC0, pSC1][bx]
                    mm = None
                    for hc in range(16):
                        mm = E("tensor", lambda e, wp=wp, hc=hc, pX=pX: e.matmul(
                            pX[:, :], ws[wp][:, hc * 128:(hc + 1) * 128],
                            xn_sb[:, hc * TT:hc * TT + TT],
                            start=(hc == 0), stop=(hc == 15)),
                            deps=[wdep(hc), ln0_toks[hc],
                                  pX_free[bx] if hc == 0 else None],
                            inc=sT if hc == 15 else None)
                    ws_free[wp] = mm
                    etok = evac(oc, pX, mm)
                    pX_free[bx] = etok
                    toks.append(etok)
                    gather_pump(9)
                    if tile0_extra is not None:
                        mmg = None
                        for hc in range(16):
                            mmg = E("tensor", lambda e, wp=wp, hc=hc: e.matmul(
                                pSC0[:, 0:G], ws[wp][:, hc * 128:(hc + 1) * 128],
                                gfm_b[:, hc * G:(hc + 1) * G],
                                start=(hc == 0), stop=(hc == 15)),
                                deps=[t_gfm, pSC_free[0] if hc == 0 else None],
                                inc=sT if hc == 15 else None)
                        ws_free[wp] = mmg
                        pSC_free[0] = tile0_extra(oc, mmg)
                    if tau == 0:
                        ws_cast_deps[wp] = [ws_free[wp]] + stoks
                        cst["limit"] = max(cst["limit"],
                                           (m * 16 + oc + 5) * 4)
                        cast_pump(2)
                return toks

            def k_evac(oc, pX, mm):
                t = E("scalar", lambda e, oc=oc, pX=pX, buf=buf: e.copy(
                    KB(buf)[:, oc * TT:(oc + 1) * TT], pX[:, :]),
                    deps=[mm, kv_free[buf]], inc=sS)
                k_ready[buf][oc] = t
                return t

            def kg_extra(oc, mmg):
                t = E("vector", lambda e, oc=oc: e.tensor_copy(
                    kg_sb[:, oc * G:(oc + 1) * G], pSC0[:, 0:G]),
                    deps=[mmg], inc=sV)
                kg_ready[oc] = t
                return t

            proj(0, k_evac, kg_extra if tau == 0 else None)
            if STOP == "kproj" and tau == 0:
                E("sync", lambda e: e.nop(),
                  deps=[sm.cur for sm in sems.values()])
                finalize(nc, P)
                return nc

            def v_evac(oc, pX, mm):
                tv = E("vector", lambda e, pX=pX: e.tensor_copy(vtmp[:], pX[:, :]),
                       deps=[mm], inc=sV)
                if STOP == "vprojnotr":
                    last = None
                    for j in range(4):
                        last = E("vector", lambda e, j=j, oc=oc, buf=buf: e.tensor_copy(
                            VB(buf)[:, j * 2048 + oc * 128: j * 2048 + (oc + 1) * 128],
                            vtmp[:, j * 128:(j + 1) * 128]),
                            deps=[tv, kv_free[buf]], inc=sV)
                    pT_state["free"] = last
                    v_ready[buf][oc] = last
                    return last
                trs = []
                for j in range(4):
                    trs.append(E("tensor", lambda e, j=j: e.matmul(
                        pT[:, j * 128:(j + 1) * 128], vtmp[:, j * 128:(j + 1) * 128],
                        eye_b[:], start=True, stop=True),
                        deps=[tv, t_eyeb, pT_state["free"] if j == 0 else None],
                        inc=sT if j == 3 else None))
                last = None
                for j in range(4):
                    last = E("scalar", lambda e, j=j, oc=oc, buf=buf: e.copy(
                        VB(buf)[:, j * 2048 + oc * 128: j * 2048 + (oc + 1) * 128],
                        pT[:, j * 128:(j + 1) * 128]),
                        deps=[trs[3], kv_free[buf]], inc=sS)
                pT_state["free"] = last
                v_ready[buf][oc] = last
                return last

            def vg_extra(oc, mmg):
                tv = E("vector", lambda e: e.tensor_copy(
                    vgt[:, 0:G], pSC0[:, 0:G]), deps=[mmg], inc=sV)
                tr = E("tensor", lambda e: e.matmul(
                    pT[0:G, 0:128], vgt[:, 0:G], eye_b[:], start=True, stop=True),
                    deps=[tv, t_eyeb, pT_state["free"]], inc=sT)
                t = E("vector", lambda e, oc=oc: e.tensor_copy(
                    vg_sb[0:G, oc * 128:(oc + 1) * 128], pT[0:G, 0:128]),
                    deps=[tr], inc=sV)
                pT_state["free"] = t
                vg_ready[oc] = t
                return t

            proj(1, v_evac, vg_extra if (tau == 0 and STOP not in ('vprojnog', 'vprojnotr')) else None)
            if STOP in ("vproj", "vprojnog", "vprojnotr") and tau == 0:
                E("sync", lambda e: e.nop(),
                  deps=[sm.cur for sm in sems.values()])
                finalize(nc, P)
                return nc

            def q_evac(oc, pX, mm):
                return E("scalar", lambda e, oc=oc, pX=pX: e.mul(
                    q_sb[:, oc * TT:(oc + 1) * TT], pX[:, :], float(QSCALE)),
                    deps=[mm, q_free, t_cp1], inc=sS)

            q_toks = proj(2, q_evac, None)
            if STOP == "qproj" and tau == 0:
                E("sync", lambda e: e.nop(),
                  deps=[sm.cur for sm in sems.values()])
                finalize(nc, P)
                return nc

            psc3 = [pSC0, pSC1, pBC1]
            pTb3 = [pT, pA, pB]
            if tau == 0:
                att = {"sc": {}, "exp": {}, "ri": {},
                       "scl": {}, "tr": {}, "ate": {}, "av": {}, "ao": {}}
                attention_state["att"] = att
            att = attention_state["att"]
            git0 = tau * 64
            if tau > 0:
                pSC_free[0] = pX_free[2]
                pSC_free[1] = pX_free[3]
            psc_init = [pSC_free[0], pSC_free[1], ln0_toks[-1]]
            pT_init = [pT_state["free"], pX_free[0], pX_free[1]]
            seen_bank = [False, False, False]
            seen_tbank = [False, False, False]
            seen_avb = [False, False]
            ao_toks = []

            def blocks_of(it):
                qc = it // NH
                a_blk = tau * 4 + qc
                return [a_blk - 2, a_blk - 1, a_blk]

            def sc_e(it):
                git = git0 + it
                b3 = git % 3
                psc = psc3[b3]
                qc, h = divmod(it, NH)
                qsl = q_sb[:, h * TT + qc * 128: h * TT + (qc + 1) * 128]
                extra = att["exp"].get(git - 3)
                if not seen_bank[b3]:
                    seen_bank[b3] = True
                    extra2 = psc_init[b3]
                else:
                    extra2 = None
                msel = (1 if (tau == 0 and qc == 0)
                        else (2 if (tau == 0 and qc == 1) else 0))
                E("tensor", lambda e, psc=psc, qsl=qsl, h=h: e.matmul(
                    psc[:, 0:G], qsl, kg_sb[:, h * G:(h + 1) * G],
                    start=True, stop=True),
                    deps=[q_toks[h], kg_ready[h], extra, extra2], inc=None)
                t_mm = None
                for j, bk in enumerate(blocks_of(it)):
                    if msel == 2:
                        mof = 384 if j == 0 else j * 128
                    else:
                        mof = msel * 384 + j * 128
                    E("tensor", lambda e, psc=psc, j=j, mof=mof: e.matmul(
                        psc[:, G + j * 128:G + (j + 1) * 128], eye_b[:],
                        msk_sb[:, mof:mof + 128],
                        start=True, stop=False),
                        deps=[t_msk, t_eyeb] if j == 0 else [], inc=None)
                    bb = (bk // 4) % 2
                    slot = bk % 4
                    t_mm = E("tensor", lambda e, psc=psc, qsl=qsl, bb=bb,
                             slot=slot, h=h, j=j: e.matmul(
                        psc[:, G + j * 128:G + (j + 1) * 128], qsl,
                        KB(bb)[:, h * TT + slot * 128: h * TT + (slot + 1) * 128],
                        start=False, stop=True),
                        deps=[k_ready[bb][h]], inc=sT if j == 2 else None)
                att["sc"][git] = t_mm

            def exp_e(it):
                git = git0 + it
                psc = psc3[git % 3]
                nb = git % 3
                att["exp"][git] = E("scalar", lambda e, psc=psc, nb=nb: e.activation(
                    out=aat[nb][:, 0:400], in_=psc[:, 0:400], func=ACT.Exp,
                    bias=0.0, scale=1.0, accum_out=sume[nb][:, :]),
                    deps=[att["sc"][git], att["tr"].get(git - 3),
                          att["ri"].get(git - 3)], inc=sS)

            def ri_e(it):
                git = git0 + it
                nb = git % 3
                att["ri"][git] = E("vector", lambda e, nb=nb: e.reciprocal(
                    rinv[nb][:], sume[nb][:]),
                    deps=[att["exp"][git], att["scl"].get(git - 3)], inc=sV)

            def scl_e(it):
                git = git0 + it
                nb = git % 3
                att["scl"][git] = E("vector", lambda e, nb=nb: e.tensor_scalar(
                    out=aat[nb][:, 0:400], in0=aat[nb][:, 0:400],
                    scalar1=rinv[nb][:, 0:1], scalar2=None, op0=ALU.mult),
                    deps=[att["ri"][git]], inc=sV)

            def tr_e(it):
                git = git0 + it
                b3 = git % 3
                pTb = pTb3[b3]
                if not seen_tbank[b3]:
                    seen_tbank[b3] = True
                    extra = pT_init[b3]
                else:
                    extra = None
                E("tensor", lambda e, pTb=pTb, b3=b3: e.matmul(
                    pTb[0:G, 0:128], aat[b3][:, 0:G], eye_b[:],
                    start=True, stop=True),
                    deps=[att["scl"][git], att["ate"].get(git - 3), extra,
                          t_eyeb], inc=None)
                tr_last = None
                for j in range(3):
                    tr_last = E("tensor", lambda e, j=j, pTb=pTb, b3=b3: e.matmul(
                        pTb[:, 128 * (j + 1):128 * (j + 2)],
                        aat[b3][:, G + j * 128:G + (j + 1) * 128], eye_b[:],
                        start=True, stop=True),
                        deps=[], inc=sT if j == 2 else None)
                att["tr"][git] = tr_last

            def ate_e(it):
                git = git0 + it
                b3 = git % 3
                att["ate"][git] = E("vector", lambda e, b3=b3: e.tensor_copy(
                    at_sb[b3][:], pTb3[b3][:, 0:512]),
                    deps=[att["tr"][git], att["av"].get(git - 3)], inc=sV)

            def av_e(it):
                git = git0 + it
                b3 = git % 3
                bk2 = git % 2
                pAVb = [pAV, pBC2][bk2]
                rv = ((git // 2) % 2) * 128
                qc, h = divmod(it, NH)
                if bk2 == 1 and not seen_avb[1]:
                    seen_avb[1] = True
                    extra = ln0_toks[-1]
                elif bk2 == 0 and not seen_avb[0]:
                    seen_avb[0] = True
                    extra = pAV_free
                else:
                    extra = None
                E("tensor", lambda e, h=h, b3=b3, rv=rv, pAVb=pAVb: e.matmul(
                    pAVb[:, rv:rv + 128],
                    vg_sb[0:G, h * 128:(h + 1) * 128],
                    at_sb[b3][0:G, 0:128], start=True, stop=False),
                    deps=[att["ate"][git], vg_ready[h],
                          att["ao"].get(git - 4), extra],
                    inc=None)
                t_av = None
                for j, bkk in enumerate(blocks_of(it)):
                    bb = (bkk // 4) % 2
                    slot = bkk % 4
                    t_av = E("tensor", lambda e, bb=bb, slot=slot, h=h, j=j,
                             b3=b3, rv=rv, pAVb=pAVb: e.matmul(
                        pAVb[:, rv:rv + 128],
                        VB(bb)[:, slot * 2048 + h * 128: slot * 2048 + (h + 1) * 128],
                        at_sb[b3][:, 128 * (j + 1):128 * (j + 2)],
                        start=False, stop=(j == 2)),
                        deps=[v_ready[bb][h]], inc=sT if j == 2 else None)
                att["av"][git] = t_av

            def ao_e(it):
                git = git0 + it
                bk2 = git % 2
                pAVb = [pAV, pBC2][bk2]
                rv = ((git // 2) % 2) * 128
                qc, h = divmod(it, NH)
                t_ao = E("scalar", lambda e, h=h, qc=qc, rv=rv, pAVb=pAVb: e.copy(
                    ao_sb[:, h * TT + qc * 128: h * TT + (qc + 1) * 128],
                    pAVb[:, rv:rv + 128]),
                    deps=[att["av"][git], ao_free], inc=sS)
                att["ao"][git] = t_ao
                ao_toks.append(t_ao)

            NIT = 4 * NH
            for p in range(NIT + 6):
                cast_pump(1)
                if p < NIT:
                    sc_e(p)
                if 0 <= p - 2 < NIT:
                    exp_e(p - 2)
                if 0 <= p - 3 < NIT:
                    ri_e(p - 3)
                    scl_e(p - 3)
                if 0 <= p - 4 < NIT:
                    tr_e(p - 4)
                if 0 <= p - 5 < NIT:
                    ate_e(p - 5)
                    av_e(p - 5)
                if 0 <= p - 6 < NIT:
                    ao_e(p - 6)

            glast = git0 + NIT - 1
            pSC_free[0] = att["exp"][[g for g in range(git0, git0 + NIT) if g % 3 == 0][-1]]
            pSC_free[1] = att["exp"][[g for g in range(git0, git0 + NIT) if g % 3 == 1][-1]]
            st8["pbc_free"] = att["exp"][[g for g in range(git0, git0 + NIT) if g % 3 == 2][-1]]
            pT_state["free"] = att["ate"][[g for g in range(git0, git0 + NIT) if g % 3 == 0][-1]]
            pX_free[0] = att["ate"][[g for g in range(git0, git0 + NIT) if g % 3 == 1][-1]]
            pX_free[1] = att["ate"][[g for g in range(git0, git0 + NIT) if g % 3 == 2][-1]]
            pAV_free = att["ao"][[g for g in range(git0, git0 + NIT) if g % 2 == 0][-1]]
            st8["pbc2_free"] = att["ao"][[g for g in range(git0, git0 + NIT) if g % 2 == 1][-1]]
            score_last_tok = att["sc"][glast]
            q_free = score_last_tok
            kv_free[0] = score_last_tok
            kv_free[1] = score_last_tok
            xn_free = q_toks[-1]
            last_av_tok = ao_toks[-1]
            ao_head = [ao_toks[3 * NH + h] for h in range(NH)]
            if STOP == "attn" and tau == 0:
                E("sync", lambda e: e.nop(),
                  deps=[sm.cur for sm in sems.values()])
                finalize(nc, P)
                return nc

            wo_resid = []
            ln1_state = {}
            wo_mm_last = None
            for oc in range(16):
                wp = oc % 4
                if tau == 0:
                    ctoks, stoks = cast_direct(3, oc)
                    wdep = lambda hc, ctoks=ctoks: ctoks[hc // 4]
                else:
                    base = 3 * 256 + oc * 16
                    r0 = base * 128
                    tw = E("sync", lambda e, wp=wp, r0=r0: e.dma_start(
                        out=ws[wp][:], in_=wscr[:, r0:r0 + 2048]),
                        deps=[ws_free[wp]] + ws_cast_deps[wp] + scr_deps(3, oc),
                        inc=sWS[wp], amt=16)
                    wdep = lambda hc, tw=tw: tw
                pX = [pA, pB][oc % 2]
                mm = None
                for hc in range(16):
                    mm = E("tensor", lambda e, wp=wp, hc=hc, pX=pX: e.matmul(
                        pX[:, :], ws[wp][:, hc * 128:(hc + 1) * 128],
                        ao_sb[:, hc * TT:hc * TT + TT],
                        start=(hc == 0), stop=(hc == 15)),
                        deps=[wdep(hc), ao_head[hc],
                              pX_free[oc % 2] if hc == 0 else None],
                        inc=sT if hc == 15 else None)
                ws_free[wp] = mm
                if tau == 0:
                    ws_cast_deps[wp] = [mm] + stoks
                    cst["limit"] = max(cst["limit"], (48 + oc + 5) * 4)
                wo_mm_last = mm
                tr = E("vector", lambda e, oc=oc, pX=pX: e.scalar_tensor_tensor(
                    out=x_sb[:, oc * TT:(oc + 1) * TT],
                    in0=x_sb[:, oc * TT:(oc + 1) * TT], scalar=2.0,
                    in1=pX[:, :], op0=ALU.mult, op1=ALU.add),
                    deps=[mm, x_chunk[oc]], inc=sV)
                pX_free[oc % 2] = tr
                wo_resid.append(tr)
                gather_pump(9)
            ao_free = wo_mm_last

            if STOP == "y1":
                for dc in range(NC):
                    h_dma[dc] = E("gpsimd", lambda e, dc=dc, tau=tau: e.dma_start(
                        out=y_out[dc * 128:(dc + 1) * 128, tau * TT:(tau + 1) * TT],
                        in_=x_sb[:, dc * TT:(dc + 1) * TT]),
                        deps=[wo_resid[dc]], inc=sST, amt=16)
                x_free = wo_resid[-1]
                continue

            for dc in range(NC):
                ln_stat_chunk(x_sb, dc, wo_resid[dc], ln1_state)
            ln1_toks = ln_finish(x_sb, None, ln1_state)
            prev_h = h_dma[NC // 2 - 1] if tau > 0 else None
            for j in range(NC // 2):
                prev_h = E("scalar", lambda e, j=j, tau=tau: e.dma_start(
                    out=hscr[tau][:, 2 * j * TT:(2 * j + 2) * TT],
                    in_=x_sb[:, 2 * j * TT:(2 * j + 2) * TT]),
                    deps=[ln1_toks[2 * j], ln1_toks[2 * j + 1], prev_h],
                    inc=sHS, amt=16)
                h_dma[j] = prev_h
            prev_ln1_toks = ln1_toks
            if STOP == "h":
                for dc in range(NC):
                    E("gpsimd", lambda e, dc=dc, tau=tau: e.dma_start(
                        out=y_out[dc * 128:(dc + 1) * 128, tau * TT:(tau + 1) * TT],
                        in_=x_sb[:, dc * TT:(dc + 1) * TT]),
                        deps=[ln1_toks[dc]], inc=sST, amt=16)
            x_free = ln1_toks[-1]

        cast_pump(10 ** 9)
        gather_pump(10 ** 9)
        t_ag = E("gpsimd", lambda e: e.collective_compute(
            "AllGather", ALU.bypass, replica_groups=[list(range(NCORES))],
            ins=[agin.opt()], outs=[agout.opt()]),
            deps=[sm.cur for sm in sAG], inc=sCC)

        if STOP in ("ln0", "y1", "h"):
            E("sync", lambda e: e.nop(),
              deps=[sm.cur for sm in sems.values()])
            finalize(nc, P)
            return nc

        def ag_pieces(tile0, ntiles):
            pieces = []
            t = tile0
            off = 0
            while ntiles > 0:
                cb = t // PER_CORE
                rel = t % PER_CORE
                n = min(ntiles, PER_CORE - rel)
                pieces.append((cb, rel, off, n))
                t += n
                off += n
                ntiles -= n
            return pieces

        def ag_load(dest, tile0, ntiles, deps, sem):
            tok = None
            for cb, rel, off, n in ag_pieces(tile0, ntiles):
                tok = E("sync", lambda e, cb=cb, rel=rel, off=off, n=n, dest=dest:
                        e.dma_start(
                    out=dest[:, off * 128:(off + n) * 128],
                    in_=agout[cb][:, rel * 128:(rel + n) * 128]),
                    deps=deps, inc=sem, amt=16)
            return tok

        h2_ready = [None] * NI
        wd_free = [q_free, ao_free]
        pG_free = [wo_resid[14], pSC_free[0]]
        pU_free = [wo_resid[15], pSC_free[1]]
        pD_free = [pT_state["free"], pAV_free]
        sg_free = [None, None]
        y_dma = [None] * NC
        hload_free = list(h_dma)
        xn_cons = xn_free
        h2_cons = last_av_tok
        q_ao_free = ao_free

        XN2 = AR[:, NI * TT:NI * TT + NC * TT]
        pre_hb = {}
        cin_free2 = list(cast_in_free[:3])
        xnb_cons = [None, last_av_tok]

        def prefetch_chunk(t, dc):
            p = dc % 3
            buf = xn_sb if t % 2 == 0 else XN2
            tpin = E("scalar", lambda e, p=p, t=t, dc=dc: e.dma_start(
                out=cin[p][:], in_=hscr[t][:, dc * TT:(dc + 1) * TT]),
                deps=[cin_free2[p]], inc=sCI[p], amt=16)
            tc = E("vector", lambda e, p=p, dc=dc, buf=buf: e.tensor_copy(
                buf[:, dc * TT:(dc + 1) * TT], cin[p][:]),
                deps=[tpin, xnb_cons[t % 2]], inc=sV)
            cin_free2[p] = tc
            pre_hb.setdefault(t, []).append(tc)

        for tau in range(NTILE):
            if tau == 0:
                hb_load = []
                hb_toks = []
                for dc in range(NC):
                    p = dc % 3
                    tok = E("scalar", lambda e, dc=dc, p=p: e.dma_start(
                        out=x_sb[:, dc * TT:(dc + 1) * TT],
                        in_=hscr[0][:, dc * TT:(dc + 1) * TT]),
                        deps=[prev_ln1_toks[dc], h_dma[dc // 2], xchain[p]],
                        inc=sCI[p], amt=16)
                    xchain[p] = tok
                    hb_load.append(tok)
                    hb_toks.append(E("vector", lambda e, dc=dc: e.tensor_copy(
                        xn_sb[:, dc * TT:(dc + 1) * TT],
                        x_sb[:, dc * TT:(dc + 1) * TT]),
                        deps=[tok, xn_cons], inc=sV))
            else:
                t_h = E("scalar", lambda e, tau=tau: e.dma_start(
                    out=x_sb[:], in_=hscr[tau]),
                    deps=[x_free] + list(y_dma), inc=sX, amt=16)
                hb_toks = pre_hb[tau]
            XN = xn_sb if tau % 2 == 0 else XN2

            last_mm_u = None
            for ic in range(NI):
                wpg = (2 * ic) % 4
                wpu = (2 * ic + 1) % 4
                twg = ag_load(ws[wpg], ic * 16, 16,
                              [t_ag, ws_free[wpg]], sWS[wpg])
                twu = ag_load(ws[wpu], N_G + ic * 16, 16,
                              [t_ag, ws_free[wpu]], sWS[wpu])
                pg = [pA, pSC0][ic % 2]
                pu = [pB, pSC1][ic % 2]
                mmg = None
                for hc in range(16):
                    mmg = E("tensor", lambda e, wpg=wpg, hc=hc, pg=pg, XN=XN:
                        e.matmul(
                        pg[:, :], ws[wpg][:, hc * 128:(hc + 1) * 128],
                        XN[:, hc * TT:hc * TT + TT],
                        start=(hc == 0), stop=(hc == 15)),
                        deps=[twg, hb_toks[hc],
                              pG_free[ic % 2] if hc == 0 else None],
                        inc=sT if hc == 15 else None)
                mmu = None
                for hc in range(16):
                    mmu = E("tensor", lambda e, wpu=wpu, hc=hc, pu=pu, XN=XN:
                        e.matmul(
                        pu[:, :], ws[wpu][:, hc * 128:(hc + 1) * 128],
                        XN[:, hc * TT:hc * TT + TT],
                        start=(hc == 0), stop=(hc == 15)),
                        deps=[twu, hb_toks[hc],
                              pU_free[ic % 2] if hc == 0 else None],
                        inc=sT if hc == 15 else None)
                ws_free[wpg] = mmg
                ws_free[wpu] = mmu
                last_mm_u = mmu
                sp = ic % 2
                t_si = E("scalar", lambda e, sp=sp, pg=pg: e.activation(
                    out=sg[sp][:], in_=pg[:, :], func=ACT.Silu),
                    deps=[mmg, sg_free[sp]], inc=sS)
                pG_free[ic % 2] = t_si
                t_h2 = E("vector", lambda e, sp=sp, ic=ic, pu=pu: e.tensor_tensor(
                    out=H2[:, ic * TT:(ic + 1) * TT], in0=sg[sp][:], in1=pu[:, :],
                    op=ALU.mult), deps=[t_si, mmu, h2_cons], inc=sV)
                pU_free[ic % 2] = t_h2
                sg_free[sp] = t_h2
                h2_ready[ic] = t_h2
            xnb_cons[tau % 2] = last_mm_u

            down_resid = []
            ln2_state = {}
            last_mm_d = None
            for oc in range(16):
                wd = [q_sb, ao_sb][oc % 2]
                twd = ag_load(wd, 2 * N_G + oc * NI, NI,
                              [t_ag, wd_free[oc % 2], q_ao_free], sWD[oc % 2])
                pD = [pT, pAV][oc % 2]
                mmd = None
                for ic in range(NI):
                    mmd = E("tensor", lambda e, wd=wd, ic=ic, pD=pD: e.matmul(
                        pD[:, :], wd[:, ic * 128:(ic + 1) * 128],
                        H2[:, ic * TT:(ic + 1) * TT],
                        start=(ic == 0), stop=(ic == NI - 1)),
                        deps=[twd, h2_ready[ic],
                              pD_free[oc % 2] if ic == 0 else None],
                        inc=sT if ic == NI - 1 else None)
                wd_free[oc % 2] = mmd
                last_mm_d = mmd
                tr = E("vector", lambda e, oc=oc, pD=pD: e.tensor_tensor(
                    out=x_sb[:, oc * TT:(oc + 1) * TT],
                    in0=x_sb[:, oc * TT:(oc + 1) * TT], in1=pD[:, :], op=ALU.add),
                    deps=[mmd, hb_load[oc] if tau == 0 else t_h], inc=sV)
                pD_free[oc % 2] = tr
                down_resid.append(tr)
                if tau + 1 < NTILE:
                    prefetch_chunk(tau + 1, oc)
            q_ao_free = None
            h2_cons = last_mm_d
            st8["pbc_free"] = down_resid[-1]

            for dc in range(NC):
                ln_stat_chunk(x_sb, dc, down_resid[dc], ln2_state)
            ln2_toks = ln_finish(x_sb, None, ln2_state)
            for dc in range(NC):
                y_dma[dc] = E("scalar", lambda e, dc=dc, tau=tau: e.dma_start(
                    out=y_out[dc * 128:(dc + 1) * 128, tau * TT:(tau + 1) * TT],
                    in_=x_sb[:, dc * TT:(dc + 1) * TT]),
                    deps=[ln2_toks[dc]], inc=sST, amt=16)
            x_free = ln2_toks[-1]

        E("sync", lambda e: e.nop(),
          deps=[sm.cur for sm in sems.values()])
        finalize(nc, P)
    return nc


def finalize(nc, P):
    with nc.Block() as block:
        block.sync(lambda e: P.run("sync", e))
        block.vector(lambda e: P.run("vector", e))
        block.scalar(lambda e: P.run("scalar", e))
        block.tensor(lambda e: P.run("tensor", e))
        block.gpsimd(lambda e: P.run("gpsimd", e))



def _tileize(w):
    w4 = w.reshape(16, 128, 16, 128)
    t = w4.transpose(0, 2, 3, 1).reshape(256, 128, 128)
    return np.ascontiguousarray(t.transpose(1, 0, 2).reshape(128, 256 * 128))


def _masks():
    q = np.arange(128)[:, None]
    k = np.arange(384)[None, :]
    base = np.where((k >= q + 1) & (k <= q + 256), 0.0, NEG).astype(np.float32)
    m0 = np.where(k >= 256, base, NEG).astype(np.float32)
    return np.ascontiguousarray(np.concatenate([base, m0], axis=1))


def _idx_pack(gate_idx, up_idx, down_idx):
    gT = np.ascontiguousarray(gate_idx.T)
    uT = np.ascontiguousarray(up_idx.T)
    dT = np.ascontiguousarray(down_idx.T)
    jobs = ffn_jobs()
    packs = []
    for c in range(NCORES):
        part = jobs[c * PER_CORE:(c + 1) * PER_CORE]
        tiles = np.empty((PER_CORE, 128, 128), np.int32)
        for j, (m, a, b) in enumerate(part):
            if m == "g":
                tiles[j] = gT[b * 128:(b + 1) * 128, a * 128:(a + 1) * 128]
            elif m == "u":
                tiles[j] = uT[b * 128:(b + 1) * 128, a * 128:(a + 1) * 128]
            else:
                tiles[j] = dT[b * 128:(b + 1) * 128, a * 128:(a + 1) * 128]
        packs.append(np.ascontiguousarray(
            tiles.transpose(1, 0, 2).reshape(128, PER_CORE * 128)))
    return packs


_prog_cache = {}


def _get_prog():
    key = STOP
    if key not in _prog_cache:
        _prog_cache[key] = build_program()
    return _prog_cache[key]


def kernel(L0_attn_x, L0_attn_global_tokens, L0_attn_ln_scale, L0_attn_ln_bias,
           L0_attn_Wq, L0_attn_Wk, L0_attn_Wv, L0_attn_Wo,
           L0_ffn_lut, L0_ffn_gate_idx, L0_ffn_up_idx, L0_ffn_down_idx):
    x = np.asarray(L0_attn_x, dtype=np.float32)
    g = np.asarray(L0_attn_global_tokens, dtype=np.float32)
    gam = np.asarray(L0_attn_ln_scale, dtype=np.float32)
    bet = np.asarray(L0_attn_ln_bias, dtype=np.float32)
    wq = np.asarray(L0_attn_Wq, dtype=np.float32)
    wk = np.asarray(L0_attn_Wk, dtype=np.float32)
    wv = np.asarray(L0_attn_Wv, dtype=np.float32)
    wo = np.asarray(L0_attn_Wo, dtype=np.float32)
    lut = np.asarray(L0_ffn_lut, dtype=np.float32)
    gidx = np.asarray(L0_ffn_gate_idx, dtype=np.int32)
    uidx = np.asarray(L0_ffn_up_idx, dtype=np.int32)
    didx = np.asarray(L0_ffn_down_idx, dtype=np.int32)

    wq_t, wk_t, wv_t, wo_t = map(_tileize, (wq, wk, wv, wo))
    g_t = np.ascontiguousarray(g[0].T)
    gam_t = np.ascontiguousarray(gam.reshape(16, 128).T)
    bet_t = np.ascontiguousarray(bet.reshape(16, 128).T)
    masks = _masks()
    eye = np.eye(128, dtype=np.float32)
    packs = _idx_pack(gidx, uidx, didx)

    nc = _get_prog()
    in_maps = []
    for c in range(NCORES):
        in_maps.append({
            "x_t": np.ascontiguousarray(
                x[c].T.reshape(NC, 128, NTILE, TT).transpose(2, 1, 0, 3)
                .reshape(NTILE, 128, NC * TT)),
            "wk_t": wk_t, "wv_t": wv_t, "wq_t": wq_t, "wo_t": wo_t,
            "idxp": packs[c],
            "lut_in": lut,
            "g_in": g_t,
            "gam_in": gam_t,
            "bet_in": bet_t,
            "eye_in": eye,
            "msk_in": masks,
        })
    res = run_bass_kernel_spmd(nc, in_maps, core_ids=list(range(NCORES)))
    out = np.stack([np.ascontiguousarray(r["y"].T) for r in res.results])
    return out
